# version 23
# speedup vs baseline: 1.0588x; 1.0173x over previous
import math
import numpy as np
from contextlib import ExitStack

import concourse.bass as bass
import concourse.mybir as mybir
from concourse.bass_utils import run_bass_kernel_spmd

F32 = mybir.dt.float32
BF16 = mybir.dt.bfloat16
AF = mybir.ActivationFunctionType
ALU = mybir.AluOpType

NCORES = 8
B = 16
T = 2048
D = 1024
BL = B // NCORES
NFC = D // 128
TT = 512
NTILE = T // TT
ENGINES = ("sync", "tensor", "vector", "scalar", "gpsimd")

LN_EPS = np.float32(1e-5)
RMS_EPS = np.float32(1e-6)
GN_EPS = np.float32(1e-5)


class Ctr:

    def __init__(self, sem):
        self.sem = sem
        self.n = 0

    def bump(self, k=1):
        self.n += k
        return self.n


class Sched:

    def __init__(self):
        self.q = {e: [] for e in ENGINES}

    def on(self, e, fn):
        self.q[e].append(fn)

    def flush(self, nc):
        with nc.Block() as blk:
            for e in ENGINES:
                fns = self.q[e]

                def runner(eng, fns=fns):
                    for fn in fns:
                        fn(eng)

                getattr(blk, e)(runner)
        self.q = {e: [] for e in ENGINES}


FCONST_VALUES = [float(np.float32(v)) for v in
                 (1e-5, 1e-6, -3.5 * math.log(2.0), 1.0, 0.0, 1e-10, 1e-8, 0.5, -1.0)]


class Env:
    def __init__(self, nc, ctx):
        self.nc = nc
        self.ctx = ctx
        self.sems = {}
        for name in ("ldh", "s0", "s1", "g0", "g1", "g2", "g3",
                     "st0", "st1", "mm", "ev", "ac", "gp", "bar",
                     "mm2", "ev2", "ac2"):
            self.sems[name] = Ctr(ctx.enter_context(nc.semaphore(name)))
        self.phase_idx = 0
        self.consts = {}
        self.dram = {}
        self.cpool = ctx.enter_context(nc.sbuf_tensor("cpool", [128, len(FCONST_VALUES)], F32))
        self.rows_sb = ctx.enter_context(nc.sbuf_tensor("ln_rows", [1, 6 * TT], F32))
        self.cvals = {v: i for i, v in enumerate(FCONST_VALUES)}

    def fconst(self, v):
        v = float(np.float32(v))
        i = self.cvals[v]
        return self.cpool[:, i:i + 1]

    def _uniq(self, name):
        if not hasattr(self, "_names"):
            self._names = {}
        k = self._names.get(name, 0)
        self._names[name] = k + 1
        return name if k == 0 else f"{name}_v{k}"

    def sb(self, stack, name, shape, dtype):
        return stack.enter_context(self.nc.sbuf_tensor(self._uniq(name), shape, dtype))

    def ps(self, stack, name, shape, dtype=F32):
        return stack.enter_context(self.nc.psum_tensor(self._uniq(name), shape, dtype))


def emit_barrier(env, sch):
    env.phase_idx += 1
    bar = env.sems["bar"]
    tgt = 5 * env.phase_idx
    bar.n = tgt
    for e in ENGINES:
        def _b(eng, tgt=tgt, sem=bar.sem):
            eng.sem_inc(sem, 1)
            eng.wait_ge(sem, tgt)
        sch.on(e, _b)



def slot_dma(env, sch, eng, slot, out_ap, in_ap):
    c = env.sems[slot]
    pre = c.n
    tgt = c.bump(16)
    def _d(e, pre=pre, out_ap=out_ap, in_ap=in_ap, sem=c.sem):
        e.wait_ge(sem, pre)
        e.dma_start(out=out_ap, in_=in_ap).then_inc(sem, 16)
    sch.on(eng, _d)
    return tgt


def emit_tile_loads(env, sch, xsrc_seq, j, xf, xbf, slot_f, slot_b, lo=None, hi=None):
    lo = 0 if lo is None else lo
    hi = NFC if hi is None else hi
    nfc = hi - lo
    src = xsrc_seq[lo:hi, :, j * TT:(j + 1) * TT].rearrange("c p t -> p c t")
    dstf = xf[:, : nfc * TT].rearrange("p (c t) -> p c t", c=nfc)
    dstb = xbf[:, : nfc * TT].rearrange("p (c t) -> p c t", c=nfc)
    t1 = slot_dma(env, sch, "sync", slot_f, dstf, src)
    t2 = slot_dma(env, sch, "gpsimd", slot_b, dstb, src)
    return t1, t2


def emit_weight_load_T(env, sch, wsrc, wsb, dout, nkc=NFC, slots=("g2", "g3")):
    tgts = {}
    for kc in range(nkc):
        sl = slots[kc % len(slots)]
        tgts[sl] = slot_dma(env, sch, "gpsimd", sl,
                            wsb[:, kc * dout:(kc + 1) * dout],
                            wsrc[kc * 128:(kc + 1) * 128, :])
    return [(sl, t) for sl, t in tgts.items()]


class LNBufs:

    def __init__(self, env, stack, tag, shift_cols=0, share_ps=None):
        self.shift = shift_cols
        w = NFC * (TT + shift_cols)
        self.xln = env.sb(stack, f"{tag}_xln", [128, w], BF16)
        self.rows = env.rows_sb[:, :]
        if share_ps is None:
            self.sq = env.sb(stack, f"{tag}_sq", [128, NFC * TT], BF16)
            self.save = env.sb(stack, f"{tag}_save", [128, NFC * max(1, shift_cols)], BF16)
            self.psA = env.ps(stack, f"{tag}_psA", [1, TT])
            self.psB = env.ps(stack, f"{tag}_psB", [1, TT])
            self.psM = env.ps(stack, f"{tag}_psM", [128, TT])
            self.psR = env.ps(stack, f"{tag}_psR", [128, TT])
        else:
            self.sq = share_ps.sq
            self.save = share_ps.save
            self.psA, self.psB = share_ps.psA, share_ps.psB
            self.psM, self.psR = share_ps.psM, share_ps.psR

    def cur(self, c):
        s = TT + self.shift
        return self.xln[:, c * s + self.shift:(c + 1) * s]

    def prev(self, c, k=1):
        s = TT + self.shift
        return self.xln[:, c * s + self.shift - k: (c + 1) * s - k]


def emit_ln(env, sch, ln, j, xf, xbf, f_wait, b_wait, wait_extra=None, norm_src="f32",
            sq_guard=None, rows_guard=None, bc_guard=None, xln_guard=None,
            cap=None, sems=("mm", "ev", "ac")):
    nc = env.nc
    mm, ev, ac = (env.sems[k] for k in sems)
    if cap is None:
        put = lambda sec, eng, f: sch.on(eng, f)
    else:
        put = lambda sec, eng, f: cap.setdefault(sec, []).append((eng, f))
    fsem, ftgt = f_wait
    bsem, btgt = b_wait
    ones_bf = env.consts["ones_col_bf"]
    ones_row = env.consts["ones_row_f"]

    if ln.shift:
        k = ln.shift
        if j == 0:
            def _zz(e, k=k):
                for c in range(NFC):
                    s = TT + k
                    e.memset(ln.xln[:, c * s: c * s + k], 0.0)
            put("shift", "vector", _zz)
        else:
            def _cp(e, k=k):
                for c in range(NFC):
                    s = TT + k
                    e.tensor_copy(out=ln.xln[:, c * s: c * s + k], in_=ln.save[:, c * k:(c + 1) * k])
            put("shift", "vector", _cp)

    put("sq", "scalar", lambda e: e.wait_ge(env.sems[bsem].sem, btgt))
    if sq_guard is not None:
        put("sq", "scalar", lambda e, g=sq_guard: e.wait_ge(g[0].sem, g[1]))
    if wait_extra is not None:
        put("sq", "scalar", lambda e: e.wait_ge(wait_extra[0].sem, wait_extra[1]))
    ac_sq = ac.bump()
    put("sq", "scalar", lambda e: e.activation(out=ln.sq[:], in_=xbf[:, :NFC * TT], func=AF.Square).then_inc(ac.sem, 1))

    mm_stA = mm.bump()
    mm_st = mm.bump()
    def _stats(e, ac_sq=ac_sq):
        e.wait_ge(env.sems[bsem].sem, btgt)
        for kc in range(NFC):
            i = e.matmul(out=ln.psA[0:1, :], lhsT=ones_bf[:, 0:1], rhs=xbf[:, kc * TT:(kc + 1) * TT],
                         start=(kc == 0), stop=(kc == NFC - 1))
            if kc == NFC - 1:
                i.then_inc(mm.sem, 1)
        e.wait_ge(ac.sem, ac_sq)
        for kc in range(NFC):
            i = e.matmul(out=ln.psB[0:1, :], lhsT=ones_bf[:, 0:1], rhs=ln.sq[:, kc * TT:(kc + 1) * TT],
                         start=(kc == 0), stop=(kc == NFC - 1))
            if kc == NFC - 1:
                i.then_inc(mm.sem, 1)
    put("stats", "tensor", _stats)

    R = ln.rows
    mu, m2, var, sd, nm, rs = (R[0:1, i * TT:(i + 1) * TT] for i in range(6))
    ev_rows = ev.bump()
    def _rows(e, mm_st=mm_st, rg=rows_guard):
        e.wait_ge(mm.sem, mm_st)
        if rg is not None:
            for gg in rg:
                e.wait_ge(gg[0].sem, gg[1])
        e.tensor_scalar_mul(out=mu, in0=ln.psA[0:1, :], scalar1=1.0 / 1024.0)
        e.tensor_scalar_mul(out=m2, in0=ln.psB[0:1, :], scalar1=1.0 / 1024.0)
        e.tensor_mul(out=var, in0=mu, in1=mu)
        e.tensor_sub(out=var, in0=m2, in1=var)
        e.tensor_scalar_mul(out=nm, in0=mu, scalar1=-1.0).then_inc(ev.sem, 1)
    put("rows", "vector", _rows)
    ac_rs = ac.bump(2)
    def _sd(e, ev_rows=ev_rows):
        e.wait_ge(ev.sem, ev_rows)
        e.activation(out=sd, in_=var, func=AF.Ln, bias=env.fconst(1e-5)[0:1, :]).then_inc(ac.sem, 1)
        e.activation(out=rs, in_=sd, func=AF.Exp, scale=-0.5).then_inc(ac.sem, 1)
    put("sd", "scalar", _sd)

    mm_bcM = mm.bump()
    mm_bc = mm.bump()
    def _bc(e, ac_rs=ac_rs, bg=bc_guard):
        e.wait_ge(ac.sem, ac_rs)
        if bg is not None:
            e.wait_ge(bg[0].sem, bg[1])
        e.matmul(out=ln.psM[:], lhsT=ones_row[0:1, :], rhs=nm, start=True, stop=True).then_inc(mm.sem, 1)
        e.matmul(out=ln.psR[:], lhsT=ones_row[0:1, :], rhs=rs, start=True, stop=True).then_inc(mm.sem, 1)
    put("bc", "tensor", _bc)

    ev_ln = ev.bump()
    nsrc = xf if norm_src == "f32" else xbf
    def _norm(e, mm_bc=mm_bc, xg=xln_guard):
        e.wait_ge(mm.sem, mm_bc)
        if xg is not None:
            e.wait_ge(xg[0].sem, xg[1])
        if norm_src == "f32":
            e.wait_ge(env.sems[fsem].sem, ftgt)
        last = None
        for c in range(NFC):
            e.tensor_add(out=ln.cur(c), in0=nsrc[:, c * TT:(c + 1) * TT], in1=ln.psM[:])
            last = e.tensor_mul(out=ln.cur(c), in0=ln.cur(c), in1=ln.psR[:])
        if ln.shift:
            k = ln.shift
            for c in range(NFC):
                last = e.tensor_copy(out=ln.save[:, c * k:(c + 1) * k],
                                     in_=ln.cur(c)[:, TT - k:TT])
        last.then_inc(ev.sem, 1)
    put("norm", "vector", _norm)
    return ev_ln, {"mm_st": mm_st, "ac_rs": ac_rs, "mm_bc": mm_bc}



def emit_p_l0ffn(env, sch, xsrc, xdst):
    nc = env.nc
    ldh, mm, ev, ac = (env.sems[k] for k in ("ldh", "mm", "ev", "ac"))
    st0, st1 = env.sems["st0"], env.sems["st1"]
    with ExitStack() as stk:
        xf = [env.sb(stk, f"p2_xf{i}", [128, NFC * TT], F32) for i in range(2)]
        xbf = [env.sb(stk, f"p2_xbf{i}", [128, NFC * TT], BF16) for i in range(2)]
        xout = env.sb(stk, "p2_xout", [128, NFC * TT], F32)
        ln = LNBufs(env, stk, "p2_ln", shift_cols=1)
        wr = env.sb(stk, "p2_wr", [128, NFC * D], BF16)
        wk = env.sb(stk, "p2_wk", [128, NFC * D], BF16)
        wv = env.sb(stk, "p2_wv", [128, NFC * D], BF16)
        mur = env.sb(stk, "p2_mur", [128, NFC], F32)
        muk = env.sb(stk, "p2_muk", [128, NFC], F32)
        omr = env.sb(stk, "p2_omr", [128, NFC], F32)
        omk = env.sb(stk, "p2_omk", [128, NFC], F32)
        xr = [env.sb(stk, f"p2_xr{i}", [128, NFC * TT], BF16) for i in range(2)]
        xk = [env.sb(stk, f"p2_xk{i}", [128, NFC * TT], BF16) for i in range(2)]
        kact = ln.sq
        kf = env.sb(stk, "p2_kf", [128, TT], F32)
        psm = [env.ps(stk, f"p2_ps{i}", [128, TT]) for i in range(4)]

        wt_w = {}
        for wname, wdst in (("w2_rT", wr), ("w2_kT", wk), ("w2_vT", wv)):
            for sl, t in emit_weight_load_T(env, sch, env.dram[wname], wdst, D):
                wt_w[sl] = max(wt_w.get(sl, 0), t)
        ldh.bump(16)
        sch.on("sync", lambda e: e.dma_start(out=mur[:], in_=env.dram["p_mur"]).then_inc(ldh.sem, 16))
        wt_mu = ldh.bump(16)
        sch.on("sync", lambda e: e.dma_start(out=muk[:], in_=env.dram["p_muk"]).then_inc(ldh.sem, 16))
        def wait_weights(e):
            for sl, t in wt_w.items():
                e.wait_ge(env.sems[sl].sem, t)
        ev_om = ev.bump()
        def _om(e, wt_mu=wt_mu):
            e.wait_ge(ldh.sem, wt_mu)
            e.tensor_scalar(out=omr[:], in0=mur[:], scalar1=-1.0, scalar2=1.0, op0=ALU.mult, op1=ALU.add)
            e.tensor_scalar(out=omk[:], in0=muk[:], scalar1=-1.0, scalar2=1.0,
                            op0=ALU.mult, op1=ALU.add).then_inc(ev.sem, 1)
        sch.on("vector", _om)

        NG = BL * NTILE
        evln, lninfo, ev_mix_t, kmm_hist, acr_hist = {}, {}, {}, {}, {}
        evo_hist, st_hist, vmm_hist = {}, {}, {}
        mmc, ev2c, mm2c, ac2c = (env.sems[k] for k in ("mm", "ev2", "mm2", "ac2"))

        def do_prep(g, cap):
            s, j = divmod(g, NTILE)
            bi = g % 2
            if g >= 2 and (g - 2) in evo_hist:
                sch.on("sync", lambda e, t=evo_hist[g - 2]: e.wait_ge(ev.sem, t))
                sch.on("gpsimd", lambda e, t=evo_hist[g - 2]: e.wait_ge(ev.sem, t))
            sf, sb = ("s0", "g0") if bi == 0 else ("s1", "g1")
            ld_t, ldg_t = emit_tile_loads(env, sch, xsrc[s], j, xf[bi], xbf[bi], sf, sb)
            kw = {"cap": cap, "sems": ("mm2", "ev2", "ac2")}
            if g >= 2 and (g - 2) in vmm_hist:
                kw["wait_extra"] = (mmc, vmm_hist[g - 2])
            if g >= 1:
                kw["sq_guard"] = (mm2c, lninfo[g - 1]["mm_st"])
                kw["rows_guard"] = [(mm2c, lninfo[g - 1]["mm_bc"]), (ac2c, lninfo[g - 1]["ac_rs"])]
                kw["bc_guard"] = (ev2c, evln[g - 1])
            evln[g], lninfo[g] = emit_ln(env, sch, ln, j, xf[bi], xbf[bi], (sf, ld_t), (sb, ldg_t), **kw)
            mt = ev2c.bump()
            def _mix(e, g=g, bi=bi, ev_ln=evln[g]):
                e.wait_ge(ev2c.sem, ev_ln)
                e.wait_ge(ev.sem, ev_om)
                if (g - 2) in kmm_hist:
                    e.wait_ge(mm.sem, kmm_hist[g - 2])
                last = None
                for c in range(NFC):
                    cc = slice(c * TT, (c + 1) * TT)
                    e.tensor_scalar_mul(out=xr[bi][:, cc], in0=ln.prev(c), scalar1=omr[:, c:c + 1])
                    e.scalar_tensor_tensor(out=xr[bi][:, cc], in0=ln.cur(c), scalar=mur[:, c:c + 1],
                                           in1=xr[bi][:, cc], op0=ALU.mult, op1=ALU.add)
                    e.tensor_scalar_mul(out=xk[bi][:, cc], in0=ln.prev(c), scalar1=omk[:, c:c + 1])
                    last = e.scalar_tensor_tensor(out=xk[bi][:, cc], in0=ln.cur(c), scalar=muk[:, c:c + 1],
                                                  in1=xk[bi][:, cc], op0=ALU.mult, op1=ALU.add)
                last.then_inc(ev2c.sem, 1)
            if cap is None:
                sch.on("vector", _mix)
            else:
                cap.setdefault("mix", []).append(("vector", _mix))
            ev_mix_t[g] = mt

        do_prep(0, None)
        for g in range(NG):
            s, j = divmod(g, NTILE)
            bi = g % 2
            cap = {} if g + 1 < NG else None
            if cap is not None:
                do_prep(g + 1, cap)
            def play(sec, cap=cap):
                if cap is not None:
                    for e_, f_ in cap.get(sec, []):
                        sch.on(e_, f_)
            play("shift")
            play("sq")
            play("rows")

            def _mm_r0(e, t=ev_mix_t[g]):
                e.wait_ge(ev2c.sem, t)
                wait_weights(e)
            sch.on("tensor", _mm_r0)
            ac_r = []
            for m in range(NFC):
                pb = psm[m % 2]
                if m < 2:
                    if (g - 1) in acr_hist:
                        sch.on("tensor", lambda e, t=acr_hist[g - 1]: e.wait_ge(ac.sem, t))
                else:
                    sch.on("tensor", lambda e, t=ac_r[m - 2]: e.wait_ge(ac.sem, t))
                mmt = mm.bump()
                def _mmr(e, m=m, pb=pb, bi=bi):
                    for kc in range(NFC):
                        i = e.matmul(out=pb[:], lhsT=wr[:, kc * D + m * 128: kc * D + (m + 1) * 128],
                                     rhs=xr[bi][:, kc * TT:(kc + 1) * TT], start=(kc == 0), stop=(kc == NFC - 1))
                    i.then_inc(mm.sem, 1)
                sch.on("tensor", _mmr)
                at = ac.bump()
                def _ev(e, mmt=mmt, m=m, pb=pb, bi=bi):
                    e.wait_ge(mm.sem, mmt)
                    e.activation(out=xbf[bi][:, m * TT:(m + 1) * TT], in_=pb[:], func=AF.Sigmoid).then_inc(ac.sem, 1)
                sch.on("scalar", _ev)
                ac_r.append(at)
            acr_hist[g] = ac_r[-1]
            play("stats")

            ev_k = []
            for m in range(NFC):
                pb = psm[2 + m % 2]
                if m < 2:
                    if (g - 1) in evo_hist:
                        sch.on("tensor", lambda e, t=evo_hist[g - 1] - 1 + m: e.wait_ge(ev.sem, t))
                else:
                    sch.on("tensor", lambda e, t=ev_k[m - 2]: e.wait_ge(ev.sem, t))
                mmt = mm.bump()
                def _mmk(e, m=m, pb=pb, bi=bi):
                    for kc in range(NFC):
                        i = e.matmul(out=pb[:], lhsT=wk[:, kc * D + m * 128: kc * D + (m + 1) * 128],
                                     rhs=xk[bi][:, kc * TT:(kc + 1) * TT], start=(kc == 0), stop=(kc == NFC - 1))
                    i.then_inc(mm.sem, 1)
                sch.on("tensor", _mmk)
                et = ev.bump()
                def _evk(e, mmt=mmt, m=m, pb=pb):
                    e.wait_ge(mm.sem, mmt)
                    cc = slice(m * TT, (m + 1) * TT)
                    e.tensor_scalar_max(out=kf[:], in0=pb[:], scalar1=0.0)
                    e.tensor_mul(out=kact[:, cc], in0=kf[:], in1=kf[:]).then_inc(ev.sem, 1)
                sch.on("vector", _evk)
                ev_k.append(et)
            kmm_hist[g] = mm.n
            play("sd")
            play("bc")
            play("norm")

            def _mm_v(e, t=ev_k[-1]):
                e.wait_ge(ev.sem, t)
            sch.on("tensor", _mm_v)
            ev_o = []
            for m in range(NFC):
                pb = psm[2 + m % 2]
                if m < 2:
                    sch.on("tensor", lambda e, t=ac_r[-1]: e.wait_ge(ac.sem, t))
                else:
                    sch.on("tensor", lambda e, t=ev_o[m - 2]: e.wait_ge(ev.sem, t))
                mmt = mm.bump()
                def _mmv(e, m=m, pb=pb):
                    for kc in range(NFC):
                        i = e.matmul(out=pb[:], lhsT=wv[:, kc * D + m * 128: kc * D + (m + 1) * 128],
                                     rhs=kact[:, kc * TT:(kc + 1) * TT], start=(kc == 0), stop=(kc == NFC - 1))
                    i.then_inc(mm.sem, 1)
                sch.on("tensor", _mmv)
                if m == 0 and (g - 1) in st_hist:
                    sl, t = st_hist[g - 1]
                    sch.on("vector", lambda e, sl=sl, t=t: e.wait_ge(env.sems[sl].sem, t))
                et = ev.bump()
                def _evo(e, mmt=mmt, m=m, pb=pb, bi=bi):
                    e.wait_ge(mm.sem, mmt)
                    cc = slice(m * TT, (m + 1) * TT)
                    e.tensor_mul(out=xout[:, cc], in0=xbf[bi][:, cc], in1=pb[:])
                    e.tensor_add(out=xout[:, cc], in0=xout[:, cc],
                                 in1=xf[bi][:, cc]).then_inc(ev.sem, 1)
                sch.on("vector", _evo)
                ev_o.append(et)
                if m == 1:
                    play("mix")
            vmm_hist[g] = mm.n

            ssl = "st0" if bi == 0 else "st1"
            sch.on("gpsimd", lambda e, t=ev_o[-1]: e.wait_ge(ev.sem, t))
            dst = xdst[s][:, :, j * TT:(j + 1) * TT].rearrange("c p t -> p c t")
            st_t = slot_dma(env, sch, "gpsimd", ssl, dst,
                            xout.rearrange("p (c t) -> p c t", c=NFC))
            evo_hist[g] = ev_o[-1]
            st_hist[g] = (ssl, st_t)

        def _drain(e):
            e.wait_ge(env.sems["st0"].sem, env.sems["st0"].n)
            e.wait_ge(env.sems["st1"].sem, env.sems["st1"].n)
        sch.on("gpsimd", _drain)
        emit_barrier(env, sch)
        sch.flush(nc)





def emit_p0_wbig(env, sch):
    nc = env.nc
    ldh, mm, ev = env.sems["ldh"], env.sems["mm"], env.sems["ev"]
    wbig = env.ctx.enter_context(nc.sbuf_tensor("wbig", [128, NFC * D], BF16))
    env.consts["wbig"] = wbig[:, :]
    identf = env.consts["identf"]
    indd = env.consts["ind_d"]
    indg = env.consts["ind_g"]
    with ExitStack() as stk:
        g1t = {m: env.sb(stk, f"p0_g1{m}", [32, 128], F32) for m in "RI"}
        g2t = {m: env.sb(stk, f"p0_g2{m}", [64, 16], F32) for m in "RI"}
        g1T = {m: env.sb(stk, f"p0_g1T{m}", [32, 128], F32) for m in "RI"}
        g2T = {m: env.sb(stk, f"p0_g2T{m}", [16, 64], F32) for m in "RI"}
        Usb = {m: env.sb(stk, f"p0_U{m}", [128, 256], F32) for m in "RI"}
        Vsb = {m: env.sb(stk, f"p0_V{m}", [128, 512], F32) for m in "RI"}
        wtmp = env.sb(stk, "p0_wtmp", [128, 512], F32)
        ps1 = env.ps(stk, "p0_ps1", [128, 512])
        ps2 = env.ps(stk, "p0_ps2", [128, 512])

        for m, nm1, nm2 in (("R", "w6_G1_R", "w6_G2_R"), ("I", "w6_G1_I", "w6_G2_I")):
            ldh.bump(16)
            sch.on("sync", lambda e, d=g1t[m][:32, :], s=env.dram[nm1][0].rearrange("b c d -> b (c d)"):
                   e.dma_start(out=d, in_=s).then_inc(ldh.sem, 16))
            ldh.bump(16)
            sch.on("sync", lambda e, d=g2t[m][:64, :], s=env.dram[nm2].rearrange("c e f g -> (c e f) g"):
                   e.dma_start(out=d, in_=s).then_inc(ldh.sem, 16))
        ld_all = ldh.n

        prev_evuv = None
        for m in "RI":
            mm_t = mm.bump()
            def _tr(e, m=m, ld_all=ld_all, prev=prev_evuv):
                e.wait_ge(ldh.sem, ld_all)
                if prev is not None:
                    e.wait_ge(ev.sem, prev)
                for c in range(4):
                    e.transpose(out=ps1[:32, c * 32:(c + 1) * 32], in_=g1t[m][:32, c * 32:(c + 1) * 32],
                                identity=identf[:32, :32])
                e.transpose(out=ps2[:16, 0:64], in_=g2t[m][:64, :],
                            identity=identf[:64, :64]).then_inc(mm.sem, 1)
            sch.on("tensor", _tr)
            ev_t = ev.bump()
            def _ev(e, m=m, mm_t=mm_t):
                e.wait_ge(mm.sem, mm_t)
                e.tensor_copy(out=g1T[m][:32, :], in_=ps1[:32, 0:128])
                e.tensor_copy(out=g2T[m][:16, :], in_=ps2[:16, 0:64]).then_inc(ev.sem, 1)
            sch.on("vector", _ev)
            mm_uv = mm.bump()
            def _uv(e, m=m, ev_t=ev_t):
                e.wait_ge(ev.sem, ev_t)
                for c in range(4):
                    for r in range(4):
                        e.matmul(out=ps1[:, c * 64 + r * 16: c * 64 + (r + 1) * 16],
                                 lhsT=indg[:16, r * 128:(r + 1) * 128],
                                 rhs=g2T[m][:16, c * 16:(c + 1) * 16], start=True, stop=True)
                        i = e.matmul(out=ps2[:, c * 128 + r * 32: c * 128 + (r + 1) * 32],
                                     lhsT=indd[:32, r * 128:(r + 1) * 128],
                                     rhs=g1T[m][:32, c * 32:(c + 1) * 32], start=True, stop=True)
                i.then_inc(mm.sem, 1)
            sch.on("tensor", _uv)
            ev_uv = ev.bump()
            def _evuv(e, m=m, mm_uv=mm_uv):
                e.wait_ge(mm.sem, mm_uv)
                e.tensor_copy(out=Usb[m][:], in_=ps1[:, 0:256])
                e.tensor_copy(out=Vsb[m][:], in_=ps2[:, 0:512]).then_inc(ev.sem, 1)
            sch.on("vector", _evuv)
            prev_evuv = ev_uv

        ev_done = ev.bump()
        def _build(e):
            for m in "RI":
                for r in range(4):
                    for b in range(32):
                        dst = wtmp[:, b * 16:(b + 1) * 16]
                        e.tensor_scalar_mul(out=dst, in0=Usb[m][:, 0 + r * 16: 0 + (r + 1) * 16],
                                            scalar1=Vsb[m][:, 0 + r * 32 + b: 0 + r * 32 + b + 1])
                        for c in range(1, 4):
                            e.scalar_tensor_tensor(out=dst, in0=Usb[m][:, c * 64 + r * 16: c * 64 + (r + 1) * 16],
                                                   scalar=Vsb[m][:, c * 128 + r * 32 + b: c * 128 + r * 32 + b + 1],
                                                   in1=dst, op0=ALU.mult, op1=ALU.add)
                    if m == "R":
                        e.tensor_copy(out=wbig[:, r * D + 0:r * D + 512], in_=wtmp[:])
                        e.tensor_copy(out=wbig[:, (r + 4) * D + 512:(r + 4) * D + 1024], in_=wtmp[:])
                    else:
                        e.tensor_copy(out=wbig[:, r * D + 512:r * D + 1024], in_=wtmp[:])
                        e.tensor_scalar_mul(out=wbig[:, (r + 4) * D + 0:(r + 4) * D + 512],
                                            in0=wtmp[:], scalar1=-1.0)
            e.tensor_copy(out=wtmp[:, 0:16], in_=wtmp[:, 0:16]).then_inc(ev.sem, 1)
        sch.on("vector", _build)
        emit_barrier(env, sch)
        sch.flush(nc)
    return ev_done



def emit_p_l1ffn(env, sch, xsrc, xdst):
    nc = env.nc
    ldh, mm, ev, ac = (env.sems[k] for k in ("ldh", "mm", "ev", "ac"))
    wbig = env.consts["wbig"]
    with ExitStack() as stk:
        xf = [env.sb(stk, f"p6_xf{i}", [128, NFC * TT], F32) for i in range(2)]
        xbf = [env.sb(stk, f"p6_xbf{i}", [128, NFC * TT], BF16) for i in range(2)]
        xout = [env.sb(stk, f"p6_xout{i}", [128, NFC * TT], F32) for i in range(2)]
        ln2 = [LNBufs(env, stk, "p6_lnA")]
        ln2.append(LNBufs(env, stk, "p6_lnB", share_ps=ln2[0]))
        b6 = env.sb(stk, "p6_b6", [128, NFC], F32)
        psm = [env.ps(stk, f"p6_ps{i}", [128, TT]) for i in range(4)]

        wt_b = ldh.bump(16)
        sch.on("sync", lambda e: e.dma_start(out=b6[:], in_=env.dram["p_b6"]).then_inc(ldh.sem, 16))

        NG = BL * NTILE
        evln, lninfo, lastmm, evo_hist, st_hist = {}, {}, {}, {}, {}
        mmc, ev2c, mm2c, ac2c = (env.sems[k] for k in ("mm", "ev2", "mm2", "ac2"))

        def do_ln(g, cap):
            s, j = divmod(g, NTILE)
            bi = g % 2
            if g >= 2 and (g - 2) in evo_hist:
                sch.on("sync", lambda e, t=evo_hist[g - 2]: e.wait_ge(ev.sem, t))
                sch.on("gpsimd", lambda e, t=evo_hist[g - 2]: e.wait_ge(ev.sem, t))
            sf, sb = ("s0", "g0") if bi == 0 else ("s1", "g1")
            ld_t, ldg_t = emit_tile_loads(env, sch, xsrc[s], j, xf[bi], xbf[bi], sf, sb)
            kw = {"cap": cap, "sems": ("mm2", "ev2", "ac2")}
            if g >= 1:
                kw["sq_guard"] = (mm2c, lninfo[g - 1]["mm_st"])
                kw["rows_guard"] = [(mm2c, lninfo[g - 1]["mm_bc"]), (ac2c, lninfo[g - 1]["ac_rs"])]
                kw["bc_guard"] = (ev2c, evln[g - 1])
            if g >= 2:
                kw["xln_guard"] = (mmc, lastmm[g - 2])
            evln[g], lninfo[g] = emit_ln(env, sch, ln2[bi], j, xf[bi], xbf[bi], (sf, ld_t), (sb, ldg_t), **kw)

        do_ln(0, None)
        for g in range(NG):
            s, j = divmod(g, NTILE)
            bi = g % 2
            ln = ln2[bi]
            cap = {} if g + 1 < NG else None
            if cap is not None:
                do_ln(g + 1, cap)
            def play(sec, cap=cap):
                if cap is not None:
                    for e_, f_ in cap.get(sec, []):
                        sch.on(e_, f_)
            play("shift")
            play("sq")
            sch.on("tensor", lambda e, t=evln[g]: e.wait_ge(ev2c.sem, t))
            ev_o = []
            for m in range(NFC):
                if m == 2:
                    play("stats")
                    play("rows")
                if m == 6:
                    play("sd")
                    play("bc")
                    play("norm")
                pb = psm[2 * bi + m % 2]
                if m < 2:
                    if (g - 1) in st_hist and m == 0:
                        sl, t = st_hist[g - 1]
                        sch.on("vector", lambda e, sl=sl, t=t: e.wait_ge(env.sems[sl].sem, t))
                    if (g - 2) in evo_hist:
                        sch.on("tensor", lambda e, t=evo_hist[g - 2]: e.wait_ge(ev.sem, t))
                else:
                    sch.on("tensor", lambda e, t=ev_o[m - 2]: e.wait_ge(ev.sem, t))
                mmt = mm.bump()
                def _mm(e, m=m, pb=pb, ln=ln):
                    for kc in range(NFC):
                        i = e.matmul(out=pb[:], lhsT=wbig[:, kc * D + m * 128: kc * D + (m + 1) * 128],
                                     rhs=ln.xln[:, kc * TT:(kc + 1) * TT],
                                     start=(kc == 0), stop=(kc == NFC - 1))
                    i.then_inc(mm.sem, 1)
                sch.on("tensor", _mm)
                et = ev.bump()
                def _evo(e, mmt=mmt, m=m, pb=pb, bi=bi, wt_b=wt_b):
                    e.wait_ge(mm.sem, mmt)
                    e.wait_ge(ldh.sem, wt_b)
                    cc = slice(m * TT, (m + 1) * TT)
                    e.scalar_tensor_tensor(out=xout[bi][:, cc], in0=pb[:], scalar=b6[:, m:m + 1],
                                           in1=xf[bi][:, cc], op0=ALU.add, op1=ALU.add).then_inc(ev.sem, 1)
                sch.on("vector", _evo)
                ev_o.append(et)
            lastmm[g] = mm.n
            ssl = "st0" if bi == 0 else "st1"
            sch.on("gpsimd", lambda e, t=ev_o[-1]: e.wait_ge(ev.sem, t))
            dst = xdst[s][:, :, j * TT:(j + 1) * TT].rearrange("c p t -> p c t")
            st_t = slot_dma(env, sch, "gpsimd", ssl, dst,
                            xout[bi].rearrange("p (c t) -> p c t", c=NFC))
            evo_hist[g] = ev_o[-1]
            st_hist[g] = (ssl, st_t)

        def _drain(e):
            e.wait_ge(env.sems["st0"].sem, env.sems["st0"].n)
            e.wait_ge(env.sems["st1"].sem, env.sems["st1"].n)
        sch.on("gpsimd", _drain)
        emit_barrier(env, sch)
        sch.flush(nc)

def emit_p_l0attna(env, sch, xsrc, xdst):
    nc = env.nc
    ldh, mm, ev, ac = (env.sems[k] for k in ("ldh", "mm", "ev", "ac"))
    with ExitStack() as stk:
        xf = env.sb(stk, "p1_xf", [128, NFC * TT], F32)
        xbf = env.sb(stk, "p1_xbf", [128, NFC * TT], BF16)
        xout = env.sb(stk, "p1_xout", [128, NFC * TT], F32)
        ln = LNBufs(env, stk, "p1_ln")
        w1i = env.sb(stk, "p1_w1i", [128, NFC * 256], BF16)
        w1si = env.sb(stk, "p1_w1si", [128, NFC * 512], BF16)
        w1so = env.sb(stk, "p1_w1so", [128, 4 * D], BF16)
        w1o = env.sb(stk, "p1_w1o", [128, 2 * D], BF16)
        dparam = env.sb(stk, "p1_dp", [128, 2], F32)
        usb = env.sb(stk, "p1_usb", [128, 2 * TT], F32)
        tnb = env.sb(stk, "p1_tnb", [128, 2 * TT], F32)
        dgb = env.sb(stk, "p1_dgb", [128, 2 * TT], F32)
        selh = env.sb(stk, "p1_selh", [128, 4 * TT], BF16)
        asb = env.sb(stk, "p1_asb", [128, 2 * T], F32)
        busb = env.sb(stk, "p1_busb", [128, 2 * T], F32)
        csb = env.sb(stk, "p1_csb", [128, 2 * T], F32)
        dsb = env.sb(stk, "p1_dsb", [128, 2 * T], F32)
        hsb = xout
        ysb = busb[:, :].bitcast(BF16)[:, 0:2 * T]
        psm = [env.ps(stk, f"p1_ps{i}", [128, TT]) for i in range(2)]

        wt_w = {}
        for wname, wdst, dout, nkc in (("w1_inT", w1i, 256, NFC), ("w1_siT", w1si, 512, NFC),
                                       ("w1_soT", w1so, D, 4), ("w1_outT", w1o, D, 2)):
            for sl, t in emit_weight_load_T(env, sch, env.dram[wname], wdst, dout, nkc=nkc):
                wt_w[sl] = max(wt_w.get(sl, 0), t)
        wt_dp = ldh.bump(16)
        sch.on("sync", lambda e: e.dma_start(out=dparam[:], in_=env.dram["p_dparam"]).then_inc(ldh.sem, 16))

        def wait_weights(e):
            for sl, t in wt_w.items():
                e.wait_ge(env.sems[sl].sem, t)

        gcnt = 0
        evo_hist = {}
        st_hist = []
        prev_evo_p2 = None
        for s in range(BL):
            prev_vec_read = None
            prev_ln = None
            for j in range(NTILE):
                bi = gcnt % 2
                sb = "g0" if bi == 0 else "g1"
                srcx = xsrc[s][:, :, j * TT:(j + 1) * TT].rearrange("c p t -> p c t")
                if prev_ln is not None:
                    sch.on("gpsimd", lambda e, t=prev_ln: e.wait_ge(env.sems["ev"].sem, t))
                ldg_t = slot_dma(env, sch, "gpsimd", sb, xbf.rearrange("p (c t) -> p c t", c=NFC), srcx)
                ev_ln, _lninfo = emit_ln(env, sch, ln, j, None, xbf, (None, None), (sb, ldg_t), norm_src="bf")
                prev_ln = ev_ln

                grp_done = []
                def ps_guard(idx):
                    if len(grp_done) >= 2:
                        cse, ct = grp_done[-2]
                        sch.on("tensor", lambda e, cse=cse, ct=ct: e.wait_ge(cse.sem, ct))

                sch.on("tensor", lambda e, t=ev_ln: (e.wait_ge(ev.sem, t), wait_weights(e)))
                for m in range(2):
                    pb = psm[m % 2]
                    ps_guard(m)
                    mmt = mm.bump()
                    def _mmu(e, m=m, pb=pb):
                        for kc in range(NFC):
                            i = e.matmul(out=pb[:], lhsT=w1i[:, kc * 256 + m * 128: kc * 256 + (m + 1) * 128],
                                         rhs=ln.xln[:, kc * TT:(kc + 1) * TT], start=(kc == 0), stop=(kc == NFC - 1))
                        i.then_inc(mm.sem, 1)
                    sch.on("tensor", _mmu)
                    et = ev.bump()
                    def _evu(e, mmt=mmt, m=m, pb=pb):
                        e.wait_ge(mm.sem, mmt)
                        e.tensor_copy(out=usb[:, m * TT:(m + 1) * TT], in_=pb[:]).then_inc(ev.sem, 1)
                    sch.on("vector", _evu)
                    grp_done.append((ev, et))
                for m in range(4):
                    pb = psm[m % 2]
                    ps_guard(2 + m)
                    mmt = mm.bump()
                    def _mms(e, m=m, pb=pb):
                        for kc in range(NFC):
                            i = e.matmul(out=pb[:], lhsT=w1si[:, kc * 512 + m * 128: kc * 512 + (m + 1) * 128],
                                         rhs=ln.xln[:, kc * TT:(kc + 1) * TT], start=(kc == 0), stop=(kc == NFC - 1))
                        i.then_inc(mm.sem, 1)
                    sch.on("tensor", _mms)
                    at = ac.bump()
                    def _evs(e, mmt=mmt, m=m, pb=pb):
                        e.wait_ge(mm.sem, mmt)
                        e.activation(out=selh[:, m * TT:(m + 1) * TT], in_=pb[:], func=AF.Silu).then_inc(ac.sem, 1)
                    sch.on("scalar", _evs)
                    grp_done.append((ac, at))
                jcols = slice(j * TT, (j + 1) * TT)
                for m in range(NFC):
                    pb = psm[m % 2]
                    ps_guard(6 + m)
                    mmt = mm.bump()
                    def _mmg(e, m=m, pb=pb, t=grp_done[5][1]):
                        if m == 0:
                            e.wait_ge(ac.sem, t)
                        for kc in range(4):
                            i = e.matmul(out=pb[:], lhsT=w1so[:, kc * D + m * 128: kc * D + (m + 1) * 128],
                                         rhs=selh[:, kc * TT:(kc + 1) * TT], start=(kc == 0), stop=(kc == 3))
                        i.then_inc(mm.sem, 1)
                    sch.on("tensor", _mmg)
                    c = m % 2
                    if m < 2:
                        dst = asb[:, c * T + j * TT: c * T + (j + 1) * TT]
                        fn = AF.Sigmoid
                    elif m < 4:
                        dst = tnb[:, c * TT:(c + 1) * TT]
                        fn = AF.Tanh
                    elif m < 6:
                        dst = csb[:, c * T + j * TT: c * T + (j + 1) * TT]
                        fn = AF.Tanh
                    else:
                        dst = dgb[:, c * TT:(c + 1) * TT]
                        fn = AF.Sigmoid
                    at = ac.bump()
                    def _evg(e, mmt=mmt, dst=dst, fn=fn, pb=pb, m=m, prev=prev_vec_read):
                        e.wait_ge(mm.sem, mmt)
                        if m in (2, 6) and prev is not None:
                            e.wait_ge(ev.sem, prev)
                        e.activation(out=dst, in_=pb[:], func=fn).then_inc(ac.sem, 1)
                    sch.on("scalar", _evg)
                    grp_done.append((ac, at))
                ac_all = grp_done[-1][1]
                et = ev.bump()
                def _vmix(e, ac_all=ac_all, j=j, wt_dp=wt_dp):
                    e.wait_ge(ac.sem, ac_all)
                    e.wait_ge(ldh.sem, wt_dp)
                    last = None
                    for c in range(2):
                        dst = slice(c * T + j * TT, c * T + (j + 1) * TT)
                        cc = slice(c * TT, (c + 1) * TT)
                        e.tensor_mul(out=busb[:, dst], in0=tnb[:, cc], in1=usb[:, cc])
                        e.tensor_mul(out=dsb[:, dst], in0=dgb[:, cc], in1=usb[:, cc])
                        last = e.tensor_scalar_mul(out=dsb[:, dst], in0=dsb[:, dst],
                                                   scalar1=dparam[:, c:c + 1])
                    last.then_inc(ev.sem, 1)
                sch.on("vector", _vmix)
                prev_vec_read = et
                evo_hist[gcnt] = et
                gcnt += 1

            ev_sc = ev.bump()
            ev_y = ev.bump()
            st0n, st1n = env.sems["st0"].n, env.sems["st1"].n
            def _scan(e, ac_last=grp_done[-1][1], st0n=st0n, st1n=st1n, ev_sc=ev_sc):
                e.wait_ge(ac.sem, ac_last)
                e.wait_ge(env.sems["st0"].sem, st0n)
                e.wait_ge(env.sems["st1"].sem, st1n)
                for c in range(2):
                    cs = slice(c * T, (c + 1) * T)
                    i = e.tensor_tensor_scan(out=hsb[:, cs], data0=asb[:, cs], data1=busb[:, cs],
                                             initial=0.0, op0=ALU.mult, op1=ALU.add)
                i.then_inc(ev.sem, 1)
                e.wait_ge(ev.sem, ev_sc)
                last = None
                for c in range(2):
                    cs = slice(c * T, (c + 1) * T)
                    e.tensor_mul(out=hsb[:, cs], in0=csb[:, cs], in1=hsb[:, cs])
                    last = e.tensor_add(out=ysb[:, cs], in0=hsb[:, cs], in1=dsb[:, cs])
                last.then_inc(ev.sem, 1)
            sch.on("vector", _scan)

            for j in range(NTILE):
                bi = gcnt % 2
                if prev_evo_p2 is not None:
                    sch.on("sync", lambda e, t=prev_evo_p2[-1]: e.wait_ge(env.sems["ev"].sem, t))
                sf = "s0" if bi == 0 else "s1"
                srcx = xsrc[s][:, :, j * TT:(j + 1) * TT].rearrange("c p t -> p c t")
                ld_t = slot_dma(env, sch, "sync", sf, xf.rearrange("p (c t) -> p c t", c=NFC), srcx)
                sch.on("tensor", lambda e, t=ev_y: e.wait_ge(ev.sem, t))
                ev_o = []
                for m in range(NFC):
                    pb = psm[m % 2]
                    if m >= 2:
                        sch.on("tensor", lambda e, t=ev_o[m - 2]: e.wait_ge(ev.sem, t))
                    elif prev_evo_p2 is not None:
                        sch.on("tensor", lambda e, t=prev_evo_p2[6 + m]: e.wait_ge(ev.sem, t))
                    mmt = mm.bump()
                    def _mmo(e, m=m, pb=pb, j=j):
                        for kc in range(2):
                            i = e.matmul(out=pb[:], lhsT=w1o[:, kc * D + m * 128: kc * D + (m + 1) * 128],
                                         rhs=ysb[:, kc * T + j * TT: kc * T + (j + 1) * TT],
                                         start=(kc == 0), stop=(kc == 1))
                        i.then_inc(mm.sem, 1)
                    sch.on("tensor", _mmo)
                    if m == 0 and len(st_hist) >= 1:
                        sl, t = st_hist[-1]
                        sch.on("vector", lambda e, sl=sl, t=t: e.wait_ge(env.sems[sl].sem, t))
                    et = ev.bump()
                    def _evo(e, mmt=mmt, m=m, pb=pb, bi=bi, ld_t=ld_t, sf=sf):
                        e.wait_ge(mm.sem, mmt)
                        if m == 0:
                            e.wait_ge(env.sems[sf].sem, ld_t)
                        cc = slice(m * TT, (m + 1) * TT)
                        e.tensor_add(out=xout[:, cc], in0=pb[:], in1=xf[:, cc]).then_inc(ev.sem, 1)
                    sch.on("vector", _evo)
                    ev_o.append(et)
                ssl = "st0" if bi == 0 else "st1"
                sch.on("gpsimd", lambda e, t=ev_o[-1]: e.wait_ge(ev.sem, t))
                dst = xdst[s][:, :, j * TT:(j + 1) * TT].rearrange("c p t -> p c t")
                st_t = slot_dma(env, sch, "gpsimd", ssl, dst,
                                xout.rearrange("p (c t) -> p c t", c=NFC))
                st_hist.append((ssl, st_t))
                evo_hist[gcnt] = ev_o[-1]
                prev_evo_p2 = ev_o
                gcnt += 1

        def _drain(e):
            e.wait_ge(env.sems["st0"].sem, env.sems["st0"].n)
            e.wait_ge(env.sems["st1"].sem, env.sems["st1"].n)
        sch.on("gpsimd", _drain)
        emit_barrier(env, sch)
        sch.flush(nc)




def emit_p_l1attnb(env, sch, xsrc, xdst):
    nc = env.nc
    ldh, mm, ev, ac = (env.sems[k] for k in ("ldh", "mm", "ev", "ac"))
    identb = env.consts["identb"]
    ones_row = env.consts["ones_row_f"]
    q_s = nc.dram_tensor("p7_qs", [NFC, 128, T], BF16, kind="Internal").ap()
    with ExitStack() as stk:
        ksb = env.sb(stk, "p7_k", [128, NFC * T], BF16)
        vtok = env.sb(stk, "p7_v", [128, 16 * D], BF16)
        b7qkv = env.sb(stk, "p7_bqkv", [128, 24], F32)
        b7out = env.sb(stk, "p7_bout", [128, NFC], F32)
        vbias = env.sb(stk, "p7_vbias", [128, D], F32)
        vbrow = env.sb(stk, "p7_vbrow", [1, D], F32)

        ldh.bump(16)
        sch.on("sync", lambda e: e.dma_start(out=b7qkv[:], in_=env.dram["p_b7qkv"]).then_inc(ldh.sem, 16))
        ldh.bump(16)
        sch.on("sync", lambda e: e.dma_start(out=b7out[:], in_=env.dram["p_b7out"]).then_inc(ldh.sem, 16))
        bias_ld = ldh.bump(16)
        sch.on("sync", lambda e: e.dma_start(out=vbrow[0:1, :], in_=env.dram["p_b7v_row"]).then_inc(ldh.sem, 16))

        st_hist = []

        fence_ev = [None]

        def stage_a(s):
            if fence_ev[0] is not None:
                for eng in ("sync", "gpsimd", "tensor"):
                    sch.on(eng, lambda e, t=fence_ev[0]: e.wait_ge(ev.sem, t))
                def _wst(e):
                    e.wait_ge(env.sems["st0"].sem, env.sems["st0"].n)
                    e.wait_ge(env.sems["st1"].sem, env.sems["st1"].n)
                st0n, st1n = env.sems["st0"].n, env.sems["st1"].n
                for eng in ("sync", "gpsimd"):
                    sch.on(eng, lambda e, a=st0n, b=st1n: (e.wait_ge(env.sems["st0"].sem, a),
                                                           e.wait_ge(env.sems["st1"].sem, b)))
            with ExitStack() as stkA:
                xbf = env.sb(stkA, f"p7_xbf{s}", [128, NFC * TT], BF16)
                ln = LNBufs(env, stkA, f"p7_ln{s}")
                w7q = env.sb(stkA, f"p7_wqkv{s}", [128, NFC * 3072], BF16)
                qacc = env.sb(stkA, f"p7_qacc{s}", [128, NFC * TT], BF16)
                psm = [env.ps(stkA, f"p7_psA{i}_{s}", [128, TT]) for i in range(2)]
                prev_qst = None
                wt_q = {}
                for sl, t in emit_weight_load_T(env, sch, env.dram["w7_qkv"], w7q, 3072):
                    wt_q[sl] = max(wt_q.get(sl, 0), t)
                mm_vb = mm.bump(2)
                def _vb(e, fence=fence_ev[0]):
                    e.wait_ge(ldh.sem, bias_ld)
                    if fence is not None:
                        e.wait_ge(ev.sem, fence)
                    e.matmul(out=psm[0][:], lhsT=ones_row[0:1, :], rhs=vbrow[0:1, 0:512],
                             start=True, stop=True).then_inc(mm.sem, 1)
                    e.matmul(out=psm[1][:], lhsT=ones_row[0:1, :], rhs=vbrow[0:1, 512:1024],
                             start=True, stop=True).then_inc(mm.sem, 1)
                sch.on("tensor", _vb)
                ev_vb = ev.bump()
                def _evb(e):
                    e.wait_ge(mm.sem, mm_vb)
                    e.tensor_copy(out=vbias[:, 0:512], in_=psm[0][:])
                    e.tensor_copy(out=vbias[:, 512:1024], in_=psm[1][:]).then_inc(ev.sem, 1)
                sch.on("vector", _evb)
                grp = [(ev, ev_vb)]

                prev_ln = None
                for j in range(NTILE):
                    sb_ = "g0" if j % 2 == 0 else "g1"
                    srcx = xsrc[s][:, :, j * TT:(j + 1) * TT].rearrange("c p t -> p c t")
                    if prev_ln is not None:
                        sch.on("gpsimd", lambda e, t=prev_ln: e.wait_ge(ev.sem, t))
                    ldg_t = slot_dma(env, sch, "gpsimd", sb_, xbf.rearrange("p (c t) -> p c t", c=NFC), srcx)
                    ev_ln, _lninfo = emit_ln(env, sch, ln, j, None, xbf, (None, None), (sb_, ldg_t), norm_src="bf")
                    prev_ln = ev_ln
                    def _wq(e, t=ev_ln):
                        e.wait_ge(ev.sem, t)
                        for sl, tt in wt_q.items():
                            e.wait_ge(env.sems[sl].sem, tt)
                    sch.on("tensor", _wq)
                    for m in range(16):
                        pb = psm[m % 2]
                        if len(grp) >= 2:
                            cse, ct = grp[-2]
                            sch.on("tensor", lambda e, cse=cse, ct=ct: e.wait_ge(cse.sem, ct))
                        mmt = mm.bump()
                        def _mmqk(e, m=m, pb=pb):
                            for kc in range(NFC):
                                i = e.matmul(out=pb[:], lhsT=w7q[:, kc * 3072 + m * 128: kc * 3072 + (m + 1) * 128],
                                             rhs=ln.xln[:, kc * TT:(kc + 1) * TT], start=(kc == 0), stop=(kc == NFC - 1))
                            i.then_inc(mm.sem, 1)
                        sch.on("tensor", _mmqk)
                        et = ev.bump()
                        def _evqk(e, mmt=mmt, m=m, pb=pb, j=j, pq=prev_qst):
                            e.wait_ge(mm.sem, mmt)
                            if m < 8:
                                if m == 0 and pq is not None:
                                    e.wait_ge(env.sems["st0"].sem, pq)
                                dst = qacc[:, m * TT:(m + 1) * TT]
                                e.tensor_scalar(out=dst, in0=pb[:], scalar1=b7qkv[:, m:m + 1],
                                                scalar2=1.0 / 32.0, op0=ALU.add, op1=ALU.mult).then_inc(ev.sem, 1)
                            else:
                                dst = ksb[:, (m - 8) * T + j * TT: (m - 8) * T + (j + 1) * TT]
                                e.tensor_scalar(out=dst, in0=pb[:], scalar1=b7qkv[:, m:m + 1],
                                                scalar2=None, op0=ALU.add).then_inc(ev.sem, 1)
                        sch.on("vector", _evqk)
                        grp.append((ev, et))
                        if m == 7:
                            sch.on("gpsimd", lambda e, t=et: e.wait_ge(ev.sem, t))
                            prev_qst = slot_dma(env, sch, "gpsimd", "st0",
                                                q_s[:, :, j * TT:(j + 1) * TT].rearrange("c p t -> p c t"),
                                                qacc.rearrange("p (c t) -> p c t", c=NFC))
                    for sc in range(4):
                        for vd in range(2):
                            pb = psm[(sc * 2 + vd) % 2]
                            cse, ct = grp[-2]
                            sch.on("tensor", lambda e, cse=cse, ct=ct: e.wait_ge(cse.sem, ct))
                            mmt = mm.bump()
                            def _mmv(e, sc=sc, vd=vd, pb=pb):
                                for kc in range(NFC):
                                    i = e.matmul(out=pb[:],
                                                 lhsT=ln.xln[:, kc * TT + sc * 128: kc * TT + (sc + 1) * 128],
                                                 rhs=w7q[:, kc * 3072 + 2048 + vd * 512: kc * 3072 + 2048 + (vd + 1) * 512],
                                                 start=(kc == 0), stop=(kc == NFC - 1))
                                i.then_inc(mm.sem, 1)
                            sch.on("tensor", _mmv)
                            et = ev.bump()
                            def _evv(e, mmt=mmt, sc=sc, vd=vd, pb=pb, j=j):
                                e.wait_ge(mm.sem, mmt)
                                dst = vtok[:, (j * 4 + sc) * D + vd * 512: (j * 4 + sc) * D + (vd + 1) * 512]
                                e.tensor_add(out=dst, in0=pb[:], in1=vbias[:, vd * 512:(vd + 1) * 512]).then_inc(ev.sem, 1)
                            sch.on("vector", _evv)
                            grp.append((ev, et))
                return grp[-1][1]

        def stage_b(s, lastA):
            for eng in ("sync", "gpsimd"):
                sch.on(eng, lambda e, t=lastA: e.wait_ge(ev.sem, t))
            with ExitStack() as stkB:
                w7out = env.sb(stkB, f"p7_wout{s}", [128, NFC * D], BF16)
                wt_o = {}
                for sl, t in emit_weight_load_T(env, sch, env.dram["w7_out"], w7out, D):
                    wt_o[sl] = max(wt_o.get(sl, 0), t)
                xf = env.sb(stkB, f"p7_xf{s}", [128, NFC * TT], F32)
                xout = xf
                ssb2 = [env.sb(stkB, f"p7_ssb{i}_{s}", [128, T], F32) for i in range(2)]
                awb2 = [env.sb(stkB, f"p7_awb{i}_{s}", [128, T], BF16) for i in range(2)]
                rows2 = env.sb(stkB, f"p7_rows{s}", [128, 8], F32)
                pss = [env.ps(stkB, f"p7_pss{i}_{s}", [128, TT]) for i in range(2)]
                psT = [env.ps(stkB, f"p7_psT{i}_{s}", [128, 128], BF16) for i in range(2)]
                psc = [env.ps(stkB, f"p7_psc{i}_{s}", [128, TT]) for i in range(2)]
                pso = [env.ps(stkB, f"p7_pso{i}_{s}", [128, TT]) for i in range(2)]

                qtile = env.sb(stkB, f"p7_qtile{s}", [128, NFC * 128], BF16)
                attnT4 = env.sb(stkB, f"p7_at4{s}", [128, 16 * TT], BF16)
                tp_tail = []
                ctxb4 = env.sb(stkB, f"p7_ctx4{s}", [128, NFC * TT], BF16)
                sch.on("tensor", lambda e, t=lastA: e.wait_ge(ev.sem, t))
                prev_sc_done = None
                ctx_done = None
                for j in range(NTILE):
                    sf = "s0" if j % 2 == 0 else "s1"
                    srcx = xsrc[s][:, :, j * TT:(j + 1) * TT].rearrange("c p t -> p c t")
                    if len(st_hist) >= 1:
                        sl, t = st_hist[-1]
                        sch.on("sync", lambda e, sl=sl, t=t: e.wait_ge(env.sems[sl].sem, t))
                    ld_t = slot_dma(env, sch, "sync", sf, xf.rearrange("p (c t) -> p c t", c=NFC), srcx)
                    ev_tp_last = None
                    sm_done = {}
                    tp_done = {}
                    def emit_tp(qc):
                        ev_tp = list(tp_tail)
                        for st16 in range(len(tp_tail), len(tp_tail) + 16):
                            pt = psT[st16 % 2]
                            mmt = mm.bump()
                            def _tp(e, st16=st16, pt=pt, t=sm_done[qc], qc=qc,
                                    tprev=(ev_tp[st16 - 2] if st16 >= 2 else None), cd=ctx_done,
                                    base=len(tp_tail)):
                                if st16 == base:
                                    e.wait_ge(ev.sem, t)
                                    if cd is not None:
                                        e.wait_ge(ev.sem, cd)
                                if tprev is not None:
                                    e.wait_ge(ev.sem, tprev)
                                e.transpose(out=pt[:], in_=awb2[qc % 2][:, (st16 - base) * 128:(st16 - base + 1) * 128],
                                            identity=identb[:]).then_inc(mm.sem, 1)
                            sch.on("tensor", _tp)
                            et = ev.bump()
                            def _evt(e, mmt=mmt, st16=st16, pt=pt, qc=qc, base=len(tp_tail)):
                                e.wait_ge(mm.sem, mmt)
                                e.tensor_copy(out=attnT4[:, (st16 - base) * TT + qc * 128:
                                                           (st16 - base) * TT + (qc + 1) * 128],
                                              in_=pt[:]).then_inc(ev.sem, 1)
                            sch.on("vector", _evt)
                            ev_tp.append(et)
                        tp_done[qc] = mm.n
                        tp_tail[:] = ev_tp[-2:]
                        return ev_tp[-1]

                    for qc in range(4):
                        qq = j * 4 + qc
                        ssb = ssb2[qc % 2]
                        awb = awb2[qc % 2]
                        rows = rows2[:, 4 * (qc % 2): 4 * (qc % 2) + 4]
                        if prev_sc_done is not None:
                            sch.on("gpsimd", lambda e, t=prev_sc_done: e.wait_ge(mm.sem, t))
                        lq = slot_dma(env, sch, "gpsimd", "g2",
                                      qtile.rearrange("p (c t) -> p c t", c=NFC),
                                      q_s[:, :, qq * 128:(qq + 1) * 128].rearrange("c p t -> p c t"))
                        ac_sc = []
                        for st_ in range(4):
                            pb = pss[st_ % 2]
                            if st_ >= 2:
                                sch.on("tensor", lambda e, t=ac_sc[st_ - 2]: e.wait_ge(ac.sem, t))
                            mmt = mm.bump()
                            def _mmsc(e, st_=st_, pb=pb, qq=qq, lq=lq):
                                if st_ == 0:
                                    e.wait_ge(env.sems["g2"].sem, lq)
                                for kc in range(NFC):
                                    i = e.matmul(out=pb[:], lhsT=qtile[:, kc * 128:(kc + 1) * 128],
                                                 rhs=ksb[:, kc * T + st_ * TT: kc * T + (st_ + 1) * TT],
                                                 start=(kc == 0), stop=(kc == NFC - 1))
                                i.then_inc(mm.sem, 1)
                            sch.on("tensor", _mmsc)
                            at = ac.bump()
                            def _evsc(e, mmt=mmt, st_=st_, pb=pb, ssb=ssb, td=tp_done.get(qc - 2)):
                                e.wait_ge(mm.sem, mmt)
                                if st_ == 0 and td is not None:
                                    e.wait_ge(mm.sem, td)
                                e.activation(out=ssb[:, st_ * TT:(st_ + 1) * TT], in_=pb[:],
                                             func=AF.Copy).then_inc(ac.sem, 1)
                            sch.on("scalar", _evsc)
                            ac_sc.append(at)
                        prev_sc_done = mm.n
                        ev_mx0 = ev.bump()
                        ev_mx = ev.bump()
                        def _mx(e, t=ac_sc[-1], ev_mx0=ev_mx0, ssb=ssb, rows=rows):
                            e.wait_ge(ac.sem, t)
                            e.tensor_reduce(out=rows[:, 0:1], in_=ssb[:], axis=mybir.AxisListType.X,
                                            op=ALU.max).then_inc(ev.sem, 1)
                            e.wait_ge(ev.sem, ev_mx0)
                            e.tensor_scalar_mul(out=rows[:, 1:2], in0=rows[:, 0:1], scalar1=-1.0).then_inc(ev.sem, 1)
                        sch.on("vector", _mx)
                        ac_ex = ac.bump()
                        def _ex(e, t=ev_mx, ssb=ssb, awb=awb, rows=rows, pn=sm_done.get(qc - 1)):
                            e.wait_ge(ev.sem, t)
                            e.activation(out=awb[:], in_=ssb[:], func=AF.Exp,
                                         bias=rows[:, 1:2]).then_inc(ac.sem, 1)
                        sch.on("scalar", _ex)
                        ev_s1 = ev.bump()
                        ev_s2 = ev.bump()
                        ev_at = ev.bump()
                        def _at(e, t=ac_ex, ev_s1=ev_s1, ev_s2=ev_s2, awb=awb, rows=rows):
                            e.wait_ge(ac.sem, t)
                            e.tensor_reduce(out=rows[:, 2:3], in_=awb[:], axis=mybir.AxisListType.X,
                                            op=ALU.add).then_inc(ev.sem, 1)
                            e.wait_ge(ev.sem, ev_s1)
                            e.reciprocal(out=rows[:, 3:4], in_=rows[:, 2:3]).then_inc(ev.sem, 1)
                            e.wait_ge(ev.sem, ev_s2)
                            e.tensor_scalar_mul(out=awb[:], in0=awb[:], scalar1=rows[:, 3:4]).then_inc(ev.sem, 1)
                        sch.on("vector", _at)
                        sm_done[qc] = ev_at
                        if qc >= 1:
                            ev_tp_last = emit_tp(qc - 1)
                    ev_tp_last = emit_tp(3)
                    ev_ctx = []
                    for m in range(NFC):
                        pb = psc[m % 2]
                        if m >= 2:
                            sch.on("tensor", lambda e, t=ev_ctx[m - 2]: e.wait_ge(ev.sem, t))
                        else:
                            sch.on("tensor", lambda e, t=ev_tp_last: e.wait_ge(ev.sem, t))
                        mmt = mm.bump()
                        def _mmc(e, m=m, pb=pb):
                            for sc in range(16):
                                i = e.matmul(out=pb[:], lhsT=vtok[:, sc * D + m * 128: sc * D + (m + 1) * 128],
                                             rhs=attnT4[:, sc * TT:(sc + 1) * TT],
                                             start=(sc == 0), stop=(sc == 15))
                            i.then_inc(mm.sem, 1)
                        sch.on("tensor", _mmc)
                        et = ev.bump()
                        def _evc(e, mmt=mmt, m=m, pb=pb):
                            e.wait_ge(mm.sem, mmt)
                            e.tensor_copy(out=ctxb4[:, m * TT:(m + 1) * TT], in_=pb[:]).then_inc(ev.sem, 1)
                        sch.on("vector", _evc)
                        ev_ctx.append(et)
                    ctx_done = ev_ctx[-1]
                    ev_o = []
                    for m in range(NFC):
                        pb = pso[m % 2]
                        if m >= 2:
                            sch.on("tensor", lambda e, t=ev_o[m - 2]: e.wait_ge(ev.sem, t))
                        else:
                            sch.on("tensor", lambda e, t=ev_ctx[-1]: e.wait_ge(ev.sem, t))
                            if m == 0:
                                def _wo(e):
                                    for sl, tt in wt_o.items():
                                        e.wait_ge(env.sems[sl].sem, tt)
                                sch.on("tensor", _wo)
                        mmt = mm.bump()
                        def _mmo(e, m=m, pb=pb):
                            for kc in range(NFC):
                                i = e.matmul(out=pb[:], lhsT=w7out[:, kc * D + m * 128: kc * D + (m + 1) * 128],
                                             rhs=ctxb4[:, kc * TT:(kc + 1) * TT],
                                             start=(kc == 0), stop=(kc == NFC - 1))
                            i.then_inc(mm.sem, 1)
                        sch.on("tensor", _mmo)
                        if m == 0 and len(st_hist) >= 1:
                            sl, t = st_hist[-1]
                            sch.on("vector", lambda e, sl=sl, t=t: e.wait_ge(env.sems[sl].sem, t))
                        et = ev.bump()
                        def _evo(e, mmt=mmt, m=m, pb=pb, ld_t=ld_t, sf=sf):
                            e.wait_ge(mm.sem, mmt)
                            if m == 0:
                                e.wait_ge(env.sems[sf].sem, ld_t)
                                e.wait_ge(ldh.sem, bias_ld)
                            cc = slice(m * TT, (m + 1) * TT)
                            e.scalar_tensor_tensor(out=xout[:, cc], in0=pb[:], scalar=b7out[:, m:m + 1],
                                                   in1=xf[:, cc], op0=ALU.add, op1=ALU.add).then_inc(ev.sem, 1)
                        sch.on("vector", _evo)
                        ev_o.append(et)
                    ev_o_last = ev_o[-1]
                    ssl = "st0" if j % 2 == 0 else "st1"
                    sch.on("gpsimd", lambda e, t=ev_o_last: e.wait_ge(ev.sem, t))
                    dst = xdst[s][:, :, j * TT:(j + 1) * TT].rearrange("c p t -> p c t")
                    st_t = slot_dma(env, sch, "gpsimd", ssl, dst,
                                    xout.rearrange("p (c t) -> p c t", c=NFC))
                    st_hist.append((ssl, st_t))
                return ev_o_last

        import os as _os
        if _os.environ.get("K_P7_DEBUG"):
            lastA = stage_a(0)
            dq = nc.dram_tensor("dbg_q", [128, NFC * T], BF16, kind="ExternalOutput").ap()
            dk = nc.dram_tensor("dbg_k", [128, NFC * T], BF16, kind="ExternalOutput").ap()
            dv = nc.dram_tensor("dbg_v", [128, 16 * D], BF16, kind="ExternalOutput").ap()
            stx = env.sems["st0"]
            base = stx.n
            def _dump(e, t=lastA, base=base):
                e.wait_ge(ev.sem, t)
                e.wait_ge(stx.sem, base)
                e.dma_start(out=dq, in_=qsb[:]).then_inc(stx.sem, 16)
                e.wait_ge(stx.sem, base + 16)
                e.dma_start(out=dk, in_=ksb[:]).then_inc(stx.sem, 16)
                e.wait_ge(stx.sem, base + 32)
                e.dma_start(out=dv, in_=vtok[:]).then_inc(stx.sem, 16)
                e.wait_ge(stx.sem, base + 48)
            sch.on("gpsimd", _dump)
            stx.n += 48
            emit_barrier(env, sch)
            sch.flush(nc)
            return
        for s in range(BL):
            lastA = stage_a(s)
            fence_ev[0] = stage_b(s, lastA)

        def _drain(e):
            e.wait_ge(env.sems["st0"].sem, env.sems["st0"].n)
            e.wait_ge(env.sems["st1"].sem, env.sems["st1"].n)
        sch.on("gpsimd", _drain)
        emit_barrier(env, sch)
        sch.flush(nc)





def emit_p_l0attnb(env, sch, xsrc, xdst):
    nc = env.nc
    ldh, mm, ev, ac = (env.sems[k] for k in ("ldh", "mm", "ev", "ac"))
    ones_bf = env.consts["ones_col_bf"]
    ones_row = env.consts["ones_row_f"]
    iota_col = env.consts["iota_col"]
    iota_row = env.consts["iota_row"]
    mask01 = env.consts["mask01"]
    rt_bf = env.consts["rt_bf"]
    iota2d_c = lambda: env.consts["iota2d"]
    LN2_35 = float(np.float32(-3.5 * math.log(2.0)))

    qs_s = nc.dram_tensor("p3_qs", [8, 128, T], BF16, kind="Internal").ap()
    ks_s = nc.dram_tensor("p3_ks", [8, 128, T], BF16, kind="Internal").ap()
    v_s = nc.dram_tensor("p3_vs", [16, 128, T], BF16, kind="Internal").ap()
    gate_s = nc.dram_tensor("p3_gs", [16, 128, T], BF16, kind="Internal").ap()
    gn_s = nc.dram_tensor("p3_gns", [16, 128, T], BF16, kind="Internal").ap()

    with ExitStack() as stk:
        ldrow = env.sb(stk, "p3_ldrow", [1, 8], F32)
        ldb = env.sb(stk, "p3_ldb", [128, 8], F32)
        decr = env.sb(stk, "p3_dec", [1, 8], F32)
        fence = [None]

        dec_ld = ldh.bump(16)
        sch.on("sync", lambda e: e.dma_start(out=decr[0:1, :], in_=env.dram["p_decay"]).then_inc(ldh.sem, 16))
        ac_ld = ac.bump()
        def _ln(e):
            e.wait_ge(ldh.sem, dec_ld)
            e.activation(out=ldrow[0:1, :], in_=decr[0:1, :], func=AF.Ln).then_inc(ac.sem, 1)
        sch.on("scalar", _ln)

        def scope_fence(engines=("sync", "gpsimd", "tensor")):
            if fence[0] is not None:
                for eng in engines:
                    sch.on(eng, lambda e, t=fence[0]: e.wait_ge(ev.sem, t))
            st0n, st1n = env.sems["st0"].n, env.sems["st1"].n
            for eng in ("sync", "gpsimd", "tensor", "vector"):
                sch.on(eng, lambda e, a=st0n, b=st1n: (e.wait_ge(env.sems["st0"].sem, a),
                                                       e.wait_ge(env.sems["st1"].sem, b)))

        def stage_a1(s):
            scope_fence()
            with ExitStack() as sa:
                xbf = [env.sb(sa, f"p3a_xbf{i}_{s}", [128, NFC * TT], BF16) for i in range(2)]
                ln2 = [LNBufs(env, sa, f"p3a_lnA{s}")]
                ln2.append(LNBufs(env, sa, f"p3a_lnB{s}", share_ps=ln2[0]))
                wqk = env.sb(sa, f"p3a_wqk{s}", [128, NFC * 2048], BF16)
                ro1 = [env.sb(sa, f"p3a_ro1{i}_{s}", [128, TT], F32) for i in range(2)]
                qacc = env.sb(sa, f"p3a_qacc{s}", [128, NFC * TT], BF16)
                kacc = env.sb(sa, f"p3a_kacc{s}", [128, NFC * TT], BF16)
                tabs = [env.sb(sa, f"p3a_tab{i}_{s}", [128, NFC * TT], F32) for i in range(2)]
                psm = [env.ps(sa, f"p3a_ps{i}_{s}", [128, TT]) for i in range(2)]
                psr = env.ps(sa, f"p3a_psr{s}", [128, TT])
                psr2 = env.ps(sa, f"p3a_psr2{s}", [128, TT])
                rot_hist = []
                mmc, ev2c, mm2c, ac2c = (env.sems[k] for k in ("mm", "ev2", "mm2", "ac2"))

                wt = {}
                for kc in range(NFC):
                    sl = ("g2", "g3")[kc % 2]
                    t = slot_dma(env, sch, "gpsimd", sl, wqk[:, kc * 2048:(kc + 1) * 2048],
                                 env.dram["w3_qkv"][kc * 128:(kc + 1) * 128, 0:2048])
                    wt[sl] = t
                evln, lninfo, lastmm, evq_hist = {}, {}, {}, {}

                def do_ln(g, cap):
                    bi = g % 2
                    sb_ = "g0" if bi == 0 else "g1"
                    srcx = xsrc[s][:, :, g * TT:(g + 1) * TT].rearrange("c p t -> p c t")
                    if g >= 2:
                        sch.on("gpsimd", lambda e, t=evln[g - 2]: e.wait_ge(ev2c.sem, t))
                    ldg_t = slot_dma(env, sch, "gpsimd", sb_,
                                     xbf[bi].rearrange("p (c t) -> p c t", c=NFC), srcx)
                    kw = {"cap": cap, "sems": ("mm2", "ev2", "ac2"), "norm_src": "bf"}
                    if g >= 1:
                        kw["sq_guard"] = (mm2c, lninfo[g - 1]["mm_st"])
                        kw["rows_guard"] = [(mm2c, lninfo[g - 1]["mm_bc"]), (ac2c, lninfo[g - 1]["ac_rs"])]
                        kw["bc_guard"] = (ev2c, evln[g - 1])
                    if g >= 2:
                        kw["xln_guard"] = (mmc, lastmm[g - 2])
                    evln[g], lninfo[g] = emit_ln(env, sch, ln2[bi], g, None, xbf[bi],
                                                 (None, None), (sb_, ldg_t), **kw)

                do_ln(0, None)
                for j in range(NTILE):
                    bi = j % 2
                    ln = ln2[bi]
                    cap = {} if j + 1 < NTILE else None
                    if cap is not None:
                        do_ln(j + 1, cap)
                    def play(sec, cap=cap):
                        if cap is not None:
                            for e_, f_ in cap.get(sec, []):
                                sch.on(e_, f_)
                    play("sq")
                    def _w0(e, t=evln[j], wt=dict(wt)):
                        e.wait_ge(ev2c.sem, t)
                        for sl, tt in wt.items():
                            e.wait_ge(env.sems[sl].sem, tt)
                    sch.on("tensor", _w0)
                    evq = []
                    for idx in range(16):
                        if idx == 3:
                            play("stats")
                        if idx == 6:
                            play("bc")
                        qk, h = divmod(idx, 8)
                        col0 = 0 if qk == 0 else 1024
                        acc = qacc if qk == 0 else kacc
                        pb = psm[idx % 2]
                        if idx >= 2:
                            sch.on("tensor", lambda e, t=evq[idx - 2]: e.wait_ge(ev.sem, t))
                        elif (j - 1) in evq_hist:
                            sch.on("tensor", lambda e, t=evq_hist[j - 1][14 + idx]: e.wait_ge(ev.sem, t))
                        mm_q = mm.bump()
                        def _mmq(e, h=h, col0=col0, pb=pb, ln=ln):
                            for kc in range(NFC):
                                i = e.matmul(out=pb[:],
                                             lhsT=wqk[:, kc * 2048 + col0 + h * 128: kc * 2048 + col0 + (h + 1) * 128],
                                             rhs=ln.xln[:, kc * TT:(kc + 1) * TT],
                                             start=(kc == 0), stop=(kc == NFC - 1))
                            i.then_inc(mm.sem, 1)
                        sch.on("tensor", _mmq)
                        if idx == 4:
                            play("rows")
                            play("sd")
                        et = ev.bump()
                        def _evq(e, mm_q=mm_q, h=h, pb=pb, acc=acc):
                            e.wait_ge(mm.sem, mm_q)
                            e.tensor_copy(out=acc[:, h * TT:(h + 1) * TT], in_=pb[:]).then_inc(ev.sem, 1)
                        sch.on("vector", _evq)
                        evq.append(et)
                    evq_hist[j] = evq
                    lastmm[j] = mm.n

                    st_waits = []
                    for qk in range(2):
                        acc = qacc if qk == 0 else kacc
                        tq = ("rot_cq", "rot_sq") if qk == 0 else ("rot_ck", "rot_sk")
                        if rot_hist:
                            sch.on("gpsimd", lambda e, t=rot_hist[-1]: e.wait_ge(ev.sem, t))
                        tw = []
                        for z, nmr in enumerate(tq):
                            sl = ("g2", "g3")[z]
                            tgt = slot_dma(env, sch, "gpsimd", sl,
                                           tabs[z].rearrange("p (h t) -> p h t", h=NFC),
                                           env.dram[nmr][:, :, j * TT:(j + 1) * TT].rearrange("h p t -> p h t"))
                            tw.append((sl, tgt))
                        ev_rot_last = None
                        for h in range(NFC):
                            pr = psr if h % 2 == 0 else psr2
                            rb = ro1[h % 2]
                            mm_r = mm.bump()
                            def _mmr(e, h=h, acc=acc, pr=pr,
                                     guard=(evq[qk * 8 + 7] if h == 0 else None),
                                     pg2=(rot_hist[-2] if len(rot_hist) >= 2 else None)):
                                if guard is not None:
                                    e.wait_ge(ev.sem, guard)
                                if pg2 is not None:
                                    e.wait_ge(ev.sem, pg2)
                                e.matmul(out=pr[:], lhsT=rt_bf[:], rhs=acc[:, h * TT:(h + 1) * TT],
                                         start=True, stop=True).then_inc(mm.sem, 1)
                            sch.on("tensor", _mmr)
                            et = ev.bump()
                            def _rot(e, mm_r=mm_r, h=h, acc=acc, rb=rb, pr=pr, tw=tuple(tw)):
                                e.wait_ge(mm.sem, mm_r)
                                if h == 0:
                                    for sl, tt in tw:
                                        e.wait_ge(env.sems[sl].sem, tt)
                                e.tensor_mul(out=rb[:], in0=pr[:], in1=tabs[1][:, h * TT:(h + 1) * TT])
                                e.tensor_mul(out=acc[:, h * TT:(h + 1) * TT], in0=acc[:, h * TT:(h + 1) * TT],
                                             in1=tabs[0][:, h * TT:(h + 1) * TT])
                                e.tensor_add(out=acc[:, h * TT:(h + 1) * TT], in0=acc[:, h * TT:(h + 1) * TT],
                                             in1=rb[:]).then_inc(ev.sem, 1)
                            sch.on("vector", _rot)
                            ev_rot_last = et
                            rot_hist.append(et)
                        sch.on("gpsimd", lambda e, t=ev_rot_last: e.wait_ge(ev.sem, t))
                        dstd = (qs_s if qk == 0 else ks_s)[:, :, j * TT:(j + 1) * TT].rearrange("h p t -> p h t")
                        slot_dma(env, sch, "gpsimd", "st0" if qk == 0 else "st1",
                                 dstd, acc.rearrange("p (h t) -> p h t", h=NFC))
                        st_waits.append(("st0" if qk == 0 else "st1",
                                         env.sems["st0" if qk == 0 else "st1"].n))
                        if qk == 0:
                            play("norm")
                    for sl, a_ in st_waits:
                        sch.on("vector", lambda e, sl=sl, a=a_: e.wait_ge(env.sems[sl].sem, a))
                fence[0] = rot_hist[-1]

        def stage_a2(s):
            scope_fence()
            with ExitStack() as sa:
                xbf = env.sb(sa, f"p3b_xbf{s}", [128, NFC * TT], BF16)
                ln = LNBufs(env, sa, f"p3b_ln{s}")
                wv = env.sb(sa, f"p3b_wv{s}", [128, NFC * 2048], BF16)
                wg = env.sb(sa, f"p3b_wg{s}", [128, NFC * 2048], BF16)
                vacc = [env.sb(sa, f"p3b_vacc{i}_{s}", [128, 4 * 2048], BF16) for i in range(2)]
                gacc = [env.sb(sa, f"p3b_gacc{i}_{s}", [128, 16 * TT], BF16) for i in range(2)]
                psm = [env.ps(sa, f"p3b_ps{i}_{s}", [128, TT]) for i in range(2)]

                wt = {}
                for kc in range(NFC):
                    sl = ("g2", "g3")[kc % 2]
                    wt[sl] = slot_dma(env, sch, "gpsimd", sl, wv[:, kc * 2048:(kc + 1) * 2048],
                                      env.dram["w3_qkv"][kc * 128:(kc + 1) * 128, 2048:4096])
                for kc in range(NFC):
                    sl = ("g2", "g3")[kc % 2]
                    wt[sl] = slot_dma(env, sch, "gpsimd", sl, wg[:, kc * 2048:(kc + 1) * 2048],
                                      env.dram["w3_g"][kc * 128:(kc + 1) * 128, :])
                prev_ln = None
                prev_grp = None
                st_hist_a2 = {}
                for j in range(NTILE):
                    sb_ = "g0" if j % 2 == 0 else "g1"
                    srcx = xsrc[s][:, :, j * TT:(j + 1) * TT].rearrange("c p t -> p c t")
                    if prev_ln is not None:
                        sch.on("gpsimd", lambda e, t=prev_ln: e.wait_ge(ev.sem, t))
                    ldg_t = slot_dma(env, sch, "gpsimd", sb_, xbf.rearrange("p (c t) -> p c t", c=NFC), srcx)
                    ev_ln, _lninfo = emit_ln(env, sch, ln, j, None, xbf, (None, None), (sb_, ldg_t), norm_src="bf")
                    prev_ln = ev_ln
                    def _w0(e, t=ev_ln, wt=dict(wt)):
                        e.wait_ge(ev.sem, t)
                        for sl, tt in wt.items():
                            e.wait_ge(env.sems[sl].sem, tt)
                    sch.on("tensor", _w0)
                    grp = []
                    for sc in range(4):
                        for dvc in range(4):
                            pb = psm[(sc * 4 + dvc) % 2]
                            if len(grp) >= 2:
                                cse, ct = grp[-2]
                                sch.on("tensor", lambda e, cse=cse, ct=ct: e.wait_ge(cse.sem, ct))
                            elif prev_grp is not None:
                                cse, ct = prev_grp[-2 + len(grp)]
                                sch.on("tensor", lambda e, cse=cse, ct=ct: e.wait_ge(cse.sem, ct))
                            mmt = mm.bump()
                            def _mmv(e, sc=sc, dvc=dvc, pb=pb):
                                for kc in range(NFC):
                                    i = e.matmul(out=pb[:],
                                                 lhsT=ln.xln[:, kc * TT + sc * 128: kc * TT + (sc + 1) * 128],
                                                 rhs=wv[:, kc * 2048 + dvc * 512: kc * 2048 + (dvc + 1) * 512],
                                                 start=(kc == 0), stop=(kc == NFC - 1))
                                i.then_inc(mm.sem, 1)
                            sch.on("tensor", _mmv)
                            et = ev.bump()
                            def _evv(e, mmt=mmt, sc=sc, dvc=dvc, pb=pb, bi=j % 2,
                                     stg=(st_hist_a2.get(j - 2))):
                                e.wait_ge(mm.sem, mmt)
                                if sc == 0 and dvc == 0 and stg is not None:
                                    e.wait_ge(env.sems[stg[0][0]].sem, stg[0][1])
                                e.tensor_copy(out=vacc[bi][:, sc * 2048 + dvc * 512: sc * 2048 + (dvc + 1) * 512],
                                              in_=pb[:]).then_inc(ev.sem, 1)
                            sch.on("vector", _evv)
                            grp.append((ev, et))
                    for m in range(16):
                        pb = psm[m % 2]
                        cse, ct = grp[-2]
                        sch.on("tensor", lambda e, cse=cse, ct=ct: e.wait_ge(cse.sem, ct))
                        mmt = mm.bump()
                        def _mmg(e, m=m, pb=pb):
                            for kc in range(NFC):
                                i = e.matmul(out=pb[:],
                                             lhsT=wg[:, kc * 2048 + m * 128: kc * 2048 + (m + 1) * 128],
                                             rhs=ln.xln[:, kc * TT:(kc + 1) * TT],
                                             start=(kc == 0), stop=(kc == NFC - 1))
                            i.then_inc(mm.sem, 1)
                        sch.on("tensor", _mmg)
                        at = ac.bump()
                        def _evg(e, mmt=mmt, m=m, pb=pb, bi=j % 2, stg=(st_hist_a2.get(j - 2))):
                            e.wait_ge(mm.sem, mmt)
                            if m == 0 and stg is not None:
                                e.wait_ge(env.sems[stg[1][0]].sem, stg[1][1])
                            e.activation(out=gacc[bi][:, m * TT:(m + 1) * TT], in_=pb[:],
                                         func=AF.Silu).then_inc(ac.sem, 1)
                        sch.on("scalar", _evg)
                        grp.append((ac, at))
                    bi = j % 2
                    sch.on("gpsimd", lambda e, t=grp[4 * 4 - 1][1]: e.wait_ge(ev.sem, t))
                    t0s = slot_dma(env, sch, "gpsimd", "st0",
                                   v_s[j * 4:(j + 1) * 4].rearrange("s p v -> p s v"),
                                   vacc[bi].rearrange("p (s v) -> p s v", s=4))
                    sch.on("gpsimd", lambda e, t=grp[-1][1]: e.wait_ge(ac.sem, t))
                    t1s = slot_dma(env, sch, "gpsimd", "st1",
                                   gate_s[:, :, j * TT:(j + 1) * TT].rearrange("c p t -> p c t"),
                                   gacc[bi].rearrange("p (c t) -> p c t", c=16))
                    st_hist_a2[j] = (("st0", t0s), ("st1", t1s))
                    prev_grp = list(grp)
                fence[0] = grp[-1][1] if grp else None
                last_ac = grp[-1][1]
                evm = ev.bump()
                def _mk(e, t=last_ac):
                    e.wait_ge(ac.sem, t)
                    e.tensor_copy(out=vacc[0][:, 0:16], in_=vacc[0][:, 0:16]).then_inc(ev.sem, 1)
                sch.on("vector", _mk)
                fence[0] = evm

        def stage_b(s):
            scope_fence()
            with ExitStack() as sb_:
                qh = [env.sb(sb_, f"p3c_qh{i}_{s}", [128, T], BF16) for i in range(2)]
                kh = [env.sb(sb_, f"p3c_kh{i}_{s}", [128, T], BF16) for i in range(2)]
                vh = [env.sb(sb_, f"p3c_vh{i}_{s}", [128, 16 * 256], BF16) for i in range(2)]
                gh = [env.sb(sb_, f"p3c_gh{i}_{s}", [128, 2 * T], BF16) for i in range(2)]
                qp = [env.sb(sb_, f"p3c_qp{i}_{s}", [128, T], BF16) for i in range(2)]
                scT = [env.sb(sb_, f"p3c_scT{i}_{s}", [128, TT], BF16) for i in range(2)]
                onb = env.sb(sb_, f"p3c_onb{s}", [128, 2 * T], BF16)
                sq2 = env.sb(sb_, f"p3c_sq2{s}", [128, 2 * T], BF16)
                gnacc = [env.sb(sb_, f"p3c_gn{i}_{s}", [128, 2 * T], BF16) for i in range(2)]
                rows6 = env.sb(sb_, f"p3c_rows6{s}", [1, 6 * T], F32)
                ccol = env.sb(sb_, f"p3c_ccol{s}", [128, 2], F32)
                ccols = env.sb(sb_, f"p3c_ccols{s}", [128, 32], F32)
                prow = env.sb(sb_, f"p3c_prow{s}", [1, T], F32)
                psc = [env.ps(sb_, f"p3c_psc{i}_{s}", [128, TT]) for i in range(2)]
                pso = [env.ps(sb_, f"p3c_pso{i}_{s}", [128, TT]) for i in range(2)]
                psr = [env.ps(sb_, f"p3c_psr{i}_{s}", [128, TT]) for i in range(2)]
                psw = [env.ps(sb_, f"p3c_psw{i}_{s}", [128, TT]) for i in range(2)]

                if s == 0:
                    mm_ldb = mm.bump()
                    def _bl(e):
                        e.wait_ge(ac.sem, ac_ld)
                        e.matmul(out=psr[0][:, 0:8], lhsT=ones_row[0:1, :], rhs=ldrow[0:1, :],
                                 start=True, stop=True).then_inc(mm.sem, 1)
                    sch.on("tensor", _bl)
                    ev_ldb = ev.bump()
                    def _evl(e, mm_ldb=mm_ldb):
                        e.wait_ge(mm.sem, mm_ldb)
                        e.tensor_copy(out=ldb[:], in_=psr[0][:, 0:8]).then_inc(ev.sem, 1)
                    sch.on("vector", _evl)

                h_done = {}
                sc_done = {}
                st_gn = {}
                prev_evon = None
                for h in range(NFC):
                    hb = h % 2
                    if h >= 2:
                        sch.on("gpsimd", lambda e, t=h_done[h - 2]: e.wait_ge(ev.sem, t))
                    lq = slot_dma(env, sch, "gpsimd", "g0", qh[hb][:], qs_s[h])
                    lk = slot_dma(env, sch, "gpsimd", "g1", kh[hb][:], ks_s[h])
                    lv = slot_dma(env, sch, "gpsimd", "g2", vh[hb].rearrange("p (s v) -> p s v", s=16),
                                  v_s[:, :, h * 256:(h + 1) * 256].rearrange("s p v -> p s v"))
                    lg = slot_dma(env, sch, "gpsimd", "g3", gh[hb].rearrange("p (c t) -> p c t", c=2),
                                  gate_s[2 * h:2 * h + 2].rearrange("c p t -> p c t"))
                    ev_cc = ev.bump()
                    def _cc(e, h=h, prev=h_done.get(h - 2)):
                        if prev is not None:
                            e.wait_ge(ev.sem, prev)
                        e.wait_ge(ac.sem, ac_ld)
                        e.tensor_scalar_mul(out=ccols[:, 0:16], in0=iota2d_c()[:], scalar1=ldb[:, h:h + 1])
                        e.tensor_scalar(out=prow[0:1, :], in0=iota_row[0:1, :], scalar1=ldrow[0:1, h:h + 1],
                                        scalar2=None, op0=ALU.mult).then_inc(ev.sem, 1)
                    sch.on("vector", _cc)
                    ac_cc = ac.bump(2)
                    def _ce(e, ev_cc=ev_cc):
                        e.wait_ge(ev.sem, ev_cc)
                        e.activation(out=ccols[:, 16:32], in_=ccols[:, 0:16], func=AF.Exp, scale=-1.0).then_inc(ac.sem, 1)
                        e.activation(out=prow[0:1, :], in_=prow[0:1, :], func=AF.Exp,
                                     bias=env.fconst(LN2_35)[0:1, :]).then_inc(ac.sem, 1)
                    sch.on("scalar", _ce)
                    ev_qp = None
                    qp_hist = []
                    for tj in range(4):
                        bb = psr[tj % 2]
                        mm_b = mm.bump()
                        def _bb(e, tj=tj, bb=bb, ac_cc=ac_cc,
                                prevq=(sc_done.get(h - 1) if tj < 2 else None),
                                pq=(qp_hist[tj - 2] if tj >= 2 else None)):
                            if tj == 0:
                                e.wait_ge(ac.sem, ac_cc)
                            if prevq is not None:
                                e.wait_ge(mm.sem, prevq)
                            if pq is not None:
                                e.wait_ge(ev.sem, pq)
                            e.matmul(out=bb[:], lhsT=ones_row[0:1, :], rhs=prow[0:1, tj * TT:(tj + 1) * TT],
                                     start=True, stop=True).then_inc(mm.sem, 1)
                        sch.on("tensor", _bb)
                        et = ev.bump()
                        def _qp(e, mm_b=mm_b, tj=tj, bb=bb, lq=lq, hb=hb):
                            e.wait_ge(mm.sem, mm_b)
                            if tj == 0:
                                e.wait_ge(env.sems["g0"].sem, lq)
                            e.tensor_mul(out=qp[hb][:, tj * TT:(tj + 1) * TT], in0=qh[hb][:, tj * TT:(tj + 1) * TT],
                                         in1=bb[:]).then_inc(ev.sem, 1)
                        sch.on("vector", _qp)
                        qp_hist.append(et)
                        ev_qp = et
                    for j in range(NTILE):
                        n_i = 4 * (j + 1)
                        ev_sc = {}
                        geom = {}
                        out_emit = []
                        for i in range(n_i):
                            start_t = max(j * TT, i * 128)
                            off = start_t - j * TT
                            N = (j + 1) * TT - start_t
                            geom[i] = (start_t, off, N)
                            pb = psc[i % 2]
                            sct = scT[i % 2]
                            mmt = mm.bump()
                            def _mms(e, i=i, j=j, pb=pb, start_t=start_t, N=N, lk=lk, ev_qp=ev_qp, hb=hb,
                                     prev=(ev_sc.get(i - 2))):
                                if i == 0 and j == 0:
                                    e.wait_ge(env.sems["g1"].sem, lk)
                                    e.wait_ge(ev.sem, ev_qp)
                                if prev is not None:
                                    e.wait_ge(prev[0].sem, prev[1])
                                e.matmul(out=pb[:, 0:N], lhsT=kh[hb][:, i * 128:(i + 1) * 128],
                                         rhs=qp[hb][:, start_t:(j + 1) * TT],
                                         start=True, stop=True).then_inc(mm.sem, 1)
                            sch.on("tensor", _mms)
                            at = ac.bump()
                            def _evs2(e, mmt=mmt, i=i, pb=pb, sct=sct, off=off, N=N):
                                e.wait_ge(mm.sem, mmt)
                                e.activation(out=sct[:, off:off + N], in_=pb[:, 0:N], func=AF.Copy,
                                             scale=ccols[:, 16 + i:17 + i]).then_inc(ac.sem, 1)
                            sch.on("scalar", _evs2)
                            if i >= 4 * j:
                                et = ev.bump()
                                def _msk(e, at=at, sct=sct, off=off):
                                    e.wait_ge(ac.sem, at)
                                    e.tensor_mul(out=sct[:, off:off + 128], in0=sct[:, off:off + 128],
                                                 in1=mask01[:]).then_inc(ev.sem, 1)
                                sch.on("vector", _msk)
                                ev_sc[i] = (ev, et)
                            else:
                                ev_sc[i] = (ac, at)

                            def emit_out(io):
                                st_o, off_o, N_o = geom[io]
                                sct_o = scT[io % 2]
                                for dc in range(2):
                                    mm.bump()
                                    def _mmo(e, io=io, j=j, dc=dc, sct_o=sct_o, off_o=off_o, N_o=N_o,
                                             dep=ev_sc[io], lv=lv, n_i=n_i, ogd=prev_evon, hb=hb):
                                        if io == 0 and dc == 0:
                                            e.wait_ge(env.sems["g2"].sem, lv)
                                            if ogd is not None:
                                                e.wait_ge(ev.sem, ogd)
                                        e.wait_ge(dep[0].sem, dep[1])
                                        e.matmul(out=pso[dc][:, off_o:off_o + N_o],
                                                 lhsT=vh[hb][:, io * 256 + dc * 128: io * 256 + (dc + 1) * 128],
                                                 rhs=sct_o[:, off_o:off_o + N_o],
                                                 start=(io == 0), stop=(io == n_i - 1)).then_inc(mm.sem, 1)
                                    sch.on("tensor", _mmo)

                            if i >= 1:
                                emit_out(i - 1)
                            if i == n_i - 1:
                                emit_out(i)
                        mm_out = mm.n
                        ev_on = ev.bump()
                        def _evon(e, mm_out=mm_out, j=j):
                            e.wait_ge(mm.sem, mm_out)
                            e.tensor_copy(out=onb[:, 0 * T + j * TT: 0 * T + (j + 1) * TT], in_=pso[0][:])
                            e.tensor_copy(out=onb[:, 1 * T + j * TT: 1 * T + (j + 1) * TT],
                                          in_=pso[1][:]).then_inc(ev.sem, 1)
                        sch.on("vector", _evon)
                        prev_evon = ev_on
                    sc_done[h] = mm.n
                    ac_sq = ac.bump()
                    def _sq(e, ev_on=prev_evon):
                        e.wait_ge(ev.sem, ev_on)
                        e.activation(out=sq2[:], in_=onb[:], func=AF.Square).then_inc(ac.sem, 1)
                    sch.on("scalar", _sq)
                    R6 = rows6
                    mu6, m26, var6, sd6, nm6, rs6 = (R6[0:1, z * T:(z + 1) * T] for z in range(6))
                    ev_st = {}
                    for jj in range(4):
                        mm_st = mm.bump(2)
                        def _stm(e, jj=jj, ac_sq=ac_sq, prevw=(ev_st.get(jj - 1))):
                            if jj == 0:
                                e.wait_ge(ac.sem, ac_sq)
                            if prevw is not None:
                                e.wait_ge(ev.sem, prevw)
                            for dc in range(2):
                                i = e.matmul(out=psw[0][0:1, :], lhsT=ones_bf[:, 0:1],
                                             rhs=onb[:, dc * T + jj * TT: dc * T + (jj + 1) * TT],
                                             start=(dc == 0), stop=(dc == 1))
                            i.then_inc(mm.sem, 1)
                            for dc in range(2):
                                i = e.matmul(out=psw[1][0:1, :], lhsT=ones_bf[:, 0:1],
                                             rhs=sq2[:, dc * T + jj * TT: dc * T + (jj + 1) * TT],
                                             start=(dc == 0), stop=(dc == 1))
                            i.then_inc(mm.sem, 1)
                        sch.on("tensor", _stm)
                        et = ev.bump()
                        def _cpw(e, jj=jj, mm_st=mm_st):
                            e.wait_ge(mm.sem, mm_st)
                            e.tensor_copy(out=mu6[:, jj * TT:(jj + 1) * TT], in_=psw[0][0:1, :])
                            e.tensor_copy(out=m26[:, jj * TT:(jj + 1) * TT], in_=psw[1][0:1, :]).then_inc(ev.sem, 1)
                        sch.on("vector", _cpw)
                        ev_st[jj] = et
                    ev_rows = ev.bump()
                    def _rw(e, t=ev_st[3]):
                        e.wait_ge(ev.sem, t)
                        e.tensor_scalar_mul(out=mu6, in0=mu6, scalar1=1.0 / 256.0)
                        e.tensor_scalar_mul(out=m26, in0=m26, scalar1=1.0 / 256.0)
                        e.tensor_mul(out=var6, in0=mu6, in1=mu6)
                        e.tensor_sub(out=var6, in0=m26, in1=var6).then_inc(ev.sem, 1)
                    sch.on("vector", _rw)
                    ac_sd = ac.bump(2)
                    def _sd(e, ev_rows=ev_rows):
                        e.wait_ge(ev.sem, ev_rows)
                        e.activation(out=sd6, in_=var6, func=AF.Ln,
                                     bias=env.fconst(1e-5)[0:1, :]).then_inc(ac.sem, 1)
                        e.activation(out=rs6, in_=sd6, func=AF.Exp, scale=-0.5).then_inc(ac.sem, 1)
                    sch.on("scalar", _sd)
                    ev_rs = ev.bump()
                    def _rs(e, ac_sd=ac_sd):
                        e.wait_ge(ac.sem, ac_sd)
                        e.tensor_mul(out=nm6, in0=mu6, in1=rs6)
                        e.tensor_scalar_mul(out=nm6, in0=nm6, scalar1=-1.0).then_inc(ev.sem, 1)
                    sch.on("vector", _rs)
                    gn_last = None
                    for jj in range(4):
                        mm_bc = mm.bump(2)
                        def _bc(e, jj=jj, ev_rs=ev_rs, gprev=gn_last):
                            if jj == 0:
                                e.wait_ge(ev.sem, ev_rs)
                            else:
                                e.wait_ge(ev.sem, gprev)
                            e.matmul(out=psw[0][:], lhsT=ones_row[0:1, :], rhs=nm6[:, jj * TT:(jj + 1) * TT],
                                     start=True, stop=True).then_inc(mm.sem, 1)
                            e.matmul(out=psw[1][:], lhsT=ones_row[0:1, :], rhs=rs6[:, jj * TT:(jj + 1) * TT],
                                     start=True, stop=True).then_inc(mm.sem, 1)
                        sch.on("tensor", _bc)
                        et = ev.bump()
                        def _gn(e, jj=jj, mm_bc=mm_bc, h=h, lg=lg, hb=hb, stp=st_gn.get(h - 2)):
                            e.wait_ge(mm.sem, mm_bc)
                            if jj == 0:
                                e.wait_ge(env.sems["g3"].sem, lg)
                                if stp is not None:
                                    e.wait_ge(env.sems[stp[0]].sem, stp[1])
                            last = None
                            for dc in range(2):
                                cc = slice(dc * T + jj * TT, dc * T + (jj + 1) * TT)
                                e.tensor_mul(out=gnacc[hb][:, cc], in0=onb[:, cc], in1=psw[1][:])
                                e.tensor_add(out=gnacc[hb][:, cc], in0=gnacc[hb][:, cc], in1=psw[0][:])
                                last = e.tensor_mul(out=gnacc[hb][:, cc], in0=gnacc[hb][:, cc],
                                                    in1=gh[hb][:, cc])
                            last.then_inc(ev.sem, 1)
                        sch.on("vector", _gn)
                        gn_last = et
                    sch.on("gpsimd", lambda e, t=gn_last: e.wait_ge(ev.sem, t))
                    sslh = "st0" if hb == 0 else "st1"
                    st_t = slot_dma(env, sch, "gpsimd", sslh,
                                    gn_s[2 * h:2 * h + 2].rearrange("c p t -> p c t"),
                                    gnacc[hb].rearrange("p (c t) -> p c t", c=2))
                    st_gn[h] = (sslh, st_t)
                    h_done[h] = gn_last
                fence[0] = h_done[NFC - 1]

        def stage_b2(s):
            scope_fence()
            with ExitStack() as sb_:
                gnt = env.sb(sb_, f"p3d_gnt{s}", [128, 16 * TT], BF16)
                xf = env.sb(sb_, f"p3d_xf{s}", [128, NFC * TT], F32)
                xout = env.sb(sb_, f"p3d_xout{s}", [128, NFC * TT], F32)
                wp = env.sb(sb_, f"p3d_wp{s}", [128, 16 * D], BF16)
                psm = [env.ps(sb_, f"p3d_ps{i}_{s}", [128, TT]) for i in range(2)]
                wt = {}
                for kc in range(16):
                    sl = ("g2", "g3")[kc % 2]
                    wt[sl] = slot_dma(env, sch, "gpsimd", sl, wp[:, kc * D:(kc + 1) * D],
                                      env.dram["w3_proj"][kc * 128:(kc + 1) * 128, :])
                prev_done = None
                prev_evo = None
                for j in range(NTILE):
                    sf = "s0" if j % 2 == 0 else "s1"
                    gsl = "g0" if j % 2 == 0 else "g1"
                    srcx = xsrc[s][:, :, j * TT:(j + 1) * TT].rearrange("c p t -> p c t")
                    if prev_done is not None:
                        sch.on("sync", lambda e, t=prev_done: e.wait_ge(ev.sem, t))
                        sch.on("gpsimd", lambda e, t=prev_done: e.wait_ge(ev.sem, t))
                    ld_t = slot_dma(env, sch, "sync", sf, xf.rearrange("p (c t) -> p c t", c=NFC), srcx)
                    lg_t = slot_dma(env, sch, "gpsimd", gsl, gnt.rearrange("p (c t) -> p c t", c=16),
                                    gn_s[:, :, j * TT:(j + 1) * TT].rearrange("c p t -> p c t"))
                    ev_o = []
                    for m in range(NFC):
                        pb = psm[m % 2]
                        if m >= 2:
                            sch.on("tensor", lambda e, t=ev_o[m - 2]: e.wait_ge(ev.sem, t))
                        elif prev_evo is not None:
                            sch.on("tensor", lambda e, t=prev_evo[6 + m]: e.wait_ge(ev.sem, t))
                        mmt = mm.bump()
                        def _mm(e, m=m, pb=pb, lg_t=lg_t, gsl=gsl, wt=dict(wt)):
                            if m == 0:
                                e.wait_ge(env.sems[gsl].sem, lg_t)
                                for sl, tt in wt.items():
                                    e.wait_ge(env.sems[sl].sem, tt)
                            for kc in range(16):
                                i = e.matmul(out=pb[:], lhsT=wp[:, kc * D + m * 128: kc * D + (m + 1) * 128],
                                             rhs=gnt[:, kc * TT:(kc + 1) * TT], start=(kc == 0), stop=(kc == 15))
                            i.then_inc(mm.sem, 1)
                        sch.on("tensor", _mm)
                        st0n, st1n = env.sems["st0"].n, env.sems["st1"].n
                        et = ev.bump()
                        def _evo(e, mmt=mmt, m=m, pb=pb, ld_t=ld_t, sf=sf, a=st0n, b=st1n):
                            e.wait_ge(mm.sem, mmt)
                            if m == 0:
                                e.wait_ge(env.sems[sf].sem, ld_t)
                                e.wait_ge(env.sems["st0"].sem, a)
                                e.wait_ge(env.sems["st1"].sem, b)
                            cc = slice(m * TT, (m + 1) * TT)
                            e.tensor_add(out=xout[:, cc], in0=pb[:], in1=xf[:, cc]).then_inc(ev.sem, 1)
                        sch.on("vector", _evo)
                        ev_o.append(et)
                    ssl = "st0" if j % 2 == 0 else "st1"
                    sch.on("gpsimd", lambda e, t=ev_o[-1]: e.wait_ge(ev.sem, t))
                    dst = xdst[s][:, :, j * TT:(j + 1) * TT].rearrange("c p t -> p c t")
                    slot_dma(env, sch, "gpsimd", ssl, dst, xout.rearrange("p (c t) -> p c t", c=NFC))
                    prev_done = ev_o[-1]
                    prev_evo = ev_o
                fence[0] = prev_done

        import os as _os
        nseq = int(_os.environ.get("K_P3_DEBUG")) if _os.environ.get("K_P3_DEBUG") else BL
        for s in range(nseq):
            stage_a1(s)
            stage_a2(s)
            stage_b(s)
            stage_b2(s)
        if _os.environ.get("K_P3_DEBUG"):
            dbg = {}
            for nm, src_ap, shp in (("dbg_qs", qs_s, [8, 128, T]), ("dbg_ks", ks_s, [8, 128, T]),
                                    ("dbg_vs", v_s, [16, 128, T]), ("dbg_gs", gate_s, [16, 128, T]),
                                    ("dbg_gn", gn_s, [16, 128, T])):
                o = nc.dram_tensor(nm, shp, BF16, kind="ExternalOutput").ap()
                stx = env.sems["st0"]
                base = stx.n
                def _d(e, o=o, src_ap=src_ap, base=base, t=fence[0]):
                    e.wait_ge(ev.sem, t)
                    e.wait_ge(stx.sem, base)
                    e.dma_start(out=o, in_=src_ap).then_inc(stx.sem, 16)
                sch.on("gpsimd", _d)
                stx.n += 16

        def _drain(e):
            e.wait_ge(env.sems["st0"].sem, env.sems["st0"].n)
            e.wait_ge(env.sems["st1"].sem, env.sems["st1"].n)
        sch.on("gpsimd", _drain)
        emit_barrier(env, sch)
        sch.flush(nc)




def emit_p_l1attna(env, sch, xsrc, xdst):
    nc = env.nc
    ldh, mm, ev, ac = (env.sems[k] for k in ("ldh", "mm", "ev", "ac"))
    ones_bf = env.consts["ones_col_bf"]
    ones_row = env.consts["ones_row_f"]
    CH = 64

    xa_s = nc.dram_tensor("p5_xas", [NFC, 128, T], F32, kind="Internal").ap()

    with ExitStack() as stk:
        prew = env.sb(stk, "p5_prew", [128, NFC], F32)
        ffnw = env.sb(stk, "p5_ffnw", [128, NFC], F32)
        convw = env.sb(stk, "p5_convw", [128, NFC * 4], F32)
        convb = env.sb(stk, "p5_convb", [128, NFC], F32)
        dbias = env.sb(stk, "p5_dbias", [128, NFC], F32)
        for nm, dst in (("p_prew", prew), ("p_ffnw", ffnw), ("p_convw", convw),
                        ("p_convb", convb), ("p_dbias", dbias)):
            ldh.bump(16)
            sch.on("sync", lambda e, nm=nm, dst=dst: e.dma_start(out=dst[:], in_=env.dram[nm]).then_inc(ldh.sem, 16))
        par_ld = ldh.n
        fence = [None]

        def scope_fence():
            if fence[0] is not None:
                for eng in ("sync", "gpsimd", "tensor"):
                    sch.on(eng, lambda e, t=fence[0]: e.wait_ge(ev.sem, t))
            st0n, st1n = env.sems["st0"].n, env.sems["st1"].n
            for eng in ("sync", "gpsimd", "tensor", "vector", "scalar"):
                sch.on(eng, lambda e, a=st0n, b=st1n: (e.wait_ge(env.sems["st0"].sem, a),
                                                       e.wait_ge(env.sems["st1"].sem, b)))

        def stage_a(s):
            scope_fence()
            with ExitStack() as sa:
                xbf = env.sb(sa, f"p5a_xbf{s}", [128, NFC * TT], BF16)
                ln = LNBufs(env, sa, f"p5a_ln{s}")
                hbuf = env.sb(sa, f"p5a_h{s}", [128, NFC * (TT + 3)], F32)
                hsave = env.sb(sa, f"p5a_hs{s}", [128, NFC * 3], F32)
                hcb = env.sb(sa, f"p5a_hc{s}", [128, NFC * TT], BF16)
                wa = env.sb(sa, f"p5a_wa{s}", [128, NFC * D], BF16)
                wi = env.sb(sa, f"p5a_wi{s}", [128, NFC * D], BF16)
                wv = env.sb(sa, f"p5a_wv{s}", [128, NFC * D], BF16)
                wo = env.sb(sa, f"p5a_wo{s}", [128, NFC * D], BF16)
                asb = env.sb(sa, f"p5a_a{s}", [128, NFC * TT], F32)
                isb = env.sb(sa, f"p5a_i{s}", [128, NFC * TT], BF16)
                t0 = env.sb(sa, f"p5a_t0{s}", [128, TT], F32)
                t1 = env.sb(sa, f"p5a_t1{s}", [128, TT], F32)
                t2 = env.sb(sa, f"p5a_t2{s}", [128, TT], F32)
                onesr = env.sb(sa, f"p5a_ones{s}", [128, CH], F32)
                rec = env.sb(sa, f"p5a_rec{s}", [128, NFC * TT], F32)
                recb = isb
                stbuf = env.sb(sa, f"p5a_st{s}", [128, NFC], F32)
                endc = env.sb(sa, f"p5a_ec{s}", [128, 16], F32)
                d0s = env.sb(sa, f"p5a_d0{s}", [128, 16], F32)
                d1s = env.sb(sa, f"p5a_d1{s}", [128, 16], F32)
                xaout = rec
                psm = [env.ps(sa, f"p5a_ps{i}_{s}", [128, TT]) for i in range(2)]
                psA2 = env.ps(sa, f"p5a_psA2{s}", [1, TT])
                psR2 = env.ps(sa, f"p5a_psR2{s}", [128, TT])

                wt = {}
                prev_grp = None
                for wname, wdst in (("w5_aT", wa), ("w5_iT", wi), ("w5_vT", wv), ("w5_oT", wo)):
                    for sl, t in emit_weight_load_T(env, sch, env.dram[wname], wdst, D):
                        wt[sl] = max(wt.get(sl, 0), t)
                ev0 = ev.bump()
                def _init(e):
                    e.memset(onesr[:], 1.0)
                    e.memset(d0s[:, 0:1], 0.0)
                    e.memset(stbuf[:], 0.0).then_inc(ev.sem, 1)
                sch.on("vector", _init)

                prev_ln = None
                prev_store_done = None
                for j in range(NTILE):
                    sb_ = "g0" if j % 2 == 0 else "g1"
                    srcx = xsrc[s][:, :, j * TT:(j + 1) * TT].rearrange("c p t -> p c t")
                    if prev_ln is not None:
                        sch.on("gpsimd", lambda e, t=prev_ln: e.wait_ge(ev.sem, t))
                    ldg_t = slot_dma(env, sch, "gpsimd", sb_, xbf.rearrange("p (c t) -> p c t", c=NFC), srcx)
                    ev_ln, _lninfo = emit_ln(env, sch, ln, j, None, xbf, (None, None), (sb_, ldg_t), norm_src="bf")
                    ac_rq = ac.bump()
                    def _rq(e, t=ev_ln):
                        e.wait_ge(ev.sem, t)
                        e.activation(out=ln.sq[:], in_=ln.xln[:, :NFC * TT] if ln.shift == 0 else ln.sq[:],
                                     func=AF.Square).then_inc(ac.sem, 1)
                    sch.on("scalar", _rq)
                    mm_r2 = mm.bump()
                    def _st2(e, ac_rq=ac_rq):
                        e.wait_ge(ac.sem, ac_rq)
                        for kc in range(NFC):
                            i = e.matmul(out=psA2[0:1, :], lhsT=ones_bf[:, 0:1], rhs=ln.sq[:, kc * TT:(kc + 1) * TT],
                                         start=(kc == 0), stop=(kc == NFC - 1))
                        i.then_inc(mm.sem, 1)
                    sch.on("tensor", _st2)
                    R = ln.rows
                    m2r, sdr, rsr = R[0:1, 1 * TT:2 * TT], R[0:1, 3 * TT:4 * TT], R[0:1, 5 * TT:6 * TT]
                    ev_r2 = ev.bump()
                    def _rw2(e, mm_r2=mm_r2):
                        e.wait_ge(mm.sem, mm_r2)
                        e.tensor_scalar_mul(out=m2r, in0=psA2[0:1, :], scalar1=1.0 / 1024.0).then_inc(ev.sem, 1)
                    sch.on("vector", _rw2)
                    ac_rs2 = ac.bump(2)
                    def _sd2(e, ev_r2=ev_r2):
                        e.wait_ge(ev.sem, ev_r2)
                        e.activation(out=sdr, in_=m2r, func=AF.Ln, bias=env.fconst(1e-6)[0:1, :]).then_inc(ac.sem, 1)
                        e.activation(out=rsr, in_=sdr, func=AF.Exp, scale=-0.5).then_inc(ac.sem, 1)
                    sch.on("scalar", _sd2)
                    mm_bc2 = mm.bump()
                    def _bc2(e, ac_rs2=ac_rs2):
                        e.wait_ge(ac.sem, ac_rs2)
                        e.matmul(out=psR2[:], lhsT=ones_row[0:1, :], rhs=rsr, start=True, stop=True).then_inc(mm.sem, 1)
                    sch.on("tensor", _bc2)
                    SS = TT + 3
                    ev_hc = ev.bump()
                    def _h(e, mm_bc2=mm_bc2, j=j):
                        e.wait_ge(mm.sem, mm_bc2)
                        e.wait_ge(ldh.sem, par_ld)
                        for c in range(NFC):
                            cur = hbuf[:, c * SS + 3:(c + 1) * SS]
                            if j == 0:
                                e.memset(hbuf[:, c * SS: c * SS + 3], 0.0)
                            else:
                                e.tensor_copy(out=hbuf[:, c * SS: c * SS + 3], in_=hsave[:, c * 3:(c + 1) * 3])
                            e.tensor_mul(out=cur, in0=ln.xln[:, c * TT:(c + 1) * TT], in1=psR2[:])
                            e.tensor_scalar_mul(out=cur, in0=cur, scalar1=prew[:, c:c + 1])
                        for c in range(NFC):
                            e.tensor_copy(out=hsave[:, c * 3:(c + 1) * 3], in_=hbuf[:, (c + 1) * SS - 3:(c + 1) * SS])
                        last = None
                        for c in range(NFC):
                            dst = hcb[:, c * TT:(c + 1) * TT]
                            sh = lambda k: hbuf[:, c * SS + k: c * SS + k + TT]
                            e.tensor_scalar_mul(out=t0[:], in0=sh(0), scalar1=convw[:, c * 4 + 0: c * 4 + 1])
                            for kk in range(1, 4):
                                e.scalar_tensor_tensor(out=t0[:], in0=sh(kk), scalar=convw[:, c * 4 + kk: c * 4 + kk + 1],
                                                       in1=t0[:], op0=ALU.mult, op1=ALU.add)
                            last = e.tensor_scalar_add(out=dst, in0=t0[:], scalar1=convb[:, c:c + 1])
                        last.then_inc(ev.sem, 1)
                    sch.on("vector", _h)
                    prev_ln = ev_hc

                    grp = []
                    sch.on("tensor", lambda e, t=ev_hc, wt=dict(wt): (e.wait_ge(ev.sem, t),
                           [e.wait_ge(env.sems[sl].sem, tt) for sl, tt in wt.items()] and None))
                    for gi, (wsb, kind) in enumerate(((wa, "a"), (wi, "i"), (wv, "v"))):
                        for m in range(NFC):
                            pb = psm[m % 2]
                            if len(grp) >= 2:
                                cse, ct = grp[-2]
                                sch.on("tensor", lambda e, cse=cse, ct=ct: e.wait_ge(cse.sem, ct))
                            elif prev_grp is not None:
                                cse, ct = prev_grp[len(grp) - 2]
                                sch.on("tensor", lambda e, cse=cse, ct=ct: e.wait_ge(cse.sem, ct))
                            mmt = mm.bump()
                            def _mmx(e, m=m, pb=pb, wsb=wsb):
                                for kc in range(NFC):
                                    i = e.matmul(out=pb[:], lhsT=wsb[:, kc * D + m * 128: kc * D + (m + 1) * 128],
                                                 rhs=hcb[:, kc * TT:(kc + 1) * TT], start=(kc == 0), stop=(kc == NFC - 1))
                                i.then_inc(mm.sem, 1)
                            sch.on("tensor", _mmx)
                            cc = slice(m * TT, (m + 1) * TT)
                            if kind == "a":
                                at = ac.bump()
                                def _eva(e, mmt=mmt, pb=pb, cc=cc, m=m):
                                    e.wait_ge(mm.sem, mmt)
                                    e.activation(out=asb[:, cc], in_=pb[:], func=AF.Sigmoid,
                                                 bias=dbias[:, m:m + 1]).then_inc(ac.sem, 1)
                                sch.on("scalar", _eva)
                                grp.append((ac, at))
                            elif kind == "i":
                                at = ac.bump()
                                def _evi(e, mmt=mmt, pb=pb, cc=cc):
                                    e.wait_ge(mm.sem, mmt)
                                    e.activation(out=isb[:, cc], in_=pb[:], func=AF.Sigmoid).then_inc(ac.sem, 1)
                                sch.on("scalar", _evi)
                                grp.append((ac, at))
                            else:
                                at = ev.bump()
                                def _evv(e, mmt=mmt, pb=pb, cc=cc, ga=grp[NFC + m][1]):
                                    e.wait_ge(mm.sem, mmt)
                                    e.wait_ge(ac.sem, ga)
                                    e.tensor_mul(out=isb[:, cc], in0=pb[:], in1=isb[:, cc]).then_inc(ev.sem, 1)
                                sch.on("vector", _evv)
                                grp.append((ev, at))
                    prev_grp = list(grp)

                    ev_chunks = []
                    for c in range(NFC):
                        cc = slice(c * TT, (c + 1) * TT)
                        a_t = grp[c][1]
                        iv_t = grp[2 * NFC + c][1]
                        ac_sq3 = ac.bump()
                        ev_pre = ev.bump()
                        def _pre(e, cc=cc, a_t=a_t, c=c):
                            e.wait_ge(ac.sem, a_t)
                            e.tensor_mul(out=t0[:], in0=asb[:, cc], in1=asb[:, cc])
                            e.tensor_scalar(out=t0[:], in0=t0[:], scalar1=-1.0, scalar2=1.0,
                                            op0=ALU.mult, op1=ALU.add)
                            e.tensor_scalar_max(out=t0[:], in0=t0[:], scalar1=1e-8).then_inc(ev.sem, 1)
                        sch.on("vector", _pre)
                        def _sq3(e, ev_pre=ev_pre):
                            e.wait_ge(ev.sem, ev_pre)
                            e.activation(out=t1[:], in_=t0[:], func=AF.Sqrt).then_inc(ac.sem, 1)
                        sch.on("scalar", _sq3)
                        ev_cl = ev.bump()
                        def _cl(e, cc=cc):
                            e.tensor_scalar_max(out=t2[:], in0=asb[:, cc], scalar1=1e-10).then_inc(ev.sem, 1)
                        sch.on("vector", _cl)
                        ac_ln2 = ac.bump()
                        def _ln2(e, ev_cl=ev_cl):
                            e.wait_ge(ev.sem, ev_cl)
                            e.activation(out=t2[:], in_=t2[:], func=AF.Ln).then_inc(ac.sem, 1)
                        sch.on("scalar", _ln2)
                        ev_is = ev.bump()
                        def _is(e, ac_sq3=ac_sq3, iv_t=iv_t, cc=cc):
                            e.wait_ge(ac.sem, ac_sq3)
                            e.wait_ge(ev.sem, iv_t)
                            e.tensor_mul(out=isb[:, cc], in0=t1[:], in1=isb[:, cc]).then_inc(ev.sem, 1)
                        sch.on("vector", _is)
                        ev_cs0 = ev.bump()
                        ev_cs1 = ev.bump()
                        def _cs1(e, ac_ln2=ac_ln2, ac_sq3=ac_sq3, ev_cs0=ev_cs0):
                            e.wait_ge(ac.sem, ac_ln2)
                            e.wait_ge(ac.sem, ac_sq3)
                            for sub in range(TT // CH):
                                ss = slice(sub * CH, (sub + 1) * CH)
                                i = e.tensor_tensor_scan(out=t0[:, ss], data0=onesr[:], data1=t2[:, ss],
                                                         initial=0.0, op0=ALU.mult, op1=ALU.add)
                            i.then_inc(ev.sem, 1)
                            e.wait_ge(ev.sem, ev_cs0)
                            e.tensor_scalar_max(out=t2[:], in0=t0[:], scalar1=-23.02585093).then_inc(ev.sem, 1)
                        sch.on("vector", _cs1)
                        ac_ex2 = ac.bump(2)
                        def _ex2(e, ev_cs1=ev_cs1):
                            e.wait_ge(ev.sem, ev_cs1)
                            e.activation(out=t1[:], in_=t0[:], func=AF.Exp).then_inc(ac.sem, 1)
                            e.activation(out=t2[:], in_=t2[:], func=AF.Exp, scale=-1.0).then_inc(ac.sem, 1)
                        sch.on("scalar", _ex2)
                        ev_s2 = ev.bump()
                        ev_s3 = ev.bump()
                        ev_g = ev.bump()
                        ev_ch = ev.bump()
                        ev_st = ev.bump()
                        ev_done = ev.bump()
                        nsub = TT // CH
                        def _scan2(e, ac_ex2=ac_ex2, cc=cc, c=c, ev_s2=ev_s2, ev_s3=ev_s3,
                                   ev_g=ev_g, ev_ch=ev_ch, ev_st=ev_st):
                            e.wait_ge(ac.sem, ac_ex2)
                            e.tensor_scalar_min(out=t2[:], in0=t2[:], scalar1=1e10)
                            e.tensor_mul(out=t2[:], in0=t2[:], in1=isb[:, cc]).then_inc(ev.sem, 1)
                            e.wait_ge(ev.sem, ev_s2)
                            for sub in range(nsub):
                                ss = slice(sub * CH, (sub + 1) * CH)
                                i = e.tensor_tensor_scan(out=t2[:, ss], data0=onesr[:], data1=t2[:, ss],
                                                         initial=0.0, op0=ALU.mult, op1=ALU.add)
                            i.then_inc(ev.sem, 1)
                            e.wait_ge(ev.sem, ev_s3)
                            e.tensor_mul(out=t2[:], in0=t1[:], in1=t2[:])
                            t1v = t1[:, :].rearrange("p (s c) -> p s c", c=CH)[:, :, CH - 1:CH] \
                                .rearrange("p s one -> p (s one)")
                            t2v = t2[:, :].rearrange("p (s c) -> p s c", c=CH)[:, :, CH - 1:CH] \
                                .rearrange("p s one -> p (s one)")
                            e.tensor_copy(out=d0s[:, 1:1 + nsub], in_=t1v)
                            e.tensor_copy(out=d1s[:, 1:1 + nsub], in_=t2v)
                            e.tensor_copy(out=d1s[:, 0:1], in_=stbuf[:, c:c + 1]).then_inc(ev.sem, 1)
                            e.wait_ge(ev.sem, ev_g)
                            e.tensor_tensor_scan(out=endc[:, 0:nsub + 1], data0=d0s[:, 0:nsub + 1],
                                                 data1=d1s[:, 0:nsub + 1], initial=0.0,
                                                 op0=ALU.mult, op1=ALU.add).then_inc(ev.sem, 1)
                            e.wait_ge(ev.sem, ev_ch)
                            e.tensor_copy(out=stbuf[:, c:c + 1], in_=endc[:, nsub:nsub + 1])
                            for sub in range(nsub):
                                ss = slice(sub * CH, (sub + 1) * CH)
                                i = e.scalar_tensor_tensor(out=rec[:, cc][:, ss], in0=t1[:, ss],
                                                           scalar=endc[:, sub:sub + 1],
                                                           in1=t2[:, ss], op0=ALU.mult, op1=ALU.add)
                            i.then_inc(ev.sem, 1)
                            e.wait_ge(ev.sem, ev_st)
                            e.tensor_copy(out=recb[:, cc], in_=rec[:, cc]).then_inc(ev.sem, 1)
                        sch.on("vector", _scan2)
                        ev_chunks.append(ev_done)
                    ev_o = []
                    for m in range(NFC):
                        pb = psm[m % 2]
                        if m >= 2:
                            sch.on("tensor", lambda e, t=ev_o[m - 2]: e.wait_ge(ev.sem, t))
                        else:
                            cse, ct = prev_grp[len(grp) - 2 + m] if False else grp[-2 + m]
                            sch.on("tensor", lambda e, cse=cse, ct=ct: e.wait_ge(cse.sem, ct))
                        sch.on("tensor", lambda e, t=ev_chunks[-1]: e.wait_ge(ev.sem, t))
                        mmt = mm.bump()
                        def _mmo(e, m=m, pb=pb):
                            for kc in range(NFC):
                                i = e.matmul(out=pb[:], lhsT=wo[:, kc * D + m * 128: kc * D + (m + 1) * 128],
                                             rhs=recb[:, kc * TT:(kc + 1) * TT], start=(kc == 0), stop=(kc == NFC - 1))
                            i.then_inc(mm.sem, 1)
                        sch.on("tensor", _mmo)
                        if m == 0 and prev_store_done is not None:
                            sch.on("vector", lambda e, a=prev_store_done: e.wait_ge(env.sems["st0"].sem, a))
                        et = ev.bump()
                        def _evo(e, mmt=mmt, m=m, pb=pb):
                            e.wait_ge(mm.sem, mmt)
                            cc = slice(m * TT, (m + 1) * TT)
                            e.tensor_add(out=xaout[:, cc], in0=pb[:], in1=ln.xln[:, cc]).then_inc(ev.sem, 1)
                        sch.on("vector", _evo)
                        ev_o.append(et)
                    sch.on("gpsimd", lambda e, t=ev_o[-1]: e.wait_ge(ev.sem, t))
                    slot_dma(env, sch, "gpsimd", "st0",
                             xa_s[:, :, j * TT:(j + 1) * TT].rearrange("c p t -> p c t"),
                             xaout.rearrange("p (c t) -> p c t", c=NFC))
                    prev_store_done = env.sems["st0"].n
                    fence[0] = ev_o[-1]

        def stage_b(s):
            scope_fence()
            with ExitStack() as sb_:
                hn = env.sb(sb_, f"p5b_hn{s}", [128, NFC * T], BF16)
                with ExitStack() as sb1:
                    xabf = env.sb(sb1, f"p5b_xabf{s}", [128, NFC * TT], BF16)
                    sq2 = env.sb(sb1, f"p5b_sq{s}", [128, NFC * TT], BF16)
                    rows = env.sb(sb1, f"p5b_rows{s}", [1, 2 * TT], F32)
                    psA = env.ps(sb1, f"p5b_psA{s}", [1, TT])
                    psR = env.ps(sb1, f"p5b_psR{s}", [128, TT])
                    prev_hn = None
                    for j in range(NTILE):
                        gsl = "g0" if j % 2 == 0 else "g1"
                        if prev_hn is not None:
                            sch.on("gpsimd", lambda e, t=prev_hn: e.wait_ge(ev.sem, t))
                        lb = slot_dma(env, sch, "gpsimd", gsl, xabf.rearrange("p (c t) -> p c t", c=NFC),
                                      xa_s[:, :, j * TT:(j + 1) * TT].rearrange("c p t -> p c t"))
                        ac_sq = ac.bump()
                        def _sq(e, lb=lb, gsl=gsl):
                            e.wait_ge(env.sems[gsl].sem, lb)
                            e.activation(out=sq2[:], in_=xabf[:], func=AF.Square).then_inc(ac.sem, 1)
                        sch.on("scalar", _sq)
                        mm_s = mm.bump()
                        def _st(e, ac_sq=ac_sq):
                            e.wait_ge(ac.sem, ac_sq)
                            for kc in range(NFC):
                                i = e.matmul(out=psA[0:1, :], lhsT=ones_bf[:, 0:1], rhs=sq2[:, kc * TT:(kc + 1) * TT],
                                             start=(kc == 0), stop=(kc == NFC - 1))
                            i.then_inc(mm.sem, 1)
                        sch.on("tensor", _st)
                        m2r, rsr = rows[0:1, 0:TT], rows[0:1, TT:2 * TT]
                        ev_r = ev.bump()
                        def _rw(e, mm_s=mm_s):
                            e.wait_ge(mm.sem, mm_s)
                            e.tensor_scalar_mul(out=m2r, in0=psA[0:1, :], scalar1=1.0 / 1024.0).then_inc(ev.sem, 1)
                        sch.on("vector", _rw)
                        ac_rs = ac.bump(2)
                        def _sd(e, ev_r=ev_r):
                            e.wait_ge(ev.sem, ev_r)
                            e.activation(out=m2r, in_=m2r, func=AF.Ln, bias=env.fconst(1e-6)[0:1, :]).then_inc(ac.sem, 1)
                            e.activation(out=rsr, in_=m2r, func=AF.Exp, scale=-0.5).then_inc(ac.sem, 1)
                        sch.on("scalar", _sd)
                        mm_b = mm.bump()
                        def _bc(e, ac_rs=ac_rs):
                            e.wait_ge(ac.sem, ac_rs)
                            e.matmul(out=psR[:], lhsT=ones_row[0:1, :], rhs=rsr, start=True, stop=True).then_inc(mm.sem, 1)
                        sch.on("tensor", _bc)
                        et = ev.bump()
                        def _hn(e, mm_b=mm_b, j=j):
                            e.wait_ge(mm.sem, mm_b)
                            e.wait_ge(ldh.sem, par_ld)
                            last = None
                            for c in range(NFC):
                                dst = hn[:, c * T + j * TT: c * T + (j + 1) * TT]
                                e.tensor_mul(out=dst, in0=xabf[:, c * TT:(c + 1) * TT], in1=psR[:])
                                last = e.tensor_scalar_mul(out=dst, in0=dst, scalar1=ffnw[:, c:c + 1])
                            last.then_inc(ev.sem, 1)
                        sch.on("vector", _hn)
                        prev_hn = et
                gusb = env.sb(sb_, f"p5b_gu{s}", [128, 32 * TT], BF16)
                gp = env.sb(sb_, f"p5b_gp{s}", [128, TT], F32)
                upf = env.sb(sb_, f"p5b_up{s}", [128, TT], F32)
                wgu = [env.sb(sb_, f"p5b_wgu{i}_{s}", [128, NFC * 128], BF16) for i in range(2)]
                wup = [env.sb(sb_, f"p5b_wup{i}_{s}", [128, NFC * 128], BF16) for i in range(2)]
                wdn = [env.sb(sb_, f"p5b_wdn{i}_{s}", [128, D], BF16) for i in range(4)]
                xaf = env.sb(sb_, f"p5b_xaf{s}", [128, NFC * TT], F32)
                xff = env.sb(sb_, f"p5b_xf{s}", [128, NFC * TT], F32)
                xout = env.sb(sb_, f"p5b_xo{s}", [128, NFC * TT], F32)
                hn_done = prev_hn
                prev_evo = None
                for tt_ in range(NTILE):
                    with ExitStack() as sg:
                        psg = [env.ps(sg, f"p5b_psg{i}_{s}_{tt_}", [128, TT]) for i in range(2)]
                        grp = []
                        mm_gu_done = {}
                        GS4 = ("g0", "g1", "g2", "g3")
                        for cg in range(32):
                            if cg >= 2:
                                sch.on("gpsimd", lambda e, t=mm_gu_done[cg - 2]: e.wait_ge(mm.sem, t))
                            lg = slot_dma(env, sch, "gpsimd", GS4[(2 * cg) % 4],
                                          wgu[cg % 2].rearrange("p (kc d) -> p kc d", kc=NFC),
                                          env.dram["w5_gu_k"][cg].rearrange("(kc p) d -> p kc d", p=128))
                            lu = slot_dma(env, sch, "gpsimd", GS4[(2 * cg + 1) % 4],
                                          wup[cg % 2].rearrange("p (kc d) -> p kc d", kc=NFC),
                                          env.dram["w5_gu_k"][cg + 32].rearrange("(kc p) d -> p kc d", p=128))
                            for gu_i in range(2):
                                pb = psg[gu_i]
                                if len(grp) >= 2:
                                    cse, ct = grp[-2]
                                    sch.on("tensor", lambda e, cse=cse, ct=ct: e.wait_ge(cse.sem, ct))
                                elif prev_evo is not None:
                                    sch.on("tensor", lambda e, t=prev_evo: e.wait_ge(ev.sem, t))
                                mmt = mm.bump()
                                def _mmg(e, cg=cg, gu_i=gu_i, pb=pb, lg=lg, lu=lu, tt_=tt_, hn_done=hn_done):
                                    if cg == 0 and gu_i == 0:
                                        e.wait_ge(ev.sem, hn_done)
                                    e.wait_ge(env.sems[GS4[(2 * cg + (1 if gu_i else 0)) % 4]].sem, lu if gu_i else lg)
                                    wsb = (wgu if gu_i == 0 else wup)[cg % 2]
                                    for kc in range(NFC):
                                        i = e.matmul(out=pb[:], lhsT=wsb[:, kc * 128:(kc + 1) * 128],
                                                     rhs=hn[:, kc * T + tt_ * TT: kc * T + (tt_ + 1) * TT],
                                                     start=(kc == 0), stop=(kc == NFC - 1))
                                    i.then_inc(mm.sem, 1)
                                sch.on("tensor", _mmg)
                                if gu_i == 0:
                                    at = ac.bump()
                                    gp_free = grp[-1][1] if grp else None
                                    def _evg(e, mmt=mmt, pb=pb, gp_free=gp_free):
                                        e.wait_ge(mm.sem, mmt)
                                        if gp_free is not None:
                                            e.wait_ge(ev.sem, gp_free)
                                        e.activation(out=gp[:], in_=pb[:], func=AF.Silu).then_inc(ac.sem, 1)
                                    sch.on("scalar", _evg)
                                    grp.append((ac, at))
                                else:
                                    et = ev.bump()
                                    def _evu(e, mmt=mmt, pb=pb, cg=cg, ga=grp[-1][1]):
                                        e.wait_ge(mm.sem, mmt)
                                        e.tensor_copy(out=upf[:], in_=pb[:])
                                        e.wait_ge(ac.sem, ga)
                                        e.tensor_mul(out=gusb[:, cg * TT:(cg + 1) * TT], in0=gp[:],
                                                     in1=upf[:]).then_inc(ev.sem, 1)
                                    sch.on("vector", _evu)
                                    grp.append((ev, et))
                                if gu_i == 1:
                                    mm_gu_done[cg] = mm.n
                        gu_done = grp[-1][1]
                    with ExitStack() as sd:
                        psd = [env.ps(sd, f"p5b_psd{i}_{s}_{tt_}", [128, TT]) for i in range(8)]
                        sf = "s0" if tt_ % 2 == 0 else "s1"
                        if prev_evo is not None:
                            sch.on("sync", lambda e, t=prev_evo: e.wait_ge(ev.sem, t))
                        la = slot_dma(env, sch, "sync", sf, xaf.rearrange("p (c t) -> p c t", c=NFC),
                                      xa_s[:, :, tt_ * TT:(tt_ + 1) * TT].rearrange("c p t -> p c t"))
                        lx = slot_dma(env, sch, "sync", sf, xff.rearrange("p (c t) -> p c t", c=NFC),
                                      xsrc[s][:, :, tt_ * TT:(tt_ + 1) * TT].rearrange("c p t -> p c t"))
                        ev_o = []
                        mm_dn_done = {}
                        GS = ("g0", "g1", "g2", "g3")
                        for kc in range(32):
                            if kc >= 4:
                                sch.on("gpsimd", lambda e, t=mm_dn_done[kc - 4]: e.wait_ge(mm.sem, t))
                            ld_w = slot_dma(env, sch, "gpsimd", GS[kc % 4],
                                            wdn[kc % 4][:], env.dram["w5_dn_k"][kc])
                            mm.bump(8)
                            def _mmd(e, kc=kc, ld_w=ld_w, gu_done=gu_done, pe_guard=prev_evo):
                                if kc == 0:
                                    e.wait_ge(ev.sem, gu_done)
                                    if pe_guard is not None:
                                        e.wait_ge(ev.sem, pe_guard)
                                e.wait_ge(env.sems[GS[kc % 4]].sem, ld_w)
                                for m in range(8):
                                    e.matmul(out=psd[m][:], lhsT=wdn[kc % 4][:, m * 128:(m + 1) * 128],
                                             rhs=gusb[:, kc * TT:(kc + 1) * TT],
                                             start=(kc == 0), stop=(kc == 31)).then_inc(mm.sem, 1)
                            sch.on("tensor", _mmd)
                            mm_dn_done[kc] = mm.n
                        mm_fin = mm.n
                        for m in range(NFC):
                            et = ev.bump()
                            def _evd(e, mm_fin=mm_fin, m=m, la=la, lx=lx, sf=sf):
                                e.wait_ge(mm.sem, mm_fin)
                                if m == 0:
                                    e.wait_ge(env.sems[sf].sem, lx)
                                cc = slice(m * TT, (m + 1) * TT)
                                e.tensor_add(out=xout[:, cc], in0=psd[m][:], in1=xaf[:, cc])
                                e.tensor_add(out=xout[:, cc], in0=xout[:, cc], in1=xff[:, cc]).then_inc(ev.sem, 1)
                            sch.on("vector", _evd)
                            ev_o.append(et)
                        ssl = "st0" if tt_ % 2 == 0 else "st1"
                        sch.on("gpsimd", lambda e, t=ev_o[-1]: e.wait_ge(ev.sem, t))
                        dst = xdst[s][:, :, tt_ * TT:(tt_ + 1) * TT].rearrange("c p t -> p c t")
                        slot_dma(env, sch, "gpsimd", ssl, dst, xout.rearrange("p (c t) -> p c t", c=NFC))
                        prev_evo = ev_o[-1]
                fence[0] = prev_evo

        import os as _os
        if _os.environ.get("K_P5_DEBUG"):
            stage_a(0)
            o = nc.dram_tensor("dbg_xa", [NFC, 128, T], F32, kind="ExternalOutput").ap()
            stx = env.sems["st0"]
            base = stx.n
            def _d(e, base=base, t=fence[0]):
                e.wait_ge(ev.sem, t)
                e.wait_ge(stx.sem, base)
                e.dma_start(out=o, in_=xa_s).then_inc(stx.sem, 16)
                e.wait_ge(stx.sem, base + 16)
            sch.on("gpsimd", _d)
            stx.n += 16
            emit_barrier(env, sch)
            sch.flush(nc)
            return
        for s in range(BL):
            stage_a(s)
            stage_b(s)

        def _drain(e):
            e.wait_ge(env.sems["st0"].sem, env.sems["st0"].n)
            e.wait_ge(env.sems["st1"].sem, env.sems["st1"].n)
        sch.on("gpsimd", _drain)
        emit_barrier(env, sch)
        sch.flush(nc)


def _host_consts():
    c = {}
    c["identf"] = np.eye(128, dtype=np.float32)
    import ml_dtypes
    c["identb"] = np.eye(128, dtype=np.float32).astype(ml_dtypes.bfloat16)
    c["ones_col_bf"] = np.ones((128, 1), np.float32).astype(ml_dtypes.bfloat16)
    c["ones_row_f"] = np.ones((1, 128), np.float32)
    c["iota_col"] = np.arange(128, dtype=np.float32).reshape(128, 1)
    c["iota2d"] = (np.arange(128)[:, None] + 128 * np.arange(16)[None, :]).astype(np.float32)
    c["iota_row"] = np.arange(T, dtype=np.float32).reshape(1, T)
    s = np.arange(128)[:, None]
    t = np.arange(128)[None, :]
    c["mask01"] = (t >= s).astype(np.float32)
    dd, gg = np.divmod(np.arange(512), 16)
    c["ind_d"] = (np.arange(32)[:, None] == dd[None, :]).astype(np.float32)
    c["ind_g"] = (np.arange(16)[:, None] == gg[None, :]).astype(np.float32)
    R = np.zeros((128, 128), np.float32)
    for i in range(64):
        R[2 * i, 2 * i + 1] = -1.0
        R[2 * i + 1, 2 * i] = 1.0
    c["rt_bf"] = R.T.astype(ml_dtypes.bfloat16)
    return c


def _host_rotary():
    QK = 1024
    SCALE_BASE = 512.0
    dim_half = QK // 2
    base_scale = (np.arange(0, QK, 2, dtype=np.float32) + 0.4 * QK) / (1.4 * QK)
    pos = np.arange(T, dtype=np.float32)
    scale = base_scale[None, :] ** (pos[:, None] / SCALE_BASE)
    inv_freq = 1.0 / 10000.0 ** (np.arange(dim_half, dtype=np.float32) / dim_half)
    sinus = pos[:, None] * inv_freq[None, :]
    sin = np.sin(sinus).astype(np.float32)
    cos = np.cos(sinus).astype(np.float32)
    sc2 = np.repeat(scale, 2, -1)
    sin2 = np.repeat(sin, 2, -1)
    cos2 = np.repeat(cos, 2, -1)
    def per_head(tab):
        return np.ascontiguousarray(tab.T.reshape(8, 128, T))
    return {
        "rot_cq": per_head((cos2 * sc2).astype(np.float32)),
        "rot_sq": per_head((sin2 * sc2).astype(np.float32)),
        "rot_ck": per_head((cos2 / sc2).astype(np.float32)),
        "rot_sk": per_head((sin2 / sc2).astype(np.float32)),
    }


def host_prep(inputs):
    g = {}
    tr = lambda a: np.ascontiguousarray(np.asarray(a).T)
    rs = lambda a, *shp: np.ascontiguousarray(np.asarray(a, np.float32).reshape(*shp))
    pcol = lambda a: np.ascontiguousarray(np.asarray(a, np.float32).reshape(-1, 128).T)
    g.update(_host_consts())
    g.update(_host_rotary())
    g["w1_inT"] = tr(inputs["L0_attn_a_W_in"])
    g["w1_siT"] = tr(inputs["L0_attn_a_W_sel_in"])
    g["w1_soT"] = tr(inputs["L0_attn_a_W_sel_out"])
    g["w1_outT"] = tr(inputs["L0_attn_a_W_out"])
    g["p_dparam"] = pcol(inputs["L0_attn_a_d_param"])
    g["w2_rT"] = tr(inputs["L0_ffn_W_r"])
    g["w2_kT"] = tr(inputs["L0_ffn_W_k"])
    g["w2_vT"] = tr(inputs["L0_ffn_W_v"])
    g["p_mur"] = pcol(inputs["L0_ffn_mu_r"])
    g["p_muk"] = pcol(inputs["L0_ffn_mu_k"])
    g["w3_qkv"] = np.ascontiguousarray(np.asarray(inputs["L0_attn_b_W_qkv"]))
    g["w3_g"] = np.ascontiguousarray(np.asarray(inputs["L0_attn_b_W_g"]))
    g["w3_proj"] = np.ascontiguousarray(np.asarray(inputs["L0_attn_b_W_proj"]))
    g["p_decay"] = rs(inputs["L0_attn_b_decay"], 1, 8)
    g["w5_aT"] = tr(inputs["L1_attn_a_w_a"])
    g["w5_iT"] = tr(inputs["L1_attn_a_w_i"])
    g["w5_vT"] = tr(inputs["L1_attn_a_w_v"])
    g["w5_oT"] = tr(inputs["L1_attn_a_out_proj_w"])
    Wgu = tr(inputs["L1_attn_a_w_gate_up"])
    g["w5_gu_k"] = np.ascontiguousarray(Wgu.reshape(1024, 64, 128).transpose(1, 0, 2))
    g["w5_dn_k"] = np.ascontiguousarray(tr(inputs["L1_attn_a_w_down"]).reshape(32, 128, 1024))
    g["p_convw"] = np.ascontiguousarray(np.asarray(inputs["L1_attn_a_conv_w"], np.float32).reshape(NFC, 128, 4).transpose(1, 0, 2).reshape(128, NFC * 4))
    g["p_convb"] = pcol(inputs["L1_attn_a_conv_b"])
    g["p_prew"] = pcol(inputs["L1_attn_a_pre_norm_w"])
    g["p_ffnw"] = pcol(inputs["L1_attn_a_ffn_norm_w"])
    g["p_dbias"] = pcol(inputs["L1_attn_a_decay_bias"])
    for k in ("G1_R", "G2_R", "G1_I", "G2_I"):
        g["w6_" + k] = np.ascontiguousarray(np.asarray(inputs["L1_ffn_" + k], np.float32))
    g["p_b6"] = pcol(np.concatenate([np.asarray(inputs["L1_ffn_bias_real"]),
                                     np.asarray(inputs["L1_ffn_bias_imag"])]))
    g["w7_qkv"] = np.ascontiguousarray(np.asarray(inputs["L1_attn_b_W_qkv"]))
    g["w7_out"] = np.ascontiguousarray(np.asarray(inputs["L1_attn_b_W_out"]))
    g["p_b7qkv"] = pcol(inputs["L1_attn_b_b_qkv"])
    g["p_b7v_row"] = np.ascontiguousarray(np.asarray(inputs["L1_attn_b_b_qkv"], np.float32)[2048:].reshape(1, 1024))
    g["p_b7out"] = pcol(inputs["L1_attn_b_b_out"])
    x = np.asarray(inputs["L0_attn_a_x"], np.float32)
    xs = []
    for c in range(NCORES):
        sl = x[c * BL:(c + 1) * BL]
        fm = np.ascontiguousarray(sl.transpose(0, 2, 1).reshape(BL, NFC, 128, T))
        xs.append(fm)
    return g, xs


def declare_dram(nc, g, names, x_shape=None, want_y=True):
    import ml_dtypes
    dram = {}
    DT = {np.dtype(np.float32): F32, np.dtype(ml_dtypes.bfloat16): BF16}
    for n in names:
        a = g[n]
        dram[n] = nc.dram_tensor(n, list(a.shape), DT[a.dtype], kind="ExternalInput").ap()
    if x_shape is not None:
        dram["x"] = nc.dram_tensor("x", list(x_shape), F32, kind="ExternalInput").ap()
    if want_y:
        dram["y"] = nc.dram_tensor("y", [BL, NFC, 128, T], F32, kind="ExternalOutput").ap()
    return dram


CONST_NAMES = ["identf", "identb", "ones_col_bf", "ones_row_f", "iota_col", "iota_row",
               "mask01", "ind_d", "ind_g", "rt_bf"]


def load_consts(env, sch, names=("ones_col_bf", "ones_row_f")):
    nc = env.nc
    ld = env.sems["ldh"]
    shapes = {"identf": ([128, 128], F32), "identb": ([128, 128], BF16),
              "ones_col_bf": ([128, 1], BF16), "ones_row_f": ([1, 128], F32),
              "iota_col": ([128, 1], F32), "iota_row": ([1, T], F32),
              "iota2d": ([128, 16], F32),
              "mask01": ([128, 128], F32), "ind_d": ([32, 512], F32),
              "ind_g": ([16, 512], F32), "rt_bf": ([128, 128], BF16)}
    for n in names:
        shp, dt = shapes[n]
        tile = env.ctx.enter_context(nc.sbuf_tensor("c_" + n, shp, dt))
        env.consts[n] = tile[tuple(slice(0, s) for s in shp)]
        ld.bump(16)
        sch.on("sync", lambda e, d=env.consts[n], s=env.dram[n]: e.dma_start(out=d, in_=s).then_inc(ld.sem, 16))
    ev = env.sems["ev"]
    evt = ev.bump()
    def _pool(e):
        last = None
        for v, i in env.cvals.items():
            last = e.memset(env.cpool[:, i:i + 1], v)
        last.then_inc(ev.sem, 1)
    sch.on("vector", _pool)
    t = ld.n
    for e in ENGINES:
        sch.on(e, lambda eng, t=t, evt=evt: (eng.wait_ge(ld.sem, t), eng.wait_ge(ev.sem, evt)))



PHASE_WEIGHTS = {
    "p2": ["w2_rT", "w2_kT", "w2_vT", "p_mur", "p_muk"],
    "p6": ["w6_G1_R", "w6_G2_R", "w6_G1_I", "w6_G2_I", "p_b6"],
    "p1": ["w1_inT", "w1_siT", "w1_soT", "w1_outT", "p_dparam"],
    "p7": ["w7_qkv", "w7_out", "p_b7qkv", "p_b7out", "p_b7v_row"],
    "p3": ["w3_qkv", "w3_g", "w3_proj", "p_decay", "rot_cq", "rot_sq", "rot_ck", "rot_sk"],
    "p5": ["w5_aT", "w5_iT", "w5_vT", "w5_oT", "w5_gu_k", "w5_dn_k",
           "p_prew", "p_ffnw", "p_convw", "p_convb", "p_dbias"],
}
PHASE_CONSTS = {
    "p2": ["ones_col_bf", "ones_row_f"],
    "p6": ["ones_col_bf", "ones_row_f", "identf", "ind_d", "ind_g"],
    "p1": ["ones_col_bf", "ones_row_f"],
    "p7": ["ones_col_bf", "ones_row_f", "identb"],
    "p3": ["ones_col_bf", "ones_row_f", "iota_col", "iota_row", "iota2d", "mask01", "rt_bf"],
    "p5": ["ones_col_bf", "ones_row_f"],
}
PHASE_NEEDS_WBIG = {"p6", "p8"}
PHASE_EMIT = {}


def build_phase_program(phase):
    nc = bass.Bass("TRN2", target_bir_lowering=False, detect_race_conditions=False)
    ctx = ExitStack()
    env = Env(nc, ctx)
    with ctx:
        env.dram = declare_dram(nc, _HOST_G, PHASE_CONSTS[phase] + PHASE_WEIGHTS[phase],
                                x_shape=[BL, NFC, 128, T])
        sch = Sched()
        load_consts(env, sch, PHASE_CONSTS[phase])
        sch.flush(nc)
        if phase in PHASE_NEEDS_WBIG:
            emit_p0_wbig(env, sch)
        xsrc = [env.dram["x"][s] for s in range(BL)]
        xdst = [env.dram["y"][s] for s in range(BL)]
        PHASE_EMIT[phase](env, sch, xsrc, xdst)
    return nc


PHASE_EMIT["p2"] = emit_p_l0ffn
PHASE_EMIT["p6"] = emit_p_l1ffn
PHASE_EMIT["p1"] = emit_p_l0attna
PHASE_EMIT["p7"] = emit_p_l1attnb
PHASE_EMIT["p3"] = emit_p_l0attnb
PHASE_EMIT["p5"] = emit_p_l1attna

def build_fused_program():
    nc = bass.Bass("TRN2", target_bir_lowering=False, detect_race_conditions=False)
    ctx = ExitStack()
    env = Env(nc, ctx)
    seq = PHASE_SEQ
    cnames, wnames = [], []
    for ph in seq:
        for n in PHASE_CONSTS[ph] + (["identf", "ind_d", "ind_g"] if ph == "p6" else []):
            if n not in cnames:
                cnames.append(n)
        for n in PHASE_WEIGHTS[ph]:
            if n not in wnames:
                wnames.append(n)
    with ctx:
        env.dram = declare_dram(nc, _HOST_G, cnames + wnames, x_shape=[BL, NFC, 128, T])
        x_res = nc.dram_tensor("x_res", [BL, NFC, 128, T], F32, kind="Internal").ap()
        sch = Sched()
        load_consts(env, sch, cnames)
        sch.flush(nc)
        emit_p0_wbig(env, sch)
        for pi, ph in enumerate(seq):
            src_t = env.dram["x"] if pi == 0 else x_res
            dst_t = env.dram["y"] if pi == len(seq) - 1 else x_res
            xsrc = [src_t[s] for s in range(BL)]
            xdst = [dst_t[s] for s in range(BL)]
            PHASE_EMIT[ph](env, sch, xsrc, xdst)
    return nc, cnames + wnames


_HOST_G = None

PHASE_SEQ = ["p1", "p2", "p3", "p2", "p5", "p6", "p7", "p6"]


def kernel(**inputs):
    global _HOST_G
    g, xs = host_prep(inputs)
    _HOST_G = g
    import os
    ph_env = os.environ.get("K_PHASE")
    if ph_env or os.environ.get("K_UNFUSED"):
        seq = [ph_env] if ph_env else PHASE_SEQ
        progs = {}
        for ph in set(seq):
            progs[ph] = build_phase_program(ph)
        cur = xs
        for ph in seq:
            names = PHASE_CONSTS[ph] + PHASE_WEIGHTS[ph]
            in_maps = [{**{n: g[n] for n in names}, "x": cur[c]} for c in range(NCORES)]
            res = run_bass_kernel_spmd(progs[ph], in_maps, core_ids=list(range(NCORES)))
            cur = [np.ascontiguousarray(res.results[c]["y"]) for c in range(NCORES)]
        out = np.empty((B, T, D), np.float32)
        for c in range(NCORES):
            fm = cur[c].reshape(BL, NFC * 128, T)
            out[c * BL:(c + 1) * BL] = fm.transpose(0, 2, 1)
        return out
    nc, names = build_fused_program()
    in_maps = [{**{n: g[n] for n in names}, "x": xs[c]} for c in range(NCORES)]
    res = run_bass_kernel_spmd(nc, in_maps, core_ids=list(range(NCORES)))
    out = np.empty((B, T, D), np.float32)
    for c in range(NCORES):
        fm = res.results[c]["y"].reshape(BL, NFC * 128, T)
        out[c * BL:(c + 1) * BL] = fm.transpose(0, 2, 1)
    return out


# version 25
# speedup vs baseline: 1.0717x; 1.0122x over previous
import math
import numpy as np
from contextlib import ExitStack

import concourse.bass as bass
import concourse.mybir as mybir
from concourse.bass_utils import run_bass_kernel_spmd

F32 = mybir.dt.float32
BF16 = mybir.dt.bfloat16
AF = mybir.ActivationFunctionType
ALU = mybir.AluOpType

NCORES = 8
B = 16
T = 2048
D = 1024
BL = B // NCORES
NFC = D // 128
TT = 512
NTILE = T // TT
ENGINES = ("sync", "tensor", "vector", "scalar", "gpsimd")

LN_EPS = np.float32(1e-5)
RMS_EPS = np.float32(1e-6)
GN_EPS = np.float32(1e-5)


class Ctr:

    def __init__(self, sem):
        self.sem = sem
        self.n = 0

    def bump(self, k=1):
        self.n += k
        return self.n


class Sched:

    def __init__(self):
        self.q = {e: [] for e in ENGINES}

    def on(self, e, fn):
        self.q[e].append(fn)

    def flush(self, nc):
        with nc.Block() as blk:
            for e in ENGINES:
                fns = self.q[e]

                def runner(eng, fns=fns):
                    for fn in fns:
                        fn(eng)

                getattr(blk, e)(runner)
        self.q = {e: [] for e in ENGINES}


FCONST_VALUES = [float(np.float32(v)) for v in
                 (1e-5, 1e-6, -3.5 * math.log(2.0), 1.0, 0.0, 1e-10, 1e-8, 0.5, -1.0)]


class Env:
    def __init__(self, nc, ctx):
        self.nc = nc
        self.ctx = ctx
        self.sems = {}
        for name in ("ldh", "s0", "s1", "g0", "g1", "g2", "g3",
                     "st0", "st1", "mm", "ev", "ac", "gp", "bar",
                     "mm2", "ev2", "ac2"):
            self.sems[name] = Ctr(ctx.enter_context(nc.semaphore(name)))
        self.phase_idx = 0
        self.consts = {}
        self.dram = {}
        self.cpool = ctx.enter_context(nc.sbuf_tensor("cpool", [128, len(FCONST_VALUES)], F32))
        self.rows_sb = ctx.enter_context(nc.sbuf_tensor("ln_rows", [1, 6 * TT], F32))
        self.cvals = {v: i for i, v in enumerate(FCONST_VALUES)}

    def fconst(self, v):
        v = float(np.float32(v))
        i = self.cvals[v]
        return self.cpool[:, i:i + 1]

    def _uniq(self, name):
        if not hasattr(self, "_names"):
            self._names = {}
        k = self._names.get(name, 0)
        self._names[name] = k + 1
        return name if k == 0 else f"{name}_v{k}"

    def sb(self, stack, name, shape, dtype):
        return stack.enter_context(self.nc.sbuf_tensor(self._uniq(name), shape, dtype))

    def ps(self, stack, name, shape, dtype=F32):
        return stack.enter_context(self.nc.psum_tensor(self._uniq(name), shape, dtype))


def emit_barrier(env, sch):
    env.phase_idx += 1
    bar = env.sems["bar"]
    tgt = 5 * env.phase_idx
    bar.n = tgt
    for e in ENGINES:
        def _b(eng, tgt=tgt, sem=bar.sem):
            eng.sem_inc(sem, 1)
            eng.wait_ge(sem, tgt)
        sch.on(e, _b)



def slot_dma(env, sch, eng, slot, out_ap, in_ap):
    c = env.sems[slot]
    pre = c.n
    tgt = c.bump(16)
    def _d(e, pre=pre, out_ap=out_ap, in_ap=in_ap, sem=c.sem):
        e.wait_ge(sem, pre)
        e.dma_start(out=out_ap, in_=in_ap).then_inc(sem, 16)
    sch.on(eng, _d)
    return tgt


def emit_tile_loads(env, sch, xsrc_seq, j, xf, xbf, slot_f, slot_b, lo=None, hi=None):
    lo = 0 if lo is None else lo
    hi = NFC if hi is None else hi
    nfc = hi - lo
    src = xsrc_seq[lo:hi, :, j * TT:(j + 1) * TT].rearrange("c p t -> p c t")
    dstf = xf[:, : nfc * TT].rearrange("p (c t) -> p c t", c=nfc)
    dstb = xbf[:, : nfc * TT].rearrange("p (c t) -> p c t", c=nfc)
    t1 = slot_dma(env, sch, "sync", slot_f, dstf, src)
    t2 = slot_dma(env, sch, "gpsimd", slot_b, dstb, src)
    return t1, t2


def emit_weight_load_T(env, sch, wsrc, wsb, dout, nkc=NFC, slots=("g2", "g3")):
    tgts = {}
    for kc in range(nkc):
        sl = slots[kc % len(slots)]
        tgts[sl] = slot_dma(env, sch, "gpsimd", sl,
                            wsb[:, kc * dout:(kc + 1) * dout],
                            wsrc[kc * 128:(kc + 1) * 128, :])
    return [(sl, t) for sl, t in tgts.items()]


class LNBufs:

    def __init__(self, env, stack, tag, shift_cols=0, share_ps=None):
        self.shift = shift_cols
        w = NFC * (TT + shift_cols)
        self.xln = env.sb(stack, f"{tag}_xln", [128, w], BF16)
        self.rows = env.rows_sb[:, :]
        if share_ps is None:
            self.sq = env.sb(stack, f"{tag}_sq", [128, NFC * TT], BF16)
            self.save = env.sb(stack, f"{tag}_save", [128, NFC * max(1, shift_cols)], BF16)
            self.psA = env.ps(stack, f"{tag}_psA", [1, TT])
            self.psB = env.ps(stack, f"{tag}_psB", [1, TT])
            self.psM = env.ps(stack, f"{tag}_psM", [128, TT])
            self.psR = env.ps(stack, f"{tag}_psR", [128, TT])
        else:
            self.sq = share_ps.sq
            self.save = share_ps.save
            self.psA, self.psB = share_ps.psA, share_ps.psB
            self.psM, self.psR = share_ps.psM, share_ps.psR

    def cur(self, c):
        s = TT + self.shift
        return self.xln[:, c * s + self.shift:(c + 1) * s]

    def prev(self, c, k=1):
        s = TT + self.shift
        return self.xln[:, c * s + self.shift - k: (c + 1) * s - k]


def emit_ln(env, sch, ln, j, xf, xbf, f_wait, b_wait, wait_extra=None, norm_src="f32",
            sq_guard=None, rows_guard=None, bc_guard=None, xln_guard=None,
            cap=None, sems=("mm", "ev", "ac")):
    nc = env.nc
    mm, ev, ac = (env.sems[k] for k in sems)
    if cap is None:
        put = lambda sec, eng, f: sch.on(eng, f)
    else:
        put = lambda sec, eng, f: cap.setdefault(sec, []).append((eng, f))
    fsem, ftgt = f_wait
    bsem, btgt = b_wait
    ones_bf = env.consts["ones_col_bf"]
    ones_row = env.consts["ones_row_f"]

    if ln.shift:
        k = ln.shift
        if j == 0:
            def _zz(e, k=k):
                for c in range(NFC):
                    s = TT + k
                    e.memset(ln.xln[:, c * s: c * s + k], 0.0)
            put("shift", "vector", _zz)
        else:
            def _cp(e, k=k):
                for c in range(NFC):
                    s = TT + k
                    e.tensor_copy(out=ln.xln[:, c * s: c * s + k], in_=ln.save[:, c * k:(c + 1) * k])
            put("shift", "vector", _cp)

    put("sq", "scalar", lambda e: e.wait_ge(env.sems[bsem].sem, btgt))
    if sq_guard is not None:
        put("sq", "scalar", lambda e, g=sq_guard: e.wait_ge(g[0].sem, g[1]))
    if wait_extra is not None:
        put("sq", "scalar", lambda e: e.wait_ge(wait_extra[0].sem, wait_extra[1]))
    ac_sq = ac.bump()
    put("sq", "scalar", lambda e: e.activation(out=ln.sq[:], in_=xbf[:, :NFC * TT], func=AF.Square).then_inc(ac.sem, 1))

    mm_stA = mm.bump()
    mm_st = mm.bump()
    def _stats(e, ac_sq=ac_sq):
        e.wait_ge(env.sems[bsem].sem, btgt)
        for kc in range(NFC):
            i = e.matmul(out=ln.psA[0:1, :], lhsT=ones_bf[:, 0:1], rhs=xbf[:, kc * TT:(kc + 1) * TT],
                         start=(kc == 0), stop=(kc == NFC - 1))
            if kc == NFC - 1:
                i.then_inc(mm.sem, 1)
        e.wait_ge(ac.sem, ac_sq)
        for kc in range(NFC):
            i = e.matmul(out=ln.psB[0:1, :], lhsT=ones_bf[:, 0:1], rhs=ln.sq[:, kc * TT:(kc + 1) * TT],
                         start=(kc == 0), stop=(kc == NFC - 1))
            if kc == NFC - 1:
                i.then_inc(mm.sem, 1)
    put("stats", "tensor", _stats)

    R = ln.rows
    mu, m2, var, sd, nm, rs = (R[0:1, i * TT:(i + 1) * TT] for i in range(6))
    ev_rows = ev.bump()
    def _rows(e, mm_st=mm_st, rg=rows_guard):
        e.wait_ge(mm.sem, mm_st)
        if rg is not None:
            for gg in rg:
                e.wait_ge(gg[0].sem, gg[1])
        e.tensor_scalar_mul(out=mu, in0=ln.psA[0:1, :], scalar1=1.0 / 1024.0)
        e.tensor_scalar_mul(out=m2, in0=ln.psB[0:1, :], scalar1=1.0 / 1024.0)
        e.tensor_mul(out=var, in0=mu, in1=mu)
        e.tensor_sub(out=var, in0=m2, in1=var)
        e.tensor_scalar_mul(out=nm, in0=mu, scalar1=-1.0).then_inc(ev.sem, 1)
    put("rows", "vector", _rows)
    ac_rs = ac.bump(2)
    def _sd(e, ev_rows=ev_rows):
        e.wait_ge(ev.sem, ev_rows)
        e.activation(out=sd, in_=var, func=AF.Ln, bias=env.fconst(1e-5)[0:1, :]).then_inc(ac.sem, 1)
        e.activation(out=rs, in_=sd, func=AF.Exp, scale=-0.5).then_inc(ac.sem, 1)
    put("sd", "scalar", _sd)

    mm_bcM = mm.bump()
    mm_bc = mm.bump()
    def _bc(e, ac_rs=ac_rs, bg=bc_guard):
        e.wait_ge(ac.sem, ac_rs)
        if bg is not None:
            e.wait_ge(bg[0].sem, bg[1])
        e.matmul(out=ln.psM[:], lhsT=ones_row[0:1, :], rhs=nm, start=True, stop=True).then_inc(mm.sem, 1)
        e.matmul(out=ln.psR[:], lhsT=ones_row[0:1, :], rhs=rs, start=True, stop=True).then_inc(mm.sem, 1)
    put("bc", "tensor", _bc)

    ev_ln = ev.bump()
    nsrc = xf if norm_src == "f32" else xbf
    def _norm(e, mm_bc=mm_bc, xg=xln_guard):
        e.wait_ge(mm.sem, mm_bc)
        if xg is not None:
            e.wait_ge(xg[0].sem, xg[1])
        if norm_src == "f32":
            e.wait_ge(env.sems[fsem].sem, ftgt)
        last = None
        for c in range(NFC):
            e.tensor_add(out=ln.cur(c), in0=nsrc[:, c * TT:(c + 1) * TT], in1=ln.psM[:])
            last = e.tensor_mul(out=ln.cur(c), in0=ln.cur(c), in1=ln.psR[:])
        if ln.shift:
            k = ln.shift
            for c in range(NFC):
                last = e.tensor_copy(out=ln.save[:, c * k:(c + 1) * k],
                                     in_=ln.cur(c)[:, TT - k:TT])
        last.then_inc(ev.sem, 1)
    put("norm", "vector", _norm)
    return ev_ln, {"mm_st": mm_st, "ac_rs": ac_rs, "mm_bc": mm_bc}



def emit_p_l0ffn(env, sch, xsrc, xdst):
    nc = env.nc
    ldh, mm, ev, ac = (env.sems[k] for k in ("ldh", "mm", "ev", "ac"))
    st0, st1 = env.sems["st0"], env.sems["st1"]
    with ExitStack() as stk:
        xf = [env.sb(stk, f"p2_xf{i}", [128, NFC * TT], F32) for i in range(2)]
        xbf = [env.sb(stk, f"p2_xbf{i}", [128, NFC * TT], BF16) for i in range(2)]
        xout = env.sb(stk, "p2_xout", [128, NFC * TT], F32)
        ln = LNBufs(env, stk, "p2_ln", shift_cols=1)
        wr = env.sb(stk, "p2_wr", [128, NFC * D], BF16)
        wk = env.sb(stk, "p2_wk", [128, NFC * D], BF16)
        wv = env.sb(stk, "p2_wv", [128, NFC * D], BF16)
        mur = env.sb(stk, "p2_mur", [128, NFC], F32)
        muk = env.sb(stk, "p2_muk", [128, NFC], F32)
        omr = env.sb(stk, "p2_omr", [128, NFC], F32)
        omk = env.sb(stk, "p2_omk", [128, NFC], F32)
        xr = [env.sb(stk, f"p2_xr{i}", [128, NFC * TT], BF16) for i in range(2)]
        xk = [env.sb(stk, f"p2_xk{i}", [128, NFC * TT], BF16) for i in range(2)]
        kact = ln.sq
        kf = env.sb(stk, "p2_kf", [128, TT], F32)
        psm = [env.ps(stk, f"p2_ps{i}", [128, TT]) for i in range(4)]

        wt_w = {}
        for wname, wdst in (("w2_rT", wr), ("w2_kT", wk), ("w2_vT", wv)):
            for sl, t in emit_weight_load_T(env, sch, env.dram[wname], wdst, D):
                wt_w[sl] = max(wt_w.get(sl, 0), t)
        ldh.bump(16)
        sch.on("sync", lambda e: e.dma_start(out=mur[:], in_=env.dram["p_mur"]).then_inc(ldh.sem, 16))
        wt_mu = ldh.bump(16)
        sch.on("sync", lambda e: e.dma_start(out=muk[:], in_=env.dram["p_muk"]).then_inc(ldh.sem, 16))
        def wait_weights(e):
            for sl, t in wt_w.items():
                e.wait_ge(env.sems[sl].sem, t)
        ev_om = ev.bump()
        def _om(e, wt_mu=wt_mu):
            e.wait_ge(ldh.sem, wt_mu)
            e.tensor_scalar(out=omr[:], in0=mur[:], scalar1=-1.0, scalar2=1.0, op0=ALU.mult, op1=ALU.add)
            e.tensor_scalar(out=omk[:], in0=muk[:], scalar1=-1.0, scalar2=1.0,
                            op0=ALU.mult, op1=ALU.add).then_inc(ev.sem, 1)
        sch.on("vector", _om)

        NG = BL * NTILE
        evln, lninfo, ev_mix_t, kmm_hist, acr_hist = {}, {}, {}, {}, {}
        evo_hist, st_hist, vmm_hist = {}, {}, {}
        mmc, ev2c, mm2c, ac2c = (env.sems[k] for k in ("mm", "ev2", "mm2", "ac2"))

        def do_prep(g, cap):
            s, j = divmod(g, NTILE)
            bi = g % 2
            if g >= 2 and (g - 2) in evo_hist:
                sch.on("sync", lambda e, t=evo_hist[g - 2]: e.wait_ge(ev.sem, t))
                sch.on("gpsimd", lambda e, t=evo_hist[g - 2]: e.wait_ge(ev.sem, t))
            sf, sb = ("s0", "g0") if bi == 0 else ("s1", "g1")
            ld_t, ldg_t = emit_tile_loads(env, sch, xsrc[s], j, xf[bi], xbf[bi], sf, sb)
            kw = {"cap": cap, "sems": ("mm2", "ev2", "ac2")}
            if g >= 2 and (g - 2) in vmm_hist:
                kw["wait_extra"] = (mmc, vmm_hist[g - 2])
            if g >= 1:
                kw["sq_guard"] = (mm2c, lninfo[g - 1]["mm_st"])
                kw["rows_guard"] = [(mm2c, lninfo[g - 1]["mm_bc"]), (ac2c, lninfo[g - 1]["ac_rs"])]
                kw["bc_guard"] = (ev2c, evln[g - 1])
            evln[g], lninfo[g] = emit_ln(env, sch, ln, j, xf[bi], xbf[bi], (sf, ld_t), (sb, ldg_t), **kw)
            mt = ev2c.bump()
            def _mix(e, g=g, bi=bi, ev_ln=evln[g]):
                e.wait_ge(ev2c.sem, ev_ln)
                e.wait_ge(ev.sem, ev_om)
                if (g - 2) in kmm_hist:
                    e.wait_ge(mm.sem, kmm_hist[g - 2])
                last = None
                for c in range(NFC):
                    cc = slice(c * TT, (c + 1) * TT)
                    e.tensor_scalar_mul(out=xr[bi][:, cc], in0=ln.prev(c), scalar1=omr[:, c:c + 1])
                    e.scalar_tensor_tensor(out=xr[bi][:, cc], in0=ln.cur(c), scalar=mur[:, c:c + 1],
                                           in1=xr[bi][:, cc], op0=ALU.mult, op1=ALU.add)
                    e.tensor_scalar_mul(out=xk[bi][:, cc], in0=ln.prev(c), scalar1=omk[:, c:c + 1])
                    last = e.scalar_tensor_tensor(out=xk[bi][:, cc], in0=ln.cur(c), scalar=muk[:, c:c + 1],
                                                  in1=xk[bi][:, cc], op0=ALU.mult, op1=ALU.add)
                last.then_inc(ev2c.sem, 1)
            if cap is None:
                sch.on("vector", _mix)
            else:
                cap.setdefault("mix", []).append(("vector", _mix))
            ev_mix_t[g] = mt

        do_prep(0, None)
        for g in range(NG):
            s, j = divmod(g, NTILE)
            bi = g % 2
            cap = {} if g + 1 < NG else None
            if cap is not None:
                do_prep(g + 1, cap)
            def play(sec, cap=cap):
                if cap is not None:
                    for e_, f_ in cap.get(sec, []):
                        sch.on(e_, f_)
            play("shift")
            play("sq")
            play("rows")

            def _mm_r0(e, t=ev_mix_t[g]):
                e.wait_ge(ev2c.sem, t)
                wait_weights(e)
            sch.on("tensor", _mm_r0)
            ac_r = []
            for m in range(NFC):
                pb = psm[m % 2]
                if m < 2:
                    if (g - 1) in acr_hist:
                        sch.on("tensor", lambda e, t=acr_hist[g - 1]: e.wait_ge(ac.sem, t))
                else:
                    sch.on("tensor", lambda e, t=ac_r[m - 2]: e.wait_ge(ac.sem, t))
                mmt = mm.bump()
                def _mmr(e, m=m, pb=pb, bi=bi):
                    for kc in range(NFC):
                        i = e.matmul(out=pb[:], lhsT=wr[:, kc * D + m * 128: kc * D + (m + 1) * 128],
                                     rhs=xr[bi][:, kc * TT:(kc + 1) * TT], start=(kc == 0), stop=(kc == NFC - 1))
                    i.then_inc(mm.sem, 1)
                sch.on("tensor", _mmr)
                at = ac.bump()
                def _ev(e, mmt=mmt, m=m, pb=pb, bi=bi):
                    e.wait_ge(mm.sem, mmt)
                    e.activation(out=xbf[bi][:, m * TT:(m + 1) * TT], in_=pb[:], func=AF.Sigmoid).then_inc(ac.sem, 1)
                sch.on("scalar", _ev)
                ac_r.append(at)
            acr_hist[g] = ac_r[-1]
            play("stats")

            ev_k = []
            for m in range(NFC):
                pb = psm[2 + m % 2]
                if m < 2:
                    if (g - 1) in evo_hist:
                        sch.on("tensor", lambda e, t=evo_hist[g - 1] - 1 + m: e.wait_ge(ev.sem, t))
                else:
                    sch.on("tensor", lambda e, t=ev_k[m - 2]: e.wait_ge(ev.sem, t))
                mmt = mm.bump()
                def _mmk(e, m=m, pb=pb, bi=bi):
                    for kc in range(NFC):
                        i = e.matmul(out=pb[:], lhsT=wk[:, kc * D + m * 128: kc * D + (m + 1) * 128],
                                     rhs=xk[bi][:, kc * TT:(kc + 1) * TT], start=(kc == 0), stop=(kc == NFC - 1))
                    i.then_inc(mm.sem, 1)
                sch.on("tensor", _mmk)
                et = ev.bump()
                def _evk(e, mmt=mmt, m=m, pb=pb):
                    e.wait_ge(mm.sem, mmt)
                    cc = slice(m * TT, (m + 1) * TT)
                    e.tensor_scalar_max(out=kf[:], in0=pb[:], scalar1=0.0)
                    e.tensor_mul(out=kact[:, cc], in0=kf[:], in1=kf[:]).then_inc(ev.sem, 1)
                sch.on("vector", _evk)
                ev_k.append(et)
            kmm_hist[g] = mm.n
            play("sd")
            play("bc")
            play("norm")

            def _mm_v(e, t=ev_k[-1]):
                e.wait_ge(ev.sem, t)
            sch.on("tensor", _mm_v)
            ev_o = []
            for m in range(NFC):
                pb = psm[2 + m % 2]
                if m < 2:
                    sch.on("tensor", lambda e, t=ac_r[-1]: e.wait_ge(ac.sem, t))
                else:
                    sch.on("tensor", lambda e, t=ev_o[m - 2]: e.wait_ge(ev.sem, t))
                mmt = mm.bump()
                def _mmv(e, m=m, pb=pb):
                    for kc in range(NFC):
                        i = e.matmul(out=pb[:], lhsT=wv[:, kc * D + m * 128: kc * D + (m + 1) * 128],
                                     rhs=kact[:, kc * TT:(kc + 1) * TT], start=(kc == 0), stop=(kc == NFC - 1))
                    i.then_inc(mm.sem, 1)
                sch.on("tensor", _mmv)
                if m == 0 and (g - 1) in st_hist:
                    sl, t = st_hist[g - 1]
                    sch.on("vector", lambda e, sl=sl, t=t: e.wait_ge(env.sems[sl].sem, t))
                et = ev.bump()
                def _evo(e, mmt=mmt, m=m, pb=pb, bi=bi):
                    e.wait_ge(mm.sem, mmt)
                    cc = slice(m * TT, (m + 1) * TT)
                    e.tensor_mul(out=xout[:, cc], in0=xbf[bi][:, cc], in1=pb[:])
                    e.tensor_add(out=xout[:, cc], in0=xout[:, cc],
                                 in1=xf[bi][:, cc]).then_inc(ev.sem, 1)
                sch.on("vector", _evo)
                ev_o.append(et)
                if m == 1:
                    play("mix")
            vmm_hist[g] = mm.n

            ssl = "st0" if bi == 0 else "st1"
            sch.on("gpsimd", lambda e, t=ev_o[-1]: e.wait_ge(ev.sem, t))
            dst = xdst[s][:, :, j * TT:(j + 1) * TT].rearrange("c p t -> p c t")
            st_t = slot_dma(env, sch, "gpsimd", ssl, dst,
                            xout.rearrange("p (c t) -> p c t", c=NFC))
            evo_hist[g] = ev_o[-1]
            st_hist[g] = (ssl, st_t)

        def _drain(e):
            e.wait_ge(env.sems["st0"].sem, env.sems["st0"].n)
            e.wait_ge(env.sems["st1"].sem, env.sems["st1"].n)
        sch.on("gpsimd", _drain)
        emit_barrier(env, sch)
        sch.flush(nc)





def emit_p0_wbig(env, sch):
    nc = env.nc
    ldh, mm, ev = env.sems["ldh"], env.sems["mm"], env.sems["ev"]
    wbig = env.ctx.enter_context(nc.sbuf_tensor("wbig", [128, NFC * D], BF16))
    env.consts["wbig"] = wbig[:, :]
    identf = env.consts["identf"]
    indd = env.consts["ind_d"]
    indg = env.consts["ind_g"]
    with ExitStack() as stk:
        g1t = {m: env.sb(stk, f"p0_g1{m}", [32, 128], F32) for m in "RI"}
        g2t = {m: env.sb(stk, f"p0_g2{m}", [64, 16], F32) for m in "RI"}
        g1T = {m: env.sb(stk, f"p0_g1T{m}", [32, 128], F32) for m in "RI"}
        g2T = {m: env.sb(stk, f"p0_g2T{m}", [16, 64], F32) for m in "RI"}
        Usb = {m: env.sb(stk, f"p0_U{m}", [128, 256], F32) for m in "RI"}
        Vsb = {m: env.sb(stk, f"p0_V{m}", [128, 512], F32) for m in "RI"}
        wtmp = env.sb(stk, "p0_wtmp", [128, 512], F32)
        ps1 = env.ps(stk, "p0_ps1", [128, 512])
        ps2 = env.ps(stk, "p0_ps2", [128, 512])

        for m, nm1, nm2 in (("R", "w6_G1_R", "w6_G2_R"), ("I", "w6_G1_I", "w6_G2_I")):
            ldh.bump(16)
            sch.on("sync", lambda e, d=g1t[m][:32, :], s=env.dram[nm1][0].rearrange("b c d -> b (c d)"):
                   e.dma_start(out=d, in_=s).then_inc(ldh.sem, 16))
            ldh.bump(16)
            sch.on("sync", lambda e, d=g2t[m][:64, :], s=env.dram[nm2].rearrange("c e f g -> (c e f) g"):
                   e.dma_start(out=d, in_=s).then_inc(ldh.sem, 16))
        ld_all = ldh.n

        prev_evuv = None
        for m in "RI":
            mm_t = mm.bump()
            def _tr(e, m=m, ld_all=ld_all, prev=prev_evuv):
                e.wait_ge(ldh.sem, ld_all)
                if prev is not None:
                    e.wait_ge(ev.sem, prev)
                for c in range(4):
                    e.transpose(out=ps1[:32, c * 32:(c + 1) * 32], in_=g1t[m][:32, c * 32:(c + 1) * 32],
                                identity=identf[:32, :32])
                e.transpose(out=ps2[:16, 0:64], in_=g2t[m][:64, :],
                            identity=identf[:64, :64]).then_inc(mm.sem, 1)
            sch.on("tensor", _tr)
            ev_t = ev.bump()
            def _ev(e, m=m, mm_t=mm_t):
                e.wait_ge(mm.sem, mm_t)
                e.tensor_copy(out=g1T[m][:32, :], in_=ps1[:32, 0:128])
                e.tensor_copy(out=g2T[m][:16, :], in_=ps2[:16, 0:64]).then_inc(ev.sem, 1)
            sch.on("vector", _ev)
            mm_uv = mm.bump()
            def _uv(e, m=m, ev_t=ev_t):
                e.wait_ge(ev.sem, ev_t)
                for c in range(4):
                    for r in range(4):
                        e.matmul(out=ps1[:, c * 64 + r * 16: c * 64 + (r + 1) * 16],
                                 lhsT=indg[:16, r * 128:(r + 1) * 128],
                                 rhs=g2T[m][:16, c * 16:(c + 1) * 16], start=True, stop=True)
                        i = e.matmul(out=ps2[:, c * 128 + r * 32: c * 128 + (r + 1) * 32],
                                     lhsT=indd[:32, r * 128:(r + 1) * 128],
                                     rhs=g1T[m][:32, c * 32:(c + 1) * 32], start=True, stop=True)
                i.then_inc(mm.sem, 1)
            sch.on("tensor", _uv)
            ev_uv = ev.bump()
            def _evuv(e, m=m, mm_uv=mm_uv):
                e.wait_ge(mm.sem, mm_uv)
                e.tensor_copy(out=Usb[m][:], in_=ps1[:, 0:256])
                e.tensor_copy(out=Vsb[m][:], in_=ps2[:, 0:512]).then_inc(ev.sem, 1)
            sch.on("vector", _evuv)
            prev_evuv = ev_uv

        ev_done = ev.bump()
        def _build(e):
            for m in "RI":
                for r in range(4):
                    for b in range(32):
                        dst = wtmp[:, b * 16:(b + 1) * 16]
                        e.tensor_scalar_mul(out=dst, in0=Usb[m][:, 0 + r * 16: 0 + (r + 1) * 16],
                                            scalar1=Vsb[m][:, 0 + r * 32 + b: 0 + r * 32 + b + 1])
                        for c in range(1, 4):
                            e.scalar_tensor_tensor(out=dst, in0=Usb[m][:, c * 64 + r * 16: c * 64 + (r + 1) * 16],
                                                   scalar=Vsb[m][:, c * 128 + r * 32 + b: c * 128 + r * 32 + b + 1],
                                                   in1=dst, op0=ALU.mult, op1=ALU.add)
                    if m == "R":
                        e.tensor_copy(out=wbig[:, r * D + 0:r * D + 512], in_=wtmp[:])
                        e.tensor_copy(out=wbig[:, (r + 4) * D + 512:(r + 4) * D + 1024], in_=wtmp[:])
                    else:
                        e.tensor_copy(out=wbig[:, r * D + 512:r * D + 1024], in_=wtmp[:])
                        e.tensor_scalar_mul(out=wbig[:, (r + 4) * D + 0:(r + 4) * D + 512],
                                            in0=wtmp[:], scalar1=-1.0)
            e.tensor_copy(out=wtmp[:, 0:16], in_=wtmp[:, 0:16]).then_inc(ev.sem, 1)
        sch.on("vector", _build)
        emit_barrier(env, sch)
        sch.flush(nc)
    return ev_done



def emit_p_l1ffn(env, sch, xsrc, xdst):
    nc = env.nc
    ldh, mm, ev, ac = (env.sems[k] for k in ("ldh", "mm", "ev", "ac"))
    wbig = env.consts["wbig"]
    with ExitStack() as stk:
        xf = [env.sb(stk, f"p6_xf{i}", [128, NFC * TT], F32) for i in range(2)]
        xbf = [env.sb(stk, f"p6_xbf{i}", [128, NFC * TT], BF16) for i in range(2)]
        xout = [env.sb(stk, f"p6_xout{i}", [128, NFC * TT], F32) for i in range(2)]
        ln2 = [LNBufs(env, stk, "p6_lnA")]
        ln2.append(LNBufs(env, stk, "p6_lnB", share_ps=ln2[0]))
        b6 = env.sb(stk, "p6_b6", [128, NFC], F32)
        psm = [env.ps(stk, f"p6_ps{i}", [128, TT]) for i in range(4)]

        wt_b = ldh.bump(16)
        sch.on("sync", lambda e: e.dma_start(out=b6[:], in_=env.dram["p_b6"]).then_inc(ldh.sem, 16))

        NG = BL * NTILE
        evln, lninfo, lastmm, evo_hist, st_hist = {}, {}, {}, {}, {}
        mmc, ev2c, mm2c, ac2c = (env.sems[k] for k in ("mm", "ev2", "mm2", "ac2"))

        def do_ln(g, cap):
            s, j = divmod(g, NTILE)
            bi = g % 2
            if g >= 2 and (g - 2) in evo_hist:
                sch.on("sync", lambda e, t=evo_hist[g - 2]: e.wait_ge(ev.sem, t))
                sch.on("gpsimd", lambda e, t=evo_hist[g - 2]: e.wait_ge(ev.sem, t))
            sf, sb = ("s0", "g0") if bi == 0 else ("s1", "g1")
            ld_t, ldg_t = emit_tile_loads(env, sch, xsrc[s], j, xf[bi], xbf[bi], sf, sb)
            kw = {"cap": cap, "sems": ("mm2", "ev2", "ac2")}
            if g >= 1:
                kw["sq_guard"] = (mm2c, lninfo[g - 1]["mm_st"])
                kw["rows_guard"] = [(mm2c, lninfo[g - 1]["mm_bc"]), (ac2c, lninfo[g - 1]["ac_rs"])]
                kw["bc_guard"] = (ev2c, evln[g - 1])
            if g >= 2:
                kw["xln_guard"] = (mmc, lastmm[g - 2])
            evln[g], lninfo[g] = emit_ln(env, sch, ln2[bi], j, xf[bi], xbf[bi], (sf, ld_t), (sb, ldg_t), **kw)

        do_ln(0, None)
        for g in range(NG):
            s, j = divmod(g, NTILE)
            bi = g % 2
            ln = ln2[bi]
            cap = {} if g + 1 < NG else None
            if cap is not None:
                do_ln(g + 1, cap)
            def play(sec, cap=cap):
                if cap is not None:
                    for e_, f_ in cap.get(sec, []):
                        sch.on(e_, f_)
            play("shift")
            play("sq")
            sch.on("tensor", lambda e, t=evln[g]: e.wait_ge(ev2c.sem, t))
            ev_o = []
            for m in range(NFC):
                if m == 2:
                    play("stats")
                    play("rows")
                if m == 6:
                    play("sd")
                    play("bc")
                    play("norm")
                pb = psm[2 * bi + m % 2]
                if m < 2:
                    if (g - 1) in st_hist and m == 0:
                        sl, t = st_hist[g - 1]
                        sch.on("vector", lambda e, sl=sl, t=t: e.wait_ge(env.sems[sl].sem, t))
                    if (g - 2) in evo_hist:
                        sch.on("tensor", lambda e, t=evo_hist[g - 2]: e.wait_ge(ev.sem, t))
                else:
                    sch.on("tensor", lambda e, t=ev_o[m - 2]: e.wait_ge(ev.sem, t))
                mmt = mm.bump()
                def _mm(e, m=m, pb=pb, ln=ln):
                    for kc in range(NFC):
                        i = e.matmul(out=pb[:], lhsT=wbig[:, kc * D + m * 128: kc * D + (m + 1) * 128],
                                     rhs=ln.xln[:, kc * TT:(kc + 1) * TT],
                                     start=(kc == 0), stop=(kc == NFC - 1))
                    i.then_inc(mm.sem, 1)
                sch.on("tensor", _mm)
                et = ev.bump()
                def _evo(e, mmt=mmt, m=m, pb=pb, bi=bi, wt_b=wt_b):
                    e.wait_ge(mm.sem, mmt)
                    e.wait_ge(ldh.sem, wt_b)
                    cc = slice(m * TT, (m + 1) * TT)
                    e.scalar_tensor_tensor(out=xout[bi][:, cc], in0=pb[:], scalar=b6[:, m:m + 1],
                                           in1=xf[bi][:, cc], op0=ALU.add, op1=ALU.add).then_inc(ev.sem, 1)
                sch.on("vector", _evo)
                ev_o.append(et)
            lastmm[g] = mm.n
            ssl = "st0" if bi == 0 else "st1"
            sch.on("gpsimd", lambda e, t=ev_o[-1]: e.wait_ge(ev.sem, t))
            dst = xdst[s][:, :, j * TT:(j + 1) * TT].rearrange("c p t -> p c t")
            st_t = slot_dma(env, sch, "gpsimd", ssl, dst,
                            xout[bi].rearrange("p (c t) -> p c t", c=NFC))
            evo_hist[g] = ev_o[-1]
            st_hist[g] = (ssl, st_t)

        def _drain(e):
            e.wait_ge(env.sems["st0"].sem, env.sems["st0"].n)
            e.wait_ge(env.sems["st1"].sem, env.sems["st1"].n)
        sch.on("gpsimd", _drain)
        emit_barrier(env, sch)
        sch.flush(nc)

def emit_p_l0attna(env, sch, xsrc, xdst):
    nc = env.nc
    ldh, mm, ev, ac = (env.sems[k] for k in ("ldh", "mm", "ev", "ac"))
    with ExitStack() as stk:
        xf = env.sb(stk, "p1_xf", [128, NFC * TT], F32)
        xbf = env.sb(stk, "p1_xbf", [128, NFC * TT], BF16)
        xout = env.sb(stk, "p1_xout", [128, NFC * TT], F32)
        ln = LNBufs(env, stk, "p1_ln")
        w1i = env.sb(stk, "p1_w1i", [128, NFC * 256], BF16)
        w1si = env.sb(stk, "p1_w1si", [128, NFC * 512], BF16)
        w1so = env.sb(stk, "p1_w1so", [128, 4 * D], BF16)
        w1o = env.sb(stk, "p1_w1o", [128, 2 * D], BF16)
        dparam = env.sb(stk, "p1_dp", [128, 2], F32)
        usb = env.sb(stk, "p1_usb", [128, 2 * TT], F32)
        tnb = env.sb(stk, "p1_tnb", [128, 2 * TT], F32)
        dgb = env.sb(stk, "p1_dgb", [128, 2 * TT], F32)
        selh = env.sb(stk, "p1_selh", [128, 4 * TT], BF16)
        asb = env.sb(stk, "p1_asb", [128, 2 * T], F32)
        busb = env.sb(stk, "p1_busb", [128, 2 * T], F32)
        csb = env.sb(stk, "p1_csb", [128, 2 * T], F32)
        dsb = env.sb(stk, "p1_dsb", [128, 2 * T], F32)
        hsb = xout
        ysb = busb[:, :].bitcast(BF16)[:, 0:2 * T]
        psm = [env.ps(stk, f"p1_ps{i}", [128, TT]) for i in range(2)]

        wt_w = {}
        for wname, wdst, dout, nkc in (("w1_inT", w1i, 256, NFC), ("w1_siT", w1si, 512, NFC),
                                       ("w1_soT", w1so, D, 4), ("w1_outT", w1o, D, 2)):
            for sl, t in emit_weight_load_T(env, sch, env.dram[wname], wdst, dout, nkc=nkc):
                wt_w[sl] = max(wt_w.get(sl, 0), t)
        wt_dp = ldh.bump(16)
        sch.on("sync", lambda e: e.dma_start(out=dparam[:], in_=env.dram["p_dparam"]).then_inc(ldh.sem, 16))

        def wait_weights(e):
            for sl, t in wt_w.items():
                e.wait_ge(env.sems[sl].sem, t)

        gcnt = 0
        evo_hist = {}
        st_hist = []
        prev_evo_p2 = None
        for s in range(BL):
            prev_vec_read = None
            prev_ln = None
            for j in range(NTILE):
                bi = gcnt % 2
                sb = "g0" if bi == 0 else "g1"
                srcx = xsrc[s][:, :, j * TT:(j + 1) * TT].rearrange("c p t -> p c t")
                if prev_ln is not None:
                    sch.on("gpsimd", lambda e, t=prev_ln: e.wait_ge(env.sems["ev"].sem, t))
                ldg_t = slot_dma(env, sch, "gpsimd", sb, xbf.rearrange("p (c t) -> p c t", c=NFC), srcx)
                ev_ln, _lninfo = emit_ln(env, sch, ln, j, None, xbf, (None, None), (sb, ldg_t), norm_src="bf")
                prev_ln = ev_ln

                grp_done = []
                def ps_guard(idx):
                    if len(grp_done) >= 2:
                        cse, ct = grp_done[-2]
                        sch.on("tensor", lambda e, cse=cse, ct=ct: e.wait_ge(cse.sem, ct))

                sch.on("tensor", lambda e, t=ev_ln: (e.wait_ge(ev.sem, t), wait_weights(e)))
                for m in range(2):
                    pb = psm[m % 2]
                    ps_guard(m)
                    mmt = mm.bump()
                    def _mmu(e, m=m, pb=pb):
                        for kc in range(NFC):
                            i = e.matmul(out=pb[:], lhsT=w1i[:, kc * 256 + m * 128: kc * 256 + (m + 1) * 128],
                                         rhs=ln.xln[:, kc * TT:(kc + 1) * TT], start=(kc == 0), stop=(kc == NFC - 1))
                        i.then_inc(mm.sem, 1)
                    sch.on("tensor", _mmu)
                    et = ev.bump()
                    def _evu(e, mmt=mmt, m=m, pb=pb):
                        e.wait_ge(mm.sem, mmt)
                        e.tensor_copy(out=usb[:, m * TT:(m + 1) * TT], in_=pb[:]).then_inc(ev.sem, 1)
                    sch.on("vector", _evu)
                    grp_done.append((ev, et))
                for m in range(4):
                    pb = psm[m % 2]
                    ps_guard(2 + m)
                    mmt = mm.bump()
                    def _mms(e, m=m, pb=pb):
                        for kc in range(NFC):
                            i = e.matmul(out=pb[:], lhsT=w1si[:, kc * 512 + m * 128: kc * 512 + (m + 1) * 128],
                                         rhs=ln.xln[:, kc * TT:(kc + 1) * TT], start=(kc == 0), stop=(kc == NFC - 1))
                        i.then_inc(mm.sem, 1)
                    sch.on("tensor", _mms)
                    at = ac.bump()
                    def _evs(e, mmt=mmt, m=m, pb=pb):
                        e.wait_ge(mm.sem, mmt)
                        e.activation(out=selh[:, m * TT:(m + 1) * TT], in_=pb[:], func=AF.Silu).then_inc(ac.sem, 1)
                    sch.on("scalar", _evs)
                    grp_done.append((ac, at))
                jcols = slice(j * TT, (j + 1) * TT)
                for m in range(NFC):
                    pb = psm[m % 2]
                    ps_guard(6 + m)
                    mmt = mm.bump()
                    def _mmg(e, m=m, pb=pb, t=grp_done[5][1]):
                        if m == 0:
                            e.wait_ge(ac.sem, t)
                        for kc in range(4):
                            i = e.matmul(out=pb[:], lhsT=w1so[:, kc * D + m * 128: kc * D + (m + 1) * 128],
                                         rhs=selh[:, kc * TT:(kc + 1) * TT], start=(kc == 0), stop=(kc == 3))
                        i.then_inc(mm.sem, 1)
                    sch.on("tensor", _mmg)
                    c = m % 2
                    if m < 2:
                        dst = asb[:, c * T + j * TT: c * T + (j + 1) * TT]
                        fn = AF.Sigmoid
                    elif m < 4:
                        dst = tnb[:, c * TT:(c + 1) * TT]
                        fn = AF.Tanh
                    elif m < 6:
                        dst = csb[:, c * T + j * TT: c * T + (j + 1) * TT]
                        fn = AF.Tanh
                    else:
                        dst = dgb[:, c * TT:(c + 1) * TT]
                        fn = AF.Sigmoid
                    at = ac.bump()
                    def _evg(e, mmt=mmt, dst=dst, fn=fn, pb=pb, m=m, prev=prev_vec_read):
                        e.wait_ge(mm.sem, mmt)
                        if m in (2, 6) and prev is not None:
                            e.wait_ge(ev.sem, prev)
                        e.activation(out=dst, in_=pb[:], func=fn).then_inc(ac.sem, 1)
                    sch.on("scalar", _evg)
                    grp_done.append((ac, at))
                ac_all = grp_done[-1][1]
                et = ev.bump()
                def _vmix(e, ac_all=ac_all, j=j, wt_dp=wt_dp):
                    e.wait_ge(ac.sem, ac_all)
                    e.wait_ge(ldh.sem, wt_dp)
                    last = None
                    for c in range(2):
                        dst = slice(c * T + j * TT, c * T + (j + 1) * TT)
                        cc = slice(c * TT, (c + 1) * TT)
                        e.tensor_mul(out=busb[:, dst], in0=tnb[:, cc], in1=usb[:, cc])
                        e.tensor_mul(out=dsb[:, dst], in0=dgb[:, cc], in1=usb[:, cc])
                        last = e.tensor_scalar_mul(out=dsb[:, dst], in0=dsb[:, dst],
                                                   scalar1=dparam[:, c:c + 1])
                    last.then_inc(ev.sem, 1)
                sch.on("vector", _vmix)
                prev_vec_read = et
                evo_hist[gcnt] = et
                gcnt += 1

            ev_sc = ev.bump()
            ev_y = ev.bump()
            st0n, st1n = env.sems["st0"].n, env.sems["st1"].n
            def _scan(e, ac_last=grp_done[-1][1], st0n=st0n, st1n=st1n, ev_sc=ev_sc):
                e.wait_ge(ac.sem, ac_last)
                e.wait_ge(env.sems["st0"].sem, st0n)
                e.wait_ge(env.sems["st1"].sem, st1n)
                for c in range(2):
                    cs = slice(c * T, (c + 1) * T)
                    i = e.tensor_tensor_scan(out=hsb[:, cs], data0=asb[:, cs], data1=busb[:, cs],
                                             initial=0.0, op0=ALU.mult, op1=ALU.add)
                i.then_inc(ev.sem, 1)
                e.wait_ge(ev.sem, ev_sc)
                last = None
                for c in range(2):
                    cs = slice(c * T, (c + 1) * T)
                    e.tensor_mul(out=hsb[:, cs], in0=csb[:, cs], in1=hsb[:, cs])
                    last = e.tensor_add(out=ysb[:, cs], in0=hsb[:, cs], in1=dsb[:, cs])
                last.then_inc(ev.sem, 1)
            sch.on("vector", _scan)

            for j in range(NTILE):
                bi = gcnt % 2
                if prev_evo_p2 is not None:
                    sch.on("sync", lambda e, t=prev_evo_p2[-1]: e.wait_ge(env.sems["ev"].sem, t))
                sf = "s0" if bi == 0 else "s1"
                srcx = xsrc[s][:, :, j * TT:(j + 1) * TT].rearrange("c p t -> p c t")
                ld_t = slot_dma(env, sch, "sync", sf, xf.rearrange("p (c t) -> p c t", c=NFC), srcx)
                sch.on("tensor", lambda e, t=ev_y: e.wait_ge(ev.sem, t))
                ev_o = []
                for m in range(NFC):
                    pb = psm[m % 2]
                    if m >= 2:
                        sch.on("tensor", lambda e, t=ev_o[m - 2]: e.wait_ge(ev.sem, t))
                    elif prev_evo_p2 is not None:
                        sch.on("tensor", lambda e, t=prev_evo_p2[6 + m]: e.wait_ge(ev.sem, t))
                    mmt = mm.bump()
                    def _mmo(e, m=m, pb=pb, j=j):
                        for kc in range(2):
                            i = e.matmul(out=pb[:], lhsT=w1o[:, kc * D + m * 128: kc * D + (m + 1) * 128],
                                         rhs=ysb[:, kc * T + j * TT: kc * T + (j + 1) * TT],
                                         start=(kc == 0), stop=(kc == 1))
                        i.then_inc(mm.sem, 1)
                    sch.on("tensor", _mmo)
                    if m == 0 and len(st_hist) >= 1:
                        sl, t = st_hist[-1]
                        sch.on("vector", lambda e, sl=sl, t=t: e.wait_ge(env.sems[sl].sem, t))
                    et = ev.bump()
                    def _evo(e, mmt=mmt, m=m, pb=pb, bi=bi, ld_t=ld_t, sf=sf):
                        e.wait_ge(mm.sem, mmt)
                        if m == 0:
                            e.wait_ge(env.sems[sf].sem, ld_t)
                        cc = slice(m * TT, (m + 1) * TT)
                        e.tensor_add(out=xout[:, cc], in0=pb[:], in1=xf[:, cc]).then_inc(ev.sem, 1)
                    sch.on("vector", _evo)
                    ev_o.append(et)
                ssl = "st0" if bi == 0 else "st1"
                sch.on("gpsimd", lambda e, t=ev_o[-1]: e.wait_ge(ev.sem, t))
                dst = xdst[s][:, :, j * TT:(j + 1) * TT].rearrange("c p t -> p c t")
                st_t = slot_dma(env, sch, "gpsimd", ssl, dst,
                                xout.rearrange("p (c t) -> p c t", c=NFC))
                st_hist.append((ssl, st_t))
                evo_hist[gcnt] = ev_o[-1]
                prev_evo_p2 = ev_o
                gcnt += 1

        def _drain(e):
            e.wait_ge(env.sems["st0"].sem, env.sems["st0"].n)
            e.wait_ge(env.sems["st1"].sem, env.sems["st1"].n)
        sch.on("gpsimd", _drain)
        emit_barrier(env, sch)
        sch.flush(nc)




def emit_p_l1attnb(env, sch, xsrc, xdst):
    nc = env.nc
    ldh, mm, ev, ac = (env.sems[k] for k in ("ldh", "mm", "ev", "ac"))
    identb = env.consts["identb"]
    ones_row = env.consts["ones_row_f"]
    q_s = nc.dram_tensor("p7_qs", [NFC, 128, T], BF16, kind="Internal").ap()
    with ExitStack() as stk:
        ksb = env.sb(stk, "p7_k", [128, NFC * T], BF16)
        vtok = env.sb(stk, "p7_v", [128, 16 * D], BF16)
        b7qkv = env.sb(stk, "p7_bqkv", [128, 24], F32)
        b7out = env.sb(stk, "p7_bout", [128, NFC], F32)
        vbias = env.sb(stk, "p7_vbias", [128, D], F32)
        vbrow = env.sb(stk, "p7_vbrow", [1, D], F32)

        ldh.bump(16)
        sch.on("sync", lambda e: e.dma_start(out=b7qkv[:], in_=env.dram["p_b7qkv"]).then_inc(ldh.sem, 16))
        ldh.bump(16)
        sch.on("sync", lambda e: e.dma_start(out=b7out[:], in_=env.dram["p_b7out"]).then_inc(ldh.sem, 16))
        bias_ld = ldh.bump(16)
        sch.on("sync", lambda e: e.dma_start(out=vbrow[0:1, :], in_=env.dram["p_b7v_row"]).then_inc(ldh.sem, 16))

        st_hist = []

        fence_ev = [None]

        def stage_a(s):
            if fence_ev[0] is not None:
                for eng in ("sync", "gpsimd", "tensor"):
                    sch.on(eng, lambda e, t=fence_ev[0]: e.wait_ge(ev.sem, t))
                def _wst(e):
                    e.wait_ge(env.sems["st0"].sem, env.sems["st0"].n)
                    e.wait_ge(env.sems["st1"].sem, env.sems["st1"].n)
                st0n, st1n = env.sems["st0"].n, env.sems["st1"].n
                for eng in ("sync", "gpsimd"):
                    sch.on(eng, lambda e, a=st0n, b=st1n: (e.wait_ge(env.sems["st0"].sem, a),
                                                           e.wait_ge(env.sems["st1"].sem, b)))
            with ExitStack() as stkA:
                xbf = env.sb(stkA, f"p7_xbf{s}", [128, NFC * TT], BF16)
                ln = LNBufs(env, stkA, f"p7_ln{s}")
                w7q = env.sb(stkA, f"p7_wqkv{s}", [128, NFC * 3072], BF16)
                qacc = env.sb(stkA, f"p7_qacc{s}", [128, NFC * TT], BF16)
                psm = [env.ps(stkA, f"p7_psA{i}_{s}", [128, TT]) for i in range(2)]
                prev_qst = None
                wt_q = {}
                for sl, t in emit_weight_load_T(env, sch, env.dram["w7_qkv"], w7q, 3072):
                    wt_q[sl] = max(wt_q.get(sl, 0), t)
                mm_vb = mm.bump(2)
                def _vb(e, fence=fence_ev[0]):
                    e.wait_ge(ldh.sem, bias_ld)
                    if fence is not None:
                        e.wait_ge(ev.sem, fence)
                    e.matmul(out=psm[0][:], lhsT=ones_row[0:1, :], rhs=vbrow[0:1, 0:512],
                             start=True, stop=True).then_inc(mm.sem, 1)
                    e.matmul(out=psm[1][:], lhsT=ones_row[0:1, :], rhs=vbrow[0:1, 512:1024],
                             start=True, stop=True).then_inc(mm.sem, 1)
                sch.on("tensor", _vb)
                ev_vb = ev.bump()
                def _evb(e):
                    e.wait_ge(mm.sem, mm_vb)
                    e.tensor_copy(out=vbias[:, 0:512], in_=psm[0][:])
                    e.tensor_copy(out=vbias[:, 512:1024], in_=psm[1][:]).then_inc(ev.sem, 1)
                sch.on("vector", _evb)
                grp = [(ev, ev_vb)]

                prev_ln = None
                for j in range(NTILE):
                    sb_ = "g0" if j % 2 == 0 else "g1"
                    srcx = xsrc[s][:, :, j * TT:(j + 1) * TT].rearrange("c p t -> p c t")
                    if prev_ln is not None:
                        sch.on("gpsimd", lambda e, t=prev_ln: e.wait_ge(ev.sem, t))
                    ldg_t = slot_dma(env, sch, "gpsimd", sb_, xbf.rearrange("p (c t) -> p c t", c=NFC), srcx)
                    ev_ln, _lninfo = emit_ln(env, sch, ln, j, None, xbf, (None, None), (sb_, ldg_t), norm_src="bf")
                    prev_ln = ev_ln
                    def _wq(e, t=ev_ln):
                        e.wait_ge(ev.sem, t)
                        for sl, tt in wt_q.items():
                            e.wait_ge(env.sems[sl].sem, tt)
                    sch.on("tensor", _wq)
                    for m in range(16):
                        pb = psm[m % 2]
                        if len(grp) >= 2:
                            cse, ct = grp[-2]
                            sch.on("tensor", lambda e, cse=cse, ct=ct: e.wait_ge(cse.sem, ct))
                        mmt = mm.bump()
                        def _mmqk(e, m=m, pb=pb):
                            for kc in range(NFC):
                                i = e.matmul(out=pb[:], lhsT=w7q[:, kc * 3072 + m * 128: kc * 3072 + (m + 1) * 128],
                                             rhs=ln.xln[:, kc * TT:(kc + 1) * TT], start=(kc == 0), stop=(kc == NFC - 1))
                            i.then_inc(mm.sem, 1)
                        sch.on("tensor", _mmqk)
                        et = ev.bump()
                        def _evqk(e, mmt=mmt, m=m, pb=pb, j=j, pq=prev_qst):
                            e.wait_ge(mm.sem, mmt)
                            if m < 8:
                                if m == 0 and pq is not None:
                                    e.wait_ge(env.sems["st0"].sem, pq)
                                dst = qacc[:, m * TT:(m + 1) * TT]
                                e.tensor_scalar(out=dst, in0=pb[:], scalar1=b7qkv[:, m:m + 1],
                                                scalar2=1.0 / 32.0, op0=ALU.add, op1=ALU.mult).then_inc(ev.sem, 1)
                            else:
                                dst = ksb[:, (m - 8) * T + j * TT: (m - 8) * T + (j + 1) * TT]
                                e.tensor_scalar(out=dst, in0=pb[:], scalar1=b7qkv[:, m:m + 1],
                                                scalar2=None, op0=ALU.add).then_inc(ev.sem, 1)
                        sch.on("vector", _evqk)
                        grp.append((ev, et))
                        if m == 7:
                            sch.on("gpsimd", lambda e, t=et: e.wait_ge(ev.sem, t))
                            prev_qst = slot_dma(env, sch, "gpsimd", "st0",
                                                q_s[:, :, j * TT:(j + 1) * TT].rearrange("c p t -> p c t"),
                                                qacc.rearrange("p (c t) -> p c t", c=NFC))
                    for sc in range(4):
                        for vd in range(2):
                            pb = psm[(sc * 2 + vd) % 2]
                            cse, ct = grp[-2]
                            sch.on("tensor", lambda e, cse=cse, ct=ct: e.wait_ge(cse.sem, ct))
                            mmt = mm.bump()
                            def _mmv(e, sc=sc, vd=vd, pb=pb):
                                for kc in range(NFC):
                                    i = e.matmul(out=pb[:],
                                                 lhsT=ln.xln[:, kc * TT + sc * 128: kc * TT + (sc + 1) * 128],
                                                 rhs=w7q[:, kc * 3072 + 2048 + vd * 512: kc * 3072 + 2048 + (vd + 1) * 512],
                                                 start=(kc == 0), stop=(kc == NFC - 1))
                                i.then_inc(mm.sem, 1)
                            sch.on("tensor", _mmv)
                            et = ev.bump()
                            def _evv(e, mmt=mmt, sc=sc, vd=vd, pb=pb, j=j):
                                e.wait_ge(mm.sem, mmt)
                                dst = vtok[:, (j * 4 + sc) * D + vd * 512: (j * 4 + sc) * D + (vd + 1) * 512]
                                e.tensor_add(out=dst, in0=pb[:], in1=vbias[:, vd * 512:(vd + 1) * 512]).then_inc(ev.sem, 1)
                            sch.on("vector", _evv)
                            grp.append((ev, et))
                return grp[-1][1]

        def stage_b(s, lastA):
            for eng in ("sync", "gpsimd"):
                sch.on(eng, lambda e, t=lastA: e.wait_ge(ev.sem, t))
            with ExitStack() as stkB:
                w7out = env.sb(stkB, f"p7_wout{s}", [128, NFC * D], BF16)
                wt_o = {}
                for sl, t in emit_weight_load_T(env, sch, env.dram["w7_out"], w7out, D):
                    wt_o[sl] = max(wt_o.get(sl, 0), t)
                xf = env.sb(stkB, f"p7_xf{s}", [128, NFC * TT], F32)
                xout = xf
                ssb2 = [env.sb(stkB, f"p7_ssb{i}_{s}", [128, T], F32) for i in range(2)]
                awb2 = [env.sb(stkB, f"p7_awb{i}_{s}", [128, T], BF16) for i in range(2)]
                rows2 = env.sb(stkB, f"p7_rows{s}", [128, 8], F32)
                pss = [env.ps(stkB, f"p7_pss{i}_{s}", [128, TT]) for i in range(2)]
                psT = [env.ps(stkB, f"p7_psT{i}_{s}", [128, 128], BF16) for i in range(2)]
                psc = [env.ps(stkB, f"p7_psc{i}_{s}", [128, TT]) for i in range(2)]
                pso = [env.ps(stkB, f"p7_pso{i}_{s}", [128, TT]) for i in range(2)]

                qtile = env.sb(stkB, f"p7_qtile{s}", [128, NFC * 128], BF16)
                attnT4 = env.sb(stkB, f"p7_at4{s}", [128, 16 * TT], BF16)
                tp_tail = []
                ctxb4 = env.sb(stkB, f"p7_ctx4{s}", [128, NFC * TT], BF16)
                sch.on("tensor", lambda e, t=lastA: e.wait_ge(ev.sem, t))
                prev_sc_done = None
                ctx_done = None
                for j in range(NTILE):
                    sf = "s0" if j % 2 == 0 else "s1"
                    srcx = xsrc[s][:, :, j * TT:(j + 1) * TT].rearrange("c p t -> p c t")
                    if len(st_hist) >= 1:
                        sl, t = st_hist[-1]
                        sch.on("sync", lambda e, sl=sl, t=t: e.wait_ge(env.sems[sl].sem, t))
                    ld_t = slot_dma(env, sch, "sync", sf, xf.rearrange("p (c t) -> p c t", c=NFC), srcx)
                    ev_tp_last = None
                    sm_done = {}
                    tp_done = {}
                    def emit_tp(qc):
                        ev_tp = list(tp_tail)
                        for st16 in range(len(tp_tail), len(tp_tail) + 16):
                            pt = psT[st16 % 2]
                            mmt = mm.bump()
                            def _tp(e, st16=st16, pt=pt, t=sm_done[qc], qc=qc,
                                    tprev=(ev_tp[st16 - 2] if st16 >= 2 else None), cd=ctx_done,
                                    base=len(tp_tail)):
                                if st16 == base:
                                    e.wait_ge(ev.sem, t)
                                    if cd is not None:
                                        e.wait_ge(ev.sem, cd)
                                if tprev is not None:
                                    e.wait_ge(ev.sem, tprev)
                                e.transpose(out=pt[:], in_=awb2[qc % 2][:, (st16 - base) * 128:(st16 - base + 1) * 128],
                                            identity=identb[:]).then_inc(mm.sem, 1)
                            sch.on("tensor", _tp)
                            et = ev.bump()
                            def _evt(e, mmt=mmt, st16=st16, pt=pt, qc=qc, base=len(tp_tail)):
                                e.wait_ge(mm.sem, mmt)
                                e.tensor_copy(out=attnT4[:, (st16 - base) * TT + qc * 128:
                                                           (st16 - base) * TT + (qc + 1) * 128],
                                              in_=pt[:]).then_inc(ev.sem, 1)
                            sch.on("vector", _evt)
                            ev_tp.append(et)
                        tp_done[qc] = mm.n
                        tp_tail[:] = ev_tp[-2:]
                        return ev_tp[-1]

                    for qc in range(4):
                        qq = j * 4 + qc
                        ssb = ssb2[qc % 2]
                        awb = awb2[qc % 2]
                        rows = rows2[:, 4 * (qc % 2): 4 * (qc % 2) + 4]
                        if prev_sc_done is not None:
                            sch.on("gpsimd", lambda e, t=prev_sc_done: e.wait_ge(mm.sem, t))
                        lq = slot_dma(env, sch, "gpsimd", "g2",
                                      qtile.rearrange("p (c t) -> p c t", c=NFC),
                                      q_s[:, :, qq * 128:(qq + 1) * 128].rearrange("c p t -> p c t"))
                        ac_sc = []
                        for st_ in range(4):
                            pb = pss[st_ % 2]
                            if st_ >= 2:
                                sch.on("tensor", lambda e, t=ac_sc[st_ - 2]: e.wait_ge(ac.sem, t))
                            mmt = mm.bump()
                            def _mmsc(e, st_=st_, pb=pb, qq=qq, lq=lq):
                                if st_ == 0:
                                    e.wait_ge(env.sems["g2"].sem, lq)
                                for kc in range(NFC):
                                    i = e.matmul(out=pb[:], lhsT=qtile[:, kc * 128:(kc + 1) * 128],
                                                 rhs=ksb[:, kc * T + st_ * TT: kc * T + (st_ + 1) * TT],
                                                 start=(kc == 0), stop=(kc == NFC - 1))
                                i.then_inc(mm.sem, 1)
                            sch.on("tensor", _mmsc)
                            at = ac.bump()
                            def _evsc(e, mmt=mmt, st_=st_, pb=pb, ssb=ssb, td=tp_done.get(qc - 2)):
                                e.wait_ge(mm.sem, mmt)
                                if st_ == 0 and td is not None:
                                    e.wait_ge(mm.sem, td)
                                e.activation(out=ssb[:, st_ * TT:(st_ + 1) * TT], in_=pb[:],
                                             func=AF.Copy).then_inc(ac.sem, 1)
                            sch.on("scalar", _evsc)
                            ac_sc.append(at)
                        prev_sc_done = mm.n
                        ev_mx0 = ev.bump()
                        ev_mx = ev.bump()
                        def _mx(e, t=ac_sc[-1], ev_mx0=ev_mx0, ssb=ssb, rows=rows):
                            e.wait_ge(ac.sem, t)
                            e.tensor_reduce(out=rows[:, 0:1], in_=ssb[:], axis=mybir.AxisListType.X,
                                            op=ALU.max).then_inc(ev.sem, 1)
                            e.wait_ge(ev.sem, ev_mx0)
                            e.tensor_scalar_mul(out=rows[:, 1:2], in0=rows[:, 0:1], scalar1=-1.0).then_inc(ev.sem, 1)
                        sch.on("vector", _mx)
                        ac_ex = ac.bump()
                        def _ex(e, t=ev_mx, ssb=ssb, awb=awb, rows=rows, pn=sm_done.get(qc - 1)):
                            e.wait_ge(ev.sem, t)
                            e.activation(out=awb[:], in_=ssb[:], func=AF.Exp,
                                         bias=rows[:, 1:2]).then_inc(ac.sem, 1)
                        sch.on("scalar", _ex)
                        ev_s1 = ev.bump()
                        ev_s2 = ev.bump()
                        ev_at = ev.bump()
                        def _at(e, t=ac_ex, ev_s1=ev_s1, ev_s2=ev_s2, awb=awb, rows=rows):
                            e.wait_ge(ac.sem, t)
                            e.tensor_reduce(out=rows[:, 2:3], in_=awb[:], axis=mybir.AxisListType.X,
                                            op=ALU.add).then_inc(ev.sem, 1)
                            e.wait_ge(ev.sem, ev_s1)
                            e.reciprocal(out=rows[:, 3:4], in_=rows[:, 2:3]).then_inc(ev.sem, 1)
                            e.wait_ge(ev.sem, ev_s2)
                            e.tensor_scalar_mul(out=awb[:], in0=awb[:], scalar1=rows[:, 3:4]).then_inc(ev.sem, 1)
                        sch.on("vector", _at)
                        sm_done[qc] = ev_at
                        if qc >= 1:
                            ev_tp_last = emit_tp(qc - 1)
                    ev_tp_last = emit_tp(3)
                    ev_ctx = []
                    for m in range(NFC):
                        pb = psc[m % 2]
                        if m >= 2:
                            sch.on("tensor", lambda e, t=ev_ctx[m - 2]: e.wait_ge(ev.sem, t))
                        else:
                            sch.on("tensor", lambda e, t=ev_tp_last: e.wait_ge(ev.sem, t))
                        mmt = mm.bump()
                        def _mmc(e, m=m, pb=pb):
                            for sc in range(16):
                                i = e.matmul(out=pb[:], lhsT=vtok[:, sc * D + m * 128: sc * D + (m + 1) * 128],
                                             rhs=attnT4[:, sc * TT:(sc + 1) * TT],
                                             start=(sc == 0), stop=(sc == 15))
                            i.then_inc(mm.sem, 1)
                        sch.on("tensor", _mmc)
                        et = ev.bump()
                        def _evc(e, mmt=mmt, m=m, pb=pb):
                            e.wait_ge(mm.sem, mmt)
                            e.tensor_copy(out=ctxb4[:, m * TT:(m + 1) * TT], in_=pb[:]).then_inc(ev.sem, 1)
                        sch.on("vector", _evc)
                        ev_ctx.append(et)
                    ctx_done = ev_ctx[-1]
                    ev_o = []
                    for m in range(NFC):
                        pb = pso[m % 2]
                        if m >= 2:
                            sch.on("tensor", lambda e, t=ev_o[m - 2]: e.wait_ge(ev.sem, t))
                        else:
                            sch.on("tensor", lambda e, t=ev_ctx[-1]: e.wait_ge(ev.sem, t))
                            if m == 0:
                                def _wo(e):
                                    for sl, tt in wt_o.items():
                                        e.wait_ge(env.sems[sl].sem, tt)
                                sch.on("tensor", _wo)
                        mmt = mm.bump()
                        def _mmo(e, m=m, pb=pb):
                            for kc in range(NFC):
                                i = e.matmul(out=pb[:], lhsT=w7out[:, kc * D + m * 128: kc * D + (m + 1) * 128],
                                             rhs=ctxb4[:, kc * TT:(kc + 1) * TT],
                                             start=(kc == 0), stop=(kc == NFC - 1))
                            i.then_inc(mm.sem, 1)
                        sch.on("tensor", _mmo)
                        if m == 0 and len(st_hist) >= 1:
                            sl, t = st_hist[-1]
                            sch.on("vector", lambda e, sl=sl, t=t: e.wait_ge(env.sems[sl].sem, t))
                        et = ev.bump()
                        def _evo(e, mmt=mmt, m=m, pb=pb, ld_t=ld_t, sf=sf):
                            e.wait_ge(mm.sem, mmt)
                            if m == 0:
                                e.wait_ge(env.sems[sf].sem, ld_t)
                                e.wait_ge(ldh.sem, bias_ld)
                            cc = slice(m * TT, (m + 1) * TT)
                            e.scalar_tensor_tensor(out=xout[:, cc], in0=pb[:], scalar=b7out[:, m:m + 1],
                                                   in1=xf[:, cc], op0=ALU.add, op1=ALU.add).then_inc(ev.sem, 1)
                        sch.on("vector", _evo)
                        ev_o.append(et)
                    ev_o_last = ev_o[-1]
                    ssl = "st0" if j % 2 == 0 else "st1"
                    sch.on("gpsimd", lambda e, t=ev_o_last: e.wait_ge(ev.sem, t))
                    dst = xdst[s][:, :, j * TT:(j + 1) * TT].rearrange("c p t -> p c t")
                    st_t = slot_dma(env, sch, "gpsimd", ssl, dst,
                                    xout.rearrange("p (c t) -> p c t", c=NFC))
                    st_hist.append((ssl, st_t))
                return ev_o_last

        import os as _os
        if _os.environ.get("K_P7_DEBUG"):
            lastA = stage_a(0)
            dq = nc.dram_tensor("dbg_q", [128, NFC * T], BF16, kind="ExternalOutput").ap()
            dk = nc.dram_tensor("dbg_k", [128, NFC * T], BF16, kind="ExternalOutput").ap()
            dv = nc.dram_tensor("dbg_v", [128, 16 * D], BF16, kind="ExternalOutput").ap()
            stx = env.sems["st0"]
            base = stx.n
            def _dump(e, t=lastA, base=base):
                e.wait_ge(ev.sem, t)
                e.wait_ge(stx.sem, base)
                e.dma_start(out=dq, in_=qsb[:]).then_inc(stx.sem, 16)
                e.wait_ge(stx.sem, base + 16)
                e.dma_start(out=dk, in_=ksb[:]).then_inc(stx.sem, 16)
                e.wait_ge(stx.sem, base + 32)
                e.dma_start(out=dv, in_=vtok[:]).then_inc(stx.sem, 16)
                e.wait_ge(stx.sem, base + 48)
            sch.on("gpsimd", _dump)
            stx.n += 48
            emit_barrier(env, sch)
            sch.flush(nc)
            return
        for s in range(BL):
            lastA = stage_a(s)
            fence_ev[0] = stage_b(s, lastA)

        def _drain(e):
            e.wait_ge(env.sems["st0"].sem, env.sems["st0"].n)
            e.wait_ge(env.sems["st1"].sem, env.sems["st1"].n)
        sch.on("gpsimd", _drain)
        emit_barrier(env, sch)
        sch.flush(nc)





def emit_p_l0attnb(env, sch, xsrc, xdst):
    nc = env.nc
    ldh, mm, ev, ac = (env.sems[k] for k in ("ldh", "mm", "ev", "ac"))
    ones_bf = env.consts["ones_col_bf"]
    ones_row = env.consts["ones_row_f"]
    iota_col = env.consts["iota_col"]
    iota_row = env.consts["iota_row"]
    mask01 = env.consts["mask01"]
    rt_bf = env.consts["rt_bf"]
    iota2d_c = lambda: env.consts["iota2d"]
    LN2_35 = float(np.float32(-3.5 * math.log(2.0)))

    qs_s = nc.dram_tensor("p3_qs", [8, 128, T], BF16, kind="Internal").ap()
    ks_s = nc.dram_tensor("p3_ks", [8, 128, T], BF16, kind="Internal").ap()
    v_s = nc.dram_tensor("p3_vs", [16, 128, T], BF16, kind="Internal").ap()
    gate_s = nc.dram_tensor("p3_gs", [16, 128, T], BF16, kind="Internal").ap()
    gn_s = nc.dram_tensor("p3_gns", [16, 128, T], BF16, kind="Internal").ap()

    with ExitStack() as stk:
        ldrow = env.sb(stk, "p3_ldrow", [1, 8], F32)
        ldb = env.sb(stk, "p3_ldb", [128, 8], F32)
        decr = env.sb(stk, "p3_dec", [1, 8], F32)
        fence = [None]

        dec_ld = ldh.bump(16)
        sch.on("sync", lambda e: e.dma_start(out=decr[0:1, :], in_=env.dram["p_decay"]).then_inc(ldh.sem, 16))
        ac_ld = ac.bump()
        def _ln(e):
            e.wait_ge(ldh.sem, dec_ld)
            e.activation(out=ldrow[0:1, :], in_=decr[0:1, :], func=AF.Ln).then_inc(ac.sem, 1)
        sch.on("scalar", _ln)

        def scope_fence(engines=("sync", "gpsimd", "tensor")):
            if fence[0] is not None:
                for eng in engines:
                    sch.on(eng, lambda e, t=fence[0]: e.wait_ge(ev.sem, t))
            st0n, st1n = env.sems["st0"].n, env.sems["st1"].n
            for eng in ("sync", "gpsimd", "tensor", "vector"):
                sch.on(eng, lambda e, a=st0n, b=st1n: (e.wait_ge(env.sems["st0"].sem, a),
                                                       e.wait_ge(env.sems["st1"].sem, b)))

        def stage_a1(s):
            scope_fence()
            with ExitStack() as sa:
                xbf = [env.sb(sa, f"p3a_xbf{i}_{s}", [128, NFC * TT], BF16) for i in range(2)]
                ln2 = [LNBufs(env, sa, f"p3a_lnA{s}")]
                ln2.append(LNBufs(env, sa, f"p3a_lnB{s}", share_ps=ln2[0]))
                wqk = env.sb(sa, f"p3a_wqk{s}", [128, NFC * 2048], BF16)
                ro1 = [env.sb(sa, f"p3a_ro1{i}_{s}", [128, TT], F32) for i in range(2)]
                qacc = env.sb(sa, f"p3a_qacc{s}", [128, NFC * TT], BF16)
                kacc = env.sb(sa, f"p3a_kacc{s}", [128, NFC * TT], BF16)
                tabs = [env.sb(sa, f"p3a_tab{i}_{s}", [128, NFC * TT], F32) for i in range(2)]
                psm = [env.ps(sa, f"p3a_ps{i}_{s}", [128, TT]) for i in range(2)]
                psr = env.ps(sa, f"p3a_psr{s}", [128, TT])
                psr2 = env.ps(sa, f"p3a_psr2{s}", [128, TT])
                rot_hist = []
                mmc, ev2c, mm2c, ac2c = (env.sems[k] for k in ("mm", "ev2", "mm2", "ac2"))

                wt = {}
                for kc in range(NFC):
                    sl = ("g2", "g3")[kc % 2]
                    t = slot_dma(env, sch, "gpsimd", sl, wqk[:, kc * 2048:(kc + 1) * 2048],
                                 env.dram["w3_qkv"][kc * 128:(kc + 1) * 128, 0:2048])
                    wt[sl] = t
                evln, lninfo, lastmm, evq_hist = {}, {}, {}, {}

                def do_ln(g, cap):
                    bi = g % 2
                    sb_ = "g0" if bi == 0 else "g1"
                    srcx = xsrc[s][:, :, g * TT:(g + 1) * TT].rearrange("c p t -> p c t")
                    if g >= 2:
                        sch.on("gpsimd", lambda e, t=evln[g - 2]: e.wait_ge(ev2c.sem, t))
                    ldg_t = slot_dma(env, sch, "gpsimd", sb_,
                                     xbf[bi].rearrange("p (c t) -> p c t", c=NFC), srcx)
                    kw = {"cap": cap, "sems": ("mm2", "ev2", "ac2"), "norm_src": "bf"}
                    if g >= 1:
                        kw["sq_guard"] = (mm2c, lninfo[g - 1]["mm_st"])
                        kw["rows_guard"] = [(mm2c, lninfo[g - 1]["mm_bc"]), (ac2c, lninfo[g - 1]["ac_rs"])]
                        kw["bc_guard"] = (ev2c, evln[g - 1])
                    if g >= 2:
                        kw["xln_guard"] = (mmc, lastmm[g - 2])
                    evln[g], lninfo[g] = emit_ln(env, sch, ln2[bi], g, None, xbf[bi],
                                                 (None, None), (sb_, ldg_t), **kw)

                do_ln(0, None)
                for j in range(NTILE):
                    bi = j % 2
                    ln = ln2[bi]
                    cap = {} if j + 1 < NTILE else None
                    if cap is not None:
                        do_ln(j + 1, cap)
                    def play(sec, cap=cap):
                        if cap is not None:
                            for e_, f_ in cap.get(sec, []):
                                sch.on(e_, f_)
                    play("sq")
                    def _w0(e, t=evln[j], wt=dict(wt)):
                        e.wait_ge(ev2c.sem, t)
                        for sl, tt in wt.items():
                            e.wait_ge(env.sems[sl].sem, tt)
                    sch.on("tensor", _w0)
                    evq = []
                    for idx in range(16):
                        if idx == 3:
                            play("stats")
                        if idx == 6:
                            play("bc")
                        qk, h = divmod(idx, 8)
                        col0 = 0 if qk == 0 else 1024
                        acc = qacc if qk == 0 else kacc
                        pb = psm[idx % 2]
                        if idx >= 2:
                            sch.on("tensor", lambda e, t=evq[idx - 2]: e.wait_ge(ev.sem, t))
                        elif (j - 1) in evq_hist:
                            sch.on("tensor", lambda e, t=evq_hist[j - 1][14 + idx]: e.wait_ge(ev.sem, t))
                        mm_q = mm.bump()
                        def _mmq(e, h=h, col0=col0, pb=pb, ln=ln):
                            for kc in range(NFC):
                                i = e.matmul(out=pb[:],
                                             lhsT=wqk[:, kc * 2048 + col0 + h * 128: kc * 2048 + col0 + (h + 1) * 128],
                                             rhs=ln.xln[:, kc * TT:(kc + 1) * TT],
                                             start=(kc == 0), stop=(kc == NFC - 1))
                            i.then_inc(mm.sem, 1)
                        sch.on("tensor", _mmq)
                        if idx == 4:
                            play("rows")
                            play("sd")
                        et = ev.bump()
                        def _evq(e, mm_q=mm_q, h=h, pb=pb, acc=acc):
                            e.wait_ge(mm.sem, mm_q)
                            e.tensor_copy(out=acc[:, h * TT:(h + 1) * TT], in_=pb[:]).then_inc(ev.sem, 1)
                        sch.on("vector", _evq)
                        evq.append(et)
                    evq_hist[j] = evq
                    lastmm[j] = mm.n

                    st_waits = []
                    for qk in range(2):
                        acc = qacc if qk == 0 else kacc
                        tq = ("rot_cq", "rot_sq") if qk == 0 else ("rot_ck", "rot_sk")
                        if rot_hist:
                            sch.on("gpsimd", lambda e, t=rot_hist[-1]: e.wait_ge(ev.sem, t))
                        tw = []
                        for z, nmr in enumerate(tq):
                            sl = ("g2", "g3")[z]
                            tgt = slot_dma(env, sch, "gpsimd", sl,
                                           tabs[z].rearrange("p (h t) -> p h t", h=NFC),
                                           env.dram[nmr][:, :, j * TT:(j + 1) * TT].rearrange("h p t -> p h t"))
                            tw.append((sl, tgt))
                        ev_rot_last = None
                        for h in range(NFC):
                            pr = psr if h % 2 == 0 else psr2
                            rb = ro1[h % 2]
                            mm_r = mm.bump()
                            def _mmr(e, h=h, acc=acc, pr=pr,
                                     guard=(evq[qk * 8 + 7] if h == 0 else None),
                                     pg2=(rot_hist[-2] if len(rot_hist) >= 2 else None)):
                                if guard is not None:
                                    e.wait_ge(ev.sem, guard)
                                if pg2 is not None:
                                    e.wait_ge(ev.sem, pg2)
                                e.matmul(out=pr[:], lhsT=rt_bf[:], rhs=acc[:, h * TT:(h + 1) * TT],
                                         start=True, stop=True).then_inc(mm.sem, 1)
                            sch.on("tensor", _mmr)
                            et = ev.bump()
                            def _rot(e, mm_r=mm_r, h=h, acc=acc, rb=rb, pr=pr, tw=tuple(tw)):
                                e.wait_ge(mm.sem, mm_r)
                                if h == 0:
                                    for sl, tt in tw:
                                        e.wait_ge(env.sems[sl].sem, tt)
                                e.tensor_mul(out=rb[:], in0=pr[:], in1=tabs[1][:, h * TT:(h + 1) * TT])
                                e.tensor_mul(out=acc[:, h * TT:(h + 1) * TT], in0=acc[:, h * TT:(h + 1) * TT],
                                             in1=tabs[0][:, h * TT:(h + 1) * TT])
                                e.tensor_add(out=acc[:, h * TT:(h + 1) * TT], in0=acc[:, h * TT:(h + 1) * TT],
                                             in1=rb[:]).then_inc(ev.sem, 1)
                            sch.on("vector", _rot)
                            ev_rot_last = et
                            rot_hist.append(et)
                        sch.on("gpsimd", lambda e, t=ev_rot_last: e.wait_ge(ev.sem, t))
                        dstd = (qs_s if qk == 0 else ks_s)[:, :, j * TT:(j + 1) * TT].rearrange("h p t -> p h t")
                        slot_dma(env, sch, "gpsimd", "st0" if qk == 0 else "st1",
                                 dstd, acc.rearrange("p (h t) -> p h t", h=NFC))
                        st_waits.append(("st0" if qk == 0 else "st1",
                                         env.sems["st0" if qk == 0 else "st1"].n))
                        if qk == 0:
                            play("norm")
                    for sl, a_ in st_waits:
                        sch.on("vector", lambda e, sl=sl, a=a_: e.wait_ge(env.sems[sl].sem, a))
                fence[0] = rot_hist[-1]

        def stage_a2(s):
            scope_fence()
            with ExitStack() as sa:
                xbf = env.sb(sa, f"p3b_xbf{s}", [128, NFC * TT], BF16)
                ln = LNBufs(env, sa, f"p3b_ln{s}")
                wv = env.sb(sa, f"p3b_wv{s}", [128, NFC * 2048], BF16)
                wg = env.sb(sa, f"p3b_wg{s}", [128, NFC * 2048], BF16)
                vacc = [env.sb(sa, f"p3b_vacc{i}_{s}", [128, 4 * 2048], BF16) for i in range(2)]
                gacc = [env.sb(sa, f"p3b_gacc{i}_{s}", [128, 16 * TT], BF16) for i in range(2)]
                psm = [env.ps(sa, f"p3b_ps{i}_{s}", [128, TT]) for i in range(2)]

                wt = {}
                for kc in range(NFC):
                    sl = ("g2", "g3")[kc % 2]
                    wt[sl] = slot_dma(env, sch, "gpsimd", sl, wv[:, kc * 2048:(kc + 1) * 2048],
                                      env.dram["w3_qkv"][kc * 128:(kc + 1) * 128, 2048:4096])
                for kc in range(NFC):
                    sl = ("g2", "g3")[kc % 2]
                    wt[sl] = slot_dma(env, sch, "gpsimd", sl, wg[:, kc * 2048:(kc + 1) * 2048],
                                      env.dram["w3_g"][kc * 128:(kc + 1) * 128, :])
                prev_ln = None
                prev_grp = None
                st_hist_a2 = {}
                for j in range(NTILE):
                    sb_ = "g0" if j % 2 == 0 else "g1"
                    srcx = xsrc[s][:, :, j * TT:(j + 1) * TT].rearrange("c p t -> p c t")
                    if prev_ln is not None:
                        sch.on("gpsimd", lambda e, t=prev_ln: e.wait_ge(ev.sem, t))
                    ldg_t = slot_dma(env, sch, "gpsimd", sb_, xbf.rearrange("p (c t) -> p c t", c=NFC), srcx)
                    ev_ln, _lninfo = emit_ln(env, sch, ln, j, None, xbf, (None, None), (sb_, ldg_t), norm_src="bf")
                    prev_ln = ev_ln
                    def _w0(e, t=ev_ln, wt=dict(wt)):
                        e.wait_ge(ev.sem, t)
                        for sl, tt in wt.items():
                            e.wait_ge(env.sems[sl].sem, tt)
                    sch.on("tensor", _w0)
                    grp = []
                    for sc in range(4):
                        for dvc in range(4):
                            pb = psm[(sc * 4 + dvc) % 2]
                            if len(grp) >= 2:
                                cse, ct = grp[-2]
                                sch.on("tensor", lambda e, cse=cse, ct=ct: e.wait_ge(cse.sem, ct))
                            elif prev_grp is not None:
                                cse, ct = prev_grp[-2 + len(grp)]
                                sch.on("tensor", lambda e, cse=cse, ct=ct: e.wait_ge(cse.sem, ct))
                            mmt = mm.bump()
                            def _mmv(e, sc=sc, dvc=dvc, pb=pb):
                                for kc in range(NFC):
                                    i = e.matmul(out=pb[:],
                                                 lhsT=ln.xln[:, kc * TT + sc * 128: kc * TT + (sc + 1) * 128],
                                                 rhs=wv[:, kc * 2048 + dvc * 512: kc * 2048 + (dvc + 1) * 512],
                                                 start=(kc == 0), stop=(kc == NFC - 1))
                                i.then_inc(mm.sem, 1)
                            sch.on("tensor", _mmv)
                            et = ev.bump()
                            def _evv(e, mmt=mmt, sc=sc, dvc=dvc, pb=pb, bi=j % 2,
                                     stg=(st_hist_a2.get(j - 2))):
                                e.wait_ge(mm.sem, mmt)
                                if sc == 0 and dvc == 0 and stg is not None:
                                    e.wait_ge(env.sems[stg[0][0]].sem, stg[0][1])
                                e.tensor_copy(out=vacc[bi][:, sc * 2048 + dvc * 512: sc * 2048 + (dvc + 1) * 512],
                                              in_=pb[:]).then_inc(ev.sem, 1)
                            sch.on("vector", _evv)
                            grp.append((ev, et))
                    for m in range(16):
                        pb = psm[m % 2]
                        cse, ct = grp[-2]
                        sch.on("tensor", lambda e, cse=cse, ct=ct: e.wait_ge(cse.sem, ct))
                        mmt = mm.bump()
                        def _mmg(e, m=m, pb=pb):
                            for kc in range(NFC):
                                i = e.matmul(out=pb[:],
                                             lhsT=wg[:, kc * 2048 + m * 128: kc * 2048 + (m + 1) * 128],
                                             rhs=ln.xln[:, kc * TT:(kc + 1) * TT],
                                             start=(kc == 0), stop=(kc == NFC - 1))
                            i.then_inc(mm.sem, 1)
                        sch.on("tensor", _mmg)
                        at = ac.bump()
                        def _evg(e, mmt=mmt, m=m, pb=pb, bi=j % 2, stg=(st_hist_a2.get(j - 2))):
                            e.wait_ge(mm.sem, mmt)
                            if m == 0 and stg is not None:
                                e.wait_ge(env.sems[stg[1][0]].sem, stg[1][1])
                            e.activation(out=gacc[bi][:, m * TT:(m + 1) * TT], in_=pb[:],
                                         func=AF.Silu).then_inc(ac.sem, 1)
                        sch.on("scalar", _evg)
                        grp.append((ac, at))
                    bi = j % 2
                    sch.on("gpsimd", lambda e, t=grp[4 * 4 - 1][1]: e.wait_ge(ev.sem, t))
                    t0s = slot_dma(env, sch, "gpsimd", "st0",
                                   v_s[j * 4:(j + 1) * 4].rearrange("s p v -> p s v"),
                                   vacc[bi].rearrange("p (s v) -> p s v", s=4))
                    sch.on("gpsimd", lambda e, t=grp[-1][1]: e.wait_ge(ac.sem, t))
                    t1s = slot_dma(env, sch, "gpsimd", "st1",
                                   gate_s[:, :, j * TT:(j + 1) * TT].rearrange("c p t -> p c t"),
                                   gacc[bi].rearrange("p (c t) -> p c t", c=16))
                    st_hist_a2[j] = (("st0", t0s), ("st1", t1s))
                    prev_grp = list(grp)
                fence[0] = grp[-1][1] if grp else None
                last_ac = grp[-1][1]
                evm = ev.bump()
                def _mk(e, t=last_ac):
                    e.wait_ge(ac.sem, t)
                    e.tensor_copy(out=vacc[0][:, 0:16], in_=vacc[0][:, 0:16]).then_inc(ev.sem, 1)
                sch.on("vector", _mk)
                fence[0] = evm

        def stage_b(s):
            scope_fence()
            with ExitStack() as sb_:
                qh = [env.sb(sb_, f"p3c_qh{i}_{s}", [128, T], BF16) for i in range(2)]
                kh = [env.sb(sb_, f"p3c_kh{i}_{s}", [128, T], BF16) for i in range(2)]
                vh = [env.sb(sb_, f"p3c_vh{i}_{s}", [128, 16 * 256], BF16) for i in range(2)]
                gh = [env.sb(sb_, f"p3c_gh{i}_{s}", [128, 2 * T], BF16) for i in range(2)]
                qp = [env.sb(sb_, f"p3c_qp{i}_{s}", [128, T], BF16) for i in range(2)]
                scT = [env.sb(sb_, f"p3c_scT{i}_{s}", [128, TT], BF16) for i in range(2)]
                onb = env.sb(sb_, f"p3c_onb{s}", [128, 2 * T], BF16)
                sq2 = env.sb(sb_, f"p3c_sq2{s}", [128, 2 * T], BF16)
                gnacc = [env.sb(sb_, f"p3c_gn{i}_{s}", [128, 2 * T], BF16) for i in range(2)]
                rows6 = env.sb(sb_, f"p3c_rows6{s}", [1, 6 * T], F32)
                ccol = env.sb(sb_, f"p3c_ccol{s}", [128, 2], F32)
                ccols = env.sb(sb_, f"p3c_ccols{s}", [128, 32], F32)
                prow = env.sb(sb_, f"p3c_prow{s}", [1, T], F32)
                psc = [env.ps(sb_, f"p3c_psc{i}_{s}", [128, TT]) for i in range(2)]
                pso = [env.ps(sb_, f"p3c_pso{i}_{s}", [128, TT]) for i in range(2)]
                psr = [env.ps(sb_, f"p3c_psr{i}_{s}", [128, TT]) for i in range(2)]
                psw = [env.ps(sb_, f"p3c_psw{i}_{s}", [128, TT]) for i in range(2)]

                if s == 0:
                    mm_ldb = mm.bump()
                    def _bl(e):
                        e.wait_ge(ac.sem, ac_ld)
                        e.matmul(out=psr[0][:, 0:8], lhsT=ones_row[0:1, :], rhs=ldrow[0:1, :],
                                 start=True, stop=True).then_inc(mm.sem, 1)
                    sch.on("tensor", _bl)
                    ev_ldb = ev.bump()
                    def _evl(e, mm_ldb=mm_ldb):
                        e.wait_ge(mm.sem, mm_ldb)
                        e.tensor_copy(out=ldb[:], in_=psr[0][:, 0:8]).then_inc(ev.sem, 1)
                    sch.on("vector", _evl)

                h_done = {}
                sc_done = {}
                st_gn = {}
                prev_evon = None
                for h in range(NFC):
                    hb = h % 2
                    if h >= 2:
                        sch.on("gpsimd", lambda e, t=h_done[h - 2]: e.wait_ge(ev.sem, t))
                    lq = slot_dma(env, sch, "gpsimd", "g0", qh[hb][:], qs_s[h])
                    lk = slot_dma(env, sch, "gpsimd", "g1", kh[hb][:], ks_s[h])
                    lv = slot_dma(env, sch, "gpsimd", "g2", vh[hb].rearrange("p (s v) -> p s v", s=16),
                                  v_s[:, :, h * 256:(h + 1) * 256].rearrange("s p v -> p s v"))
                    lg = slot_dma(env, sch, "gpsimd", "g3", gh[hb].rearrange("p (c t) -> p c t", c=2),
                                  gate_s[2 * h:2 * h + 2].rearrange("c p t -> p c t"))
                    ev_cc = ev.bump()
                    def _cc(e, h=h, prev=h_done.get(h - 2)):
                        if prev is not None:
                            e.wait_ge(ev.sem, prev)
                        e.wait_ge(ac.sem, ac_ld)
                        e.tensor_scalar_mul(out=ccols[:, 0:16], in0=iota2d_c()[:], scalar1=ldb[:, h:h + 1])
                        e.tensor_scalar(out=prow[0:1, :], in0=iota_row[0:1, :], scalar1=ldrow[0:1, h:h + 1],
                                        scalar2=None, op0=ALU.mult).then_inc(ev.sem, 1)
                    sch.on("vector", _cc)
                    ac_cc = ac.bump(2)
                    def _ce(e, ev_cc=ev_cc):
                        e.wait_ge(ev.sem, ev_cc)
                        e.activation(out=ccols[:, 16:32], in_=ccols[:, 0:16], func=AF.Exp, scale=-1.0).then_inc(ac.sem, 1)
                        e.activation(out=prow[0:1, :], in_=prow[0:1, :], func=AF.Exp,
                                     bias=env.fconst(LN2_35)[0:1, :]).then_inc(ac.sem, 1)
                    sch.on("scalar", _ce)
                    ev_qp = None
                    qp_hist = []
                    for tj in range(4):
                        bb = psr[tj % 2]
                        mm_b = mm.bump()
                        def _bb(e, tj=tj, bb=bb, ac_cc=ac_cc,
                                prevq=(sc_done.get(h - 1) if tj < 2 else None),
                                pq=(qp_hist[tj - 2] if tj >= 2 else None)):
                            if tj == 0:
                                e.wait_ge(ac.sem, ac_cc)
                            if prevq is not None:
                                e.wait_ge(mm.sem, prevq)
                            if pq is not None:
                                e.wait_ge(ev.sem, pq)
                            e.matmul(out=bb[:], lhsT=ones_row[0:1, :], rhs=prow[0:1, tj * TT:(tj + 1) * TT],
                                     start=True, stop=True).then_inc(mm.sem, 1)
                        sch.on("tensor", _bb)
                        et = ev.bump()
                        def _qp(e, mm_b=mm_b, tj=tj, bb=bb, lq=lq, hb=hb):
                            e.wait_ge(mm.sem, mm_b)
                            if tj == 0:
                                e.wait_ge(env.sems["g0"].sem, lq)
                            e.tensor_mul(out=qp[hb][:, tj * TT:(tj + 1) * TT], in0=qh[hb][:, tj * TT:(tj + 1) * TT],
                                         in1=bb[:]).then_inc(ev.sem, 1)
                        sch.on("vector", _qp)
                        qp_hist.append(et)
                        ev_qp = et
                    for j in range(NTILE):
                        n_i = 4 * (j + 1)
                        ev_sc = {}
                        geom = {}
                        out_emit = []
                        for i in range(n_i):
                            start_t = max(j * TT, i * 128)
                            off = start_t - j * TT
                            N = (j + 1) * TT - start_t
                            geom[i] = (start_t, off, N)
                            pb = psc[i % 2]
                            sct = scT[i % 2]
                            mmt = mm.bump()
                            def _mms(e, i=i, j=j, pb=pb, start_t=start_t, N=N, lk=lk, ev_qp=ev_qp, hb=hb,
                                     prev=(ev_sc.get(i - 2))):
                                if i == 0 and j == 0:
                                    e.wait_ge(env.sems["g1"].sem, lk)
                                    e.wait_ge(ev.sem, ev_qp)
                                if prev is not None:
                                    e.wait_ge(prev[0].sem, prev[1])
                                e.matmul(out=pb[:, 0:N], lhsT=kh[hb][:, i * 128:(i + 1) * 128],
                                         rhs=qp[hb][:, start_t:(j + 1) * TT],
                                         start=True, stop=True).then_inc(mm.sem, 1)
                            sch.on("tensor", _mms)
                            at = ac.bump()
                            def _evs2(e, mmt=mmt, i=i, pb=pb, sct=sct, off=off, N=N):
                                e.wait_ge(mm.sem, mmt)
                                e.activation(out=sct[:, off:off + N], in_=pb[:, 0:N], func=AF.Copy,
                                             scale=ccols[:, 16 + i:17 + i]).then_inc(ac.sem, 1)
                            sch.on("scalar", _evs2)
                            if i >= 4 * j:
                                et = ev.bump()
                                def _msk(e, at=at, sct=sct, off=off):
                                    e.wait_ge(ac.sem, at)
                                    e.tensor_mul(out=sct[:, off:off + 128], in0=sct[:, off:off + 128],
                                                 in1=mask01[:]).then_inc(ev.sem, 1)
                                sch.on("vector", _msk)
                                ev_sc[i] = (ev, et)
                            else:
                                ev_sc[i] = (ac, at)

                            def emit_out(io):
                                st_o, off_o, N_o = geom[io]
                                sct_o = scT[io % 2]
                                for dc in range(2):
                                    mm.bump()
                                    def _mmo(e, io=io, j=j, dc=dc, sct_o=sct_o, off_o=off_o, N_o=N_o,
                                             dep=ev_sc[io], lv=lv, n_i=n_i, ogd=prev_evon, hb=hb):
                                        if io == 0 and dc == 0:
                                            e.wait_ge(env.sems["g2"].sem, lv)
                                            if ogd is not None:
                                                e.wait_ge(ev.sem, ogd)
                                        e.wait_ge(dep[0].sem, dep[1])
                                        e.matmul(out=pso[dc][:, off_o:off_o + N_o],
                                                 lhsT=vh[hb][:, io * 256 + dc * 128: io * 256 + (dc + 1) * 128],
                                                 rhs=sct_o[:, off_o:off_o + N_o],
                                                 start=(io == 0), stop=(io == n_i - 1)).then_inc(mm.sem, 1)
                                    sch.on("tensor", _mmo)

                            if i >= 1:
                                emit_out(i - 1)
                            if i == n_i - 1:
                                emit_out(i)
                        mm_out = mm.n
                        ev_on = ev.bump()
                        def _evon(e, mm_out=mm_out, j=j):
                            e.wait_ge(mm.sem, mm_out)
                            e.tensor_copy(out=onb[:, 0 * T + j * TT: 0 * T + (j + 1) * TT], in_=pso[0][:])
                            e.tensor_copy(out=onb[:, 1 * T + j * TT: 1 * T + (j + 1) * TT],
                                          in_=pso[1][:]).then_inc(ev.sem, 1)
                        sch.on("vector", _evon)
                        prev_evon = ev_on
                    sc_done[h] = mm.n
                    ac_sq = ac.bump()
                    def _sq(e, ev_on=prev_evon):
                        e.wait_ge(ev.sem, ev_on)
                        e.activation(out=sq2[:], in_=onb[:], func=AF.Square).then_inc(ac.sem, 1)
                    sch.on("scalar", _sq)
                    R6 = rows6
                    mu6, m26, var6, sd6, nm6, rs6 = (R6[0:1, z * T:(z + 1) * T] for z in range(6))
                    ev_st = {}
                    for jj in range(4):
                        mm_st = mm.bump(2)
                        def _stm(e, jj=jj, ac_sq=ac_sq, prevw=(ev_st.get(jj - 1))):
                            if jj == 0:
                                e.wait_ge(ac.sem, ac_sq)
                            if prevw is not None:
                                e.wait_ge(ev.sem, prevw)
                            for dc in range(2):
                                i = e.matmul(out=psw[0][0:1, :], lhsT=ones_bf[:, 0:1],
                                             rhs=onb[:, dc * T + jj * TT: dc * T + (jj + 1) * TT],
                                             start=(dc == 0), stop=(dc == 1))
                            i.then_inc(mm.sem, 1)
                            for dc in range(2):
                                i = e.matmul(out=psw[1][0:1, :], lhsT=ones_bf[:, 0:1],
                                             rhs=sq2[:, dc * T + jj * TT: dc * T + (jj + 1) * TT],
                                             start=(dc == 0), stop=(dc == 1))
                            i.then_inc(mm.sem, 1)
                        sch.on("tensor", _stm)
                        et = ev.bump()
                        def _cpw(e, jj=jj, mm_st=mm_st):
                            e.wait_ge(mm.sem, mm_st)
                            e.tensor_copy(out=mu6[:, jj * TT:(jj + 1) * TT], in_=psw[0][0:1, :])
                            e.tensor_copy(out=m26[:, jj * TT:(jj + 1) * TT], in_=psw[1][0:1, :]).then_inc(ev.sem, 1)
                        sch.on("vector", _cpw)
                        ev_st[jj] = et
                    ev_rows = ev.bump()
                    def _rw(e, t=ev_st[3]):
                        e.wait_ge(ev.sem, t)
                        e.tensor_scalar_mul(out=mu6, in0=mu6, scalar1=1.0 / 256.0)
                        e.tensor_scalar_mul(out=m26, in0=m26, scalar1=1.0 / 256.0)
                        e.tensor_mul(out=var6, in0=mu6, in1=mu6)
                        e.tensor_sub(out=var6, in0=m26, in1=var6).then_inc(ev.sem, 1)
                    sch.on("vector", _rw)
                    ac_sd = ac.bump(2)
                    def _sd(e, ev_rows=ev_rows):
                        e.wait_ge(ev.sem, ev_rows)
                        e.activation(out=sd6, in_=var6, func=AF.Ln,
                                     bias=env.fconst(1e-5)[0:1, :]).then_inc(ac.sem, 1)
                        e.activation(out=rs6, in_=sd6, func=AF.Exp, scale=-0.5).then_inc(ac.sem, 1)
                    sch.on("scalar", _sd)
                    ev_rs = ev.bump()
                    def _rs(e, ac_sd=ac_sd):
                        e.wait_ge(ac.sem, ac_sd)
                        e.tensor_mul(out=nm6, in0=mu6, in1=rs6)
                        e.tensor_scalar_mul(out=nm6, in0=nm6, scalar1=-1.0).then_inc(ev.sem, 1)
                    sch.on("vector", _rs)
                    gn_last = None
                    for jj in range(4):
                        mm_bc = mm.bump(2)
                        def _bc(e, jj=jj, ev_rs=ev_rs, gprev=gn_last):
                            if jj == 0:
                                e.wait_ge(ev.sem, ev_rs)
                            else:
                                e.wait_ge(ev.sem, gprev)
                            e.matmul(out=psw[0][:], lhsT=ones_row[0:1, :], rhs=nm6[:, jj * TT:(jj + 1) * TT],
                                     start=True, stop=True).then_inc(mm.sem, 1)
                            e.matmul(out=psw[1][:], lhsT=ones_row[0:1, :], rhs=rs6[:, jj * TT:(jj + 1) * TT],
                                     start=True, stop=True).then_inc(mm.sem, 1)
                        sch.on("tensor", _bc)
                        et = ev.bump()
                        def _gn(e, jj=jj, mm_bc=mm_bc, h=h, lg=lg, hb=hb, stp=st_gn.get(h - 2)):
                            e.wait_ge(mm.sem, mm_bc)
                            if jj == 0:
                                e.wait_ge(env.sems["g3"].sem, lg)
                                if stp is not None:
                                    e.wait_ge(env.sems[stp[0]].sem, stp[1])
                            last = None
                            for dc in range(2):
                                cc = slice(dc * T + jj * TT, dc * T + (jj + 1) * TT)
                                e.tensor_mul(out=gnacc[hb][:, cc], in0=onb[:, cc], in1=psw[1][:])
                                e.tensor_add(out=gnacc[hb][:, cc], in0=gnacc[hb][:, cc], in1=psw[0][:])
                                last = e.tensor_mul(out=gnacc[hb][:, cc], in0=gnacc[hb][:, cc],
                                                    in1=gh[hb][:, cc])
                            last.then_inc(ev.sem, 1)
                        sch.on("vector", _gn)
                        gn_last = et
                    sch.on("gpsimd", lambda e, t=gn_last: e.wait_ge(ev.sem, t))
                    sslh = "st0" if hb == 0 else "st1"
                    st_t = slot_dma(env, sch, "gpsimd", sslh,
                                    gn_s[2 * h:2 * h + 2].rearrange("c p t -> p c t"),
                                    gnacc[hb].rearrange("p (c t) -> p c t", c=2))
                    st_gn[h] = (sslh, st_t)
                    h_done[h] = gn_last
                fence[0] = h_done[NFC - 1]

        def stage_b2(s):
            scope_fence()
            with ExitStack() as sb_:
                gnt = env.sb(sb_, f"p3d_gnt{s}", [128, 16 * TT], BF16)
                xf = env.sb(sb_, f"p3d_xf{s}", [128, NFC * TT], F32)
                xout = env.sb(sb_, f"p3d_xout{s}", [128, NFC * TT], F32)
                wp = env.sb(sb_, f"p3d_wp{s}", [128, 16 * D], BF16)
                psm = [env.ps(sb_, f"p3d_ps{i}_{s}", [128, TT]) for i in range(2)]
                wt = {}
                for kc in range(16):
                    sl = ("g2", "g3")[kc % 2]
                    wt[sl] = slot_dma(env, sch, "gpsimd", sl, wp[:, kc * D:(kc + 1) * D],
                                      env.dram["w3_proj"][kc * 128:(kc + 1) * 128, :])
                prev_done = None
                prev_evo = None
                for j in range(NTILE):
                    sf = "s0" if j % 2 == 0 else "s1"
                    gsl = "g0" if j % 2 == 0 else "g1"
                    srcx = xsrc[s][:, :, j * TT:(j + 1) * TT].rearrange("c p t -> p c t")
                    if prev_done is not None:
                        sch.on("sync", lambda e, t=prev_done: e.wait_ge(ev.sem, t))
                        sch.on("gpsimd", lambda e, t=prev_done: e.wait_ge(ev.sem, t))
                    ld_t = slot_dma(env, sch, "sync", sf, xf.rearrange("p (c t) -> p c t", c=NFC), srcx)
                    lg_t = slot_dma(env, sch, "gpsimd", gsl, gnt.rearrange("p (c t) -> p c t", c=16),
                                    gn_s[:, :, j * TT:(j + 1) * TT].rearrange("c p t -> p c t"))
                    ev_o = []
                    for m in range(NFC):
                        pb = psm[m % 2]
                        if m >= 2:
                            sch.on("tensor", lambda e, t=ev_o[m - 2]: e.wait_ge(ev.sem, t))
                        elif prev_evo is not None:
                            sch.on("tensor", lambda e, t=prev_evo[6 + m]: e.wait_ge(ev.sem, t))
                        mmt = mm.bump()
                        def _mm(e, m=m, pb=pb, lg_t=lg_t, gsl=gsl, wt=dict(wt)):
                            if m == 0:
                                e.wait_ge(env.sems[gsl].sem, lg_t)
                                for sl, tt in wt.items():
                                    e.wait_ge(env.sems[sl].sem, tt)
                            for kc in range(16):
                                i = e.matmul(out=pb[:], lhsT=wp[:, kc * D + m * 128: kc * D + (m + 1) * 128],
                                             rhs=gnt[:, kc * TT:(kc + 1) * TT], start=(kc == 0), stop=(kc == 15))
                            i.then_inc(mm.sem, 1)
                        sch.on("tensor", _mm)
                        st0n, st1n = env.sems["st0"].n, env.sems["st1"].n
                        et = ev.bump()
                        def _evo(e, mmt=mmt, m=m, pb=pb, ld_t=ld_t, sf=sf, a=st0n, b=st1n):
                            e.wait_ge(mm.sem, mmt)
                            if m == 0:
                                e.wait_ge(env.sems[sf].sem, ld_t)
                                e.wait_ge(env.sems["st0"].sem, a)
                                e.wait_ge(env.sems["st1"].sem, b)
                            cc = slice(m * TT, (m + 1) * TT)
                            e.tensor_add(out=xout[:, cc], in0=pb[:], in1=xf[:, cc]).then_inc(ev.sem, 1)
                        sch.on("vector", _evo)
                        ev_o.append(et)
                    ssl = "st0" if j % 2 == 0 else "st1"
                    sch.on("gpsimd", lambda e, t=ev_o[-1]: e.wait_ge(ev.sem, t))
                    dst = xdst[s][:, :, j * TT:(j + 1) * TT].rearrange("c p t -> p c t")
                    slot_dma(env, sch, "gpsimd", ssl, dst, xout.rearrange("p (c t) -> p c t", c=NFC))
                    prev_done = ev_o[-1]
                    prev_evo = ev_o
                fence[0] = prev_done

        import os as _os
        nseq = int(_os.environ.get("K_P3_DEBUG")) if _os.environ.get("K_P3_DEBUG") else BL
        for s in range(nseq):
            stage_a1(s)
            stage_a2(s)
            stage_b(s)
            stage_b2(s)
        if _os.environ.get("K_P3_DEBUG"):
            dbg = {}
            for nm, src_ap, shp in (("dbg_qs", qs_s, [8, 128, T]), ("dbg_ks", ks_s, [8, 128, T]),
                                    ("dbg_vs", v_s, [16, 128, T]), ("dbg_gs", gate_s, [16, 128, T]),
                                    ("dbg_gn", gn_s, [16, 128, T])):
                o = nc.dram_tensor(nm, shp, BF16, kind="ExternalOutput").ap()
                stx = env.sems["st0"]
                base = stx.n
                def _d(e, o=o, src_ap=src_ap, base=base, t=fence[0]):
                    e.wait_ge(ev.sem, t)
                    e.wait_ge(stx.sem, base)
                    e.dma_start(out=o, in_=src_ap).then_inc(stx.sem, 16)
                sch.on("gpsimd", _d)
                stx.n += 16

        def _drain(e):
            e.wait_ge(env.sems["st0"].sem, env.sems["st0"].n)
            e.wait_ge(env.sems["st1"].sem, env.sems["st1"].n)
        sch.on("gpsimd", _drain)
        emit_barrier(env, sch)
        sch.flush(nc)




def emit_p_l1attna(env, sch, xsrc, xdst):
    nc = env.nc
    ldh, mm, ev, ac = (env.sems[k] for k in ("ldh", "mm", "ev", "ac"))
    ones_bf = env.consts["ones_col_bf"]
    ones_row = env.consts["ones_row_f"]
    CH = 64

    xa_s = nc.dram_tensor("p5_xas", [NFC, 128, T], F32, kind="Internal").ap()

    with ExitStack() as stk:
        prew = env.sb(stk, "p5_prew", [128, NFC], F32)
        ffnw = env.sb(stk, "p5_ffnw", [128, NFC], F32)
        convw = env.sb(stk, "p5_convw", [128, NFC * 4], F32)
        convb = env.sb(stk, "p5_convb", [128, NFC], F32)
        dbias = env.sb(stk, "p5_dbias", [128, NFC], F32)
        for nm, dst in (("p_prew", prew), ("p_ffnw", ffnw), ("p_convw", convw),
                        ("p_convb", convb), ("p_dbias", dbias)):
            ldh.bump(16)
            sch.on("sync", lambda e, nm=nm, dst=dst: e.dma_start(out=dst[:], in_=env.dram[nm]).then_inc(ldh.sem, 16))
        par_ld = ldh.n
        fence = [None]

        def scope_fence():
            if fence[0] is not None:
                for eng in ("sync", "gpsimd", "tensor"):
                    sch.on(eng, lambda e, t=fence[0]: e.wait_ge(ev.sem, t))
            st0n, st1n = env.sems["st0"].n, env.sems["st1"].n
            for eng in ("sync", "gpsimd", "tensor", "vector", "scalar"):
                sch.on(eng, lambda e, a=st0n, b=st1n: (e.wait_ge(env.sems["st0"].sem, a),
                                                       e.wait_ge(env.sems["st1"].sem, b)))

        def stage_a(s):
            scope_fence()
            with ExitStack() as sa:
                xbf = env.sb(sa, f"p5a_xbf{s}", [128, NFC * TT], BF16)
                ln2 = [LNBufs(env, sa, f"p5a_lnA{s}")]
                ln2.append(LNBufs(env, sa, f"p5a_lnB{s}", share_ps=ln2[0]))
                hbuf = env.sb(sa, f"p5a_h{s}", [128, NFC * (TT + 3)], BF16)
                hsave = env.sb(sa, f"p5a_hs{s}", [128, NFC * 3], BF16)
                hcb = env.sb(sa, f"p5a_hc{s}", [128, NFC * TT], BF16)
                wa = env.sb(sa, f"p5a_wa{s}", [128, NFC * D], BF16)
                wi = env.sb(sa, f"p5a_wi{s}", [128, NFC * D], BF16)
                wv = env.sb(sa, f"p5a_wv{s}", [128, NFC * D], BF16)
                wo = env.sb(sa, f"p5a_wo{s}", [128, NFC * D], BF16)
                asb = env.sb(sa, f"p5a_a{s}", [128, NFC * TT], F32)
                isb = env.sb(sa, f"p5a_i{s}", [128, NFC * TT], BF16)
                t0 = env.sb(sa, f"p5a_t0{s}", [128, TT], F32)
                t1 = env.sb(sa, f"p5a_t1{s}", [128, TT], F32)
                t2 = env.sb(sa, f"p5a_t2{s}", [128, TT], F32)
                onesr = env.sb(sa, f"p5a_ones{s}", [128, CH], F32)
                rec = env.sb(sa, f"p5a_rec{s}", [128, NFC * TT], F32)
                recb = isb
                stbuf = env.sb(sa, f"p5a_st{s}", [128, NFC], F32)
                endc = env.sb(sa, f"p5a_ec{s}", [128, 16], F32)
                d0s = env.sb(sa, f"p5a_d0{s}", [128, 16], F32)
                d1s = env.sb(sa, f"p5a_d1{s}", [128, 16], F32)
                xaout = rec
                psm = [env.ps(sa, f"p5a_ps{i}_{s}", [128, TT]) for i in range(2)]
                psA2 = env.ps(sa, f"p5a_psA2{s}", [1, TT])
                psR2 = env.ps(sa, f"p5a_psR2{s}", [128, TT])
                mmc, ev2c, mm2c, ac2c = (env.sems[k] for k in ("mm", "ev2", "mm2", "ac2"))
                SS = TT + 3

                wt = {}
                prev_grp = None
                for wname, wdst in (("w5_aT", wa), ("w5_iT", wi), ("w5_vT", wv), ("w5_oT", wo)):
                    for sl, t in emit_weight_load_T(env, sch, env.dram[wname], wdst, D):
                        wt[sl] = max(wt.get(sl, 0), t)
                ev0 = ev.bump()
                def _init(e):
                    e.memset(onesr[:], 1.0)
                    e.memset(d0s[:, 0:1], 0.0)
                    e.memset(stbuf[:], 0.0).then_inc(ev.sem, 1)
                sch.on("vector", _init)

                evln, lninfo = {}, {}
                acrq, mmr2, evr2, acrs2, mmbc2 = {}, {}, {}, {}, {}
                lastproj, evhc = {}, {}

                def do_chain(g, cap):
                    ln = ln2[g % 2]
                    put = (lambda sec, eng, f: sch.on(eng, f)) if cap is None else \
                          (lambda sec, eng, f: cap.setdefault(sec, []).append((eng, f)))
                    sb_ = "g0" if g % 2 == 0 else "g1"
                    srcx = xsrc[s][:, :, g * TT:(g + 1) * TT].rearrange("c p t -> p c t")
                    if g >= 1:
                        sch.on("gpsimd", lambda e, t=evln[g - 1]: e.wait_ge(ev2c.sem, t))
                    ldg_t = slot_dma(env, sch, "gpsimd", sb_, xbf.rearrange("p (c t) -> p c t", c=NFC), srcx)
                    kw = {"cap": cap, "sems": ("mm2", "ev2", "ac2"), "norm_src": "bf"}
                    if g >= 1:
                        kw["sq_guard"] = (mm2c, mmr2[g - 1])
                        kw["rows_guard"] = [(mm2c, mmbc2[g - 1]), (ac2c, acrs2[g - 1])]
                        kw["bc_guard"] = (ev2c, evln[g - 1])
                    if g >= 2:
                        kw["xln_guard"] = (mmc, lastproj[g - 2])
                    evln[g], lninfo[g] = emit_ln(env, sch, ln, g, None, xbf, (None, None), (sb_, ldg_t), **kw)
                    acrq[g] = ac2c.bump()
                    def _rq(e, t=evln[g], ln=ln):
                        e.wait_ge(ev2c.sem, t)
                        e.activation(out=ln.sq[:], in_=ln.xln[:, :NFC * TT], func=AF.Square).then_inc(ac2c.sem, 1)
                    put("rq2", "scalar", _rq)
                    mmr2[g] = mm2c.bump()
                    def _st2(e, t=acrq[g], ln=ln):
                        e.wait_ge(ac2c.sem, t)
                        for kc in range(NFC):
                            i = e.matmul(out=psA2[0:1, :], lhsT=ones_bf[:, 0:1], rhs=ln.sq[:, kc * TT:(kc + 1) * TT],
                                         start=(kc == 0), stop=(kc == NFC - 1))
                        i.then_inc(mm2c.sem, 1)
                    put("st2", "tensor", _st2)
                    R = ln2[0].rows
                    m2r, sdr, rsr = R[0:1, 1 * TT:2 * TT], R[0:1, 3 * TT:4 * TT], R[0:1, 5 * TT:6 * TT]
                    evr2[g] = ev2c.bump()
                    def _rw2(e, t=mmr2[g], pa=(acrs2[g - 1] if g >= 1 else None)):
                        e.wait_ge(mm2c.sem, t)
                        if pa is not None:
                            e.wait_ge(ac2c.sem, pa)
                        e.tensor_scalar_mul(out=m2r, in0=psA2[0:1, :], scalar1=1.0 / 1024.0).then_inc(ev2c.sem, 1)
                    put("rw2", "vector", _rw2)
                    acrs2[g] = ac2c.bump(2)
                    def _sd2(e, t=evr2[g]):
                        e.wait_ge(ev2c.sem, t)
                        e.activation(out=sdr, in_=m2r, func=AF.Ln, bias=env.fconst(1e-6)[0:1, :]).then_inc(ac2c.sem, 1)
                        e.activation(out=rsr, in_=sdr, func=AF.Exp, scale=-0.5).then_inc(ac2c.sem, 1)
                    put("sd2", "scalar", _sd2)
                    mmbc2[g] = mm2c.bump()
                    def _bc2(e, t=acrs2[g]):
                        e.wait_ge(ac2c.sem, t)
                        e.matmul(out=psR2[:], lhsT=ones_row[0:1, :], rhs=rsr, start=True, stop=True).then_inc(mm2c.sem, 1)
                    put("bc2", "tensor", _bc2)

                prev_store_done = None
                do_chain(0, None)
                for j in range(NTILE):
                    ln = ln2[j % 2]
                    cap = {} if j + 1 < NTILE else None
                    if cap is not None:
                        do_chain(j + 1, cap)
                    def play(sec, cap=cap):
                        if cap is not None:
                            for e_, f_ in cap.get(sec, []):
                                sch.on(e_, f_)
                    play("sq")
                    def _hstart(e, t=mmbc2[j]):
                        e.wait_ge(mm2c.sem, t)
                        e.wait_ge(ldh.sem, par_ld)
                    sch.on("vector", _hstart)
                    hc = []
                    for c in range(NFC):
                        et = ev.bump()
                        def _hc(e, c=c, j=j, ln=ln):
                            cur = hbuf[:, c * SS + 3:(c + 1) * SS]
                            if j == 0:
                                e.memset(hbuf[:, c * SS: c * SS + 3], 0.0)
                            else:
                                e.tensor_copy(out=hbuf[:, c * SS: c * SS + 3], in_=hsave[:, c * 3:(c + 1) * 3])
                            e.tensor_mul(out=cur, in0=ln.xln[:, c * TT:(c + 1) * TT], in1=psR2[:])
                            e.tensor_scalar_mul(out=cur, in0=cur, scalar1=prew[:, c:c + 1])
                            e.tensor_copy(out=hsave[:, c * 3:(c + 1) * 3], in_=hbuf[:, (c + 1) * SS - 3:(c + 1) * SS])
                            dst = hcb[:, c * TT:(c + 1) * TT]
                            sh = lambda k: hbuf[:, c * SS + k: c * SS + k + TT]
                            e.tensor_scalar_mul(out=t0[:], in0=sh(0), scalar1=convw[:, c * 4 + 0: c * 4 + 1])
                            for kk in range(1, 4):
                                e.scalar_tensor_tensor(out=t0[:], in0=sh(kk), scalar=convw[:, c * 4 + kk: c * 4 + kk + 1],
                                                       in1=t0[:], op0=ALU.mult, op1=ALU.add)
                            e.tensor_scalar_add(out=dst, in0=t0[:], scalar1=convb[:, c:c + 1]).then_inc(ev.sem, 1)
                        sch.on("vector", _hc)
                        hc.append(et)
                    evhc[j] = hc

                    grp = []
                    def _w1(e, wt=dict(wt)):
                        for sl, tt in wt.items():
                            e.wait_ge(env.sems[sl].sem, tt)
                    sch.on("tensor", _w1)
                    for gi, (wsb, kind) in enumerate(((wa, "a"), (wi, "i"), (wv, "v"))):
                        for m in range(NFC):
                            pb = psm[m % 2]
                            if len(grp) >= 2:
                                cse, ct = grp[-2]
                                sch.on("tensor", lambda e, cse=cse, ct=ct: e.wait_ge(cse.sem, ct))
                            elif prev_grp is not None:
                                cse, ct = prev_grp[len(grp) - 2]
                                sch.on("tensor", lambda e, cse=cse, ct=ct: e.wait_ge(cse.sem, ct))
                            mmt = mm.bump()
                            def _mmx(e, m=m, pb=pb, wsb=wsb, first=(gi == 0 and m == 0), hc=tuple(hc)):
                                for kc in range(NFC):
                                    if first:
                                        e.wait_ge(ev.sem, hc[kc])
                                    i = e.matmul(out=pb[:], lhsT=wsb[:, kc * D + m * 128: kc * D + (m + 1) * 128],
                                                 rhs=hcb[:, kc * TT:(kc + 1) * TT], start=(kc == 0), stop=(kc == NFC - 1))
                                i.then_inc(mm.sem, 1)
                            sch.on("tensor", _mmx)
                            cc = slice(m * TT, (m + 1) * TT)
                            if kind == "a":
                                at = ac.bump()
                                def _eva(e, mmt=mmt, pb=pb, cc=cc, m=m):
                                    e.wait_ge(mm.sem, mmt)
                                    e.activation(out=asb[:, cc], in_=pb[:], func=AF.Sigmoid,
                                                 bias=dbias[:, m:m + 1]).then_inc(ac.sem, 1)
                                sch.on("scalar", _eva)
                                grp.append((ac, at))
                            elif kind == "i":
                                at = ac.bump()
                                def _evi(e, mmt=mmt, pb=pb, cc=cc):
                                    e.wait_ge(mm.sem, mmt)
                                    e.activation(out=isb[:, cc], in_=pb[:], func=AF.Sigmoid).then_inc(ac.sem, 1)
                                sch.on("scalar", _evi)
                                grp.append((ac, at))
                            else:
                                at = ev.bump()
                                def _evv(e, mmt=mmt, pb=pb, cc=cc, ga=grp[NFC + m][1]):
                                    e.wait_ge(mm.sem, mmt)
                                    e.wait_ge(ac.sem, ga)
                                    e.tensor_mul(out=isb[:, cc], in0=pb[:], in1=isb[:, cc]).then_inc(ev.sem, 1)
                                sch.on("vector", _evv)
                                grp.append((ev, at))
                    prev_grp = list(grp)
                    lastproj[j] = mm.n

                    ev_chunks = []
                    for c in range(NFC):
                        if c == 1:
                            play("stats")
                        if c == 2:
                            play("rows")
                        if c == 3:
                            play("sd")
                            play("bc")
                        if c == 4:
                            play("norm")
                        if c == 5:
                            play("rq2")
                            play("st2")
                        if c == 6:
                            play("rw2")
                            play("sd2")
                        if c == 7:
                            play("bc2")
                        cc = slice(c * TT, (c + 1) * TT)
                        a_t = grp[c][1]
                        iv_t = grp[2 * NFC + c][1]
                        ac_sq3 = ac.bump()
                        ev_pre = ev.bump()
                        def _pre(e, cc=cc, a_t=a_t, c=c):
                            e.wait_ge(ac.sem, a_t)
                            e.tensor_mul(out=t0[:], in0=asb[:, cc], in1=asb[:, cc])
                            e.tensor_scalar(out=t0[:], in0=t0[:], scalar1=-1.0, scalar2=1.0,
                                            op0=ALU.mult, op1=ALU.add)
                            e.tensor_scalar_max(out=t0[:], in0=t0[:], scalar1=1e-8).then_inc(ev.sem, 1)
                        sch.on("vector", _pre)
                        def _sq3(e, ev_pre=ev_pre):
                            e.wait_ge(ev.sem, ev_pre)
                            e.activation(out=t1[:], in_=t0[:], func=AF.Sqrt).then_inc(ac.sem, 1)
                        sch.on("scalar", _sq3)
                        ev_cl = ev.bump()
                        def _cl(e, cc=cc):
                            e.tensor_scalar_max(out=t2[:], in0=asb[:, cc], scalar1=1e-10).then_inc(ev.sem, 1)
                        sch.on("vector", _cl)
                        ac_ln2 = ac.bump()
                        def _ln2(e, ev_cl=ev_cl):
                            e.wait_ge(ev.sem, ev_cl)
                            e.activation(out=t2[:], in_=t2[:], func=AF.Ln).then_inc(ac.sem, 1)
                        sch.on("scalar", _ln2)
                        ev_is = ev.bump()
                        def _is(e, ac_sq3=ac_sq3, iv_t=iv_t, cc=cc):
                            e.wait_ge(ac.sem, ac_sq3)
                            e.wait_ge(ev.sem, iv_t)
                            e.tensor_mul(out=isb[:, cc], in0=t1[:], in1=isb[:, cc]).then_inc(ev.sem, 1)
                        sch.on("vector", _is)
                        ev_cs0 = ev.bump()
                        ev_cs1 = ev.bump()
                        def _cs1(e, ac_ln2=ac_ln2, ac_sq3=ac_sq3, ev_cs0=ev_cs0):
                            e.wait_ge(ac.sem, ac_ln2)
                            e.wait_ge(ac.sem, ac_sq3)
                            for sub in range(TT // CH):
                                ss = slice(sub * CH, (sub + 1) * CH)
                                i = e.tensor_tensor_scan(out=t0[:, ss], data0=onesr[:], data1=t2[:, ss],
                                                         initial=0.0, op0=ALU.mult, op1=ALU.add)
                            i.then_inc(ev.sem, 1)
                            e.wait_ge(ev.sem, ev_cs0)
                            e.tensor_scalar_max(out=t2[:], in0=t0[:], scalar1=-23.02585093).then_inc(ev.sem, 1)
                        sch.on("vector", _cs1)
                        ac_ex2 = ac.bump(2)
                        def _ex2(e, ev_cs1=ev_cs1):
                            e.wait_ge(ev.sem, ev_cs1)
                            e.activation(out=t1[:], in_=t0[:], func=AF.Exp).then_inc(ac.sem, 1)
                            e.activation(out=t2[:], in_=t2[:], func=AF.Exp, scale=-1.0).then_inc(ac.sem, 1)
                        sch.on("scalar", _ex2)
                        ev_s2 = ev.bump()
                        ev_s3 = ev.bump()
                        ev_g = ev.bump()
                        ev_ch = ev.bump()
                        ev_st = ev.bump()
                        ev_done = ev.bump()
                        nsub = TT // CH
                        def _scan2(e, ac_ex2=ac_ex2, cc=cc, c=c, ev_s2=ev_s2, ev_s3=ev_s3,
                                   ev_g=ev_g, ev_ch=ev_ch, ev_st=ev_st):
                            e.wait_ge(ac.sem, ac_ex2)
                            e.tensor_scalar_min(out=t2[:], in0=t2[:], scalar1=1e10)
                            e.tensor_mul(out=t2[:], in0=t2[:], in1=isb[:, cc]).then_inc(ev.sem, 1)
                            e.wait_ge(ev.sem, ev_s2)
                            for sub in range(nsub):
                                ss = slice(sub * CH, (sub + 1) * CH)
                                i = e.tensor_tensor_scan(out=t2[:, ss], data0=onesr[:], data1=t2[:, ss],
                                                         initial=0.0, op0=ALU.mult, op1=ALU.add)
                            i.then_inc(ev.sem, 1)
                            e.wait_ge(ev.sem, ev_s3)
                            e.tensor_mul(out=t2[:], in0=t1[:], in1=t2[:])
                            t1v = t1[:, :].rearrange("p (s c) -> p s c", c=CH)[:, :, CH - 1:CH] \
                                .rearrange("p s one -> p (s one)")
                            t2v = t2[:, :].rearrange("p (s c) -> p s c", c=CH)[:, :, CH - 1:CH] \
                                .rearrange("p s one -> p (s one)")
                            e.tensor_copy(out=d0s[:, 1:1 + nsub], in_=t1v)
                            e.tensor_copy(out=d1s[:, 1:1 + nsub], in_=t2v)
                            e.tensor_copy(out=d1s[:, 0:1], in_=stbuf[:, c:c + 1]).then_inc(ev.sem, 1)
                            e.wait_ge(ev.sem, ev_g)
                            e.tensor_tensor_scan(out=endc[:, 0:nsub + 1], data0=d0s[:, 0:nsub + 1],
                                                 data1=d1s[:, 0:nsub + 1], initial=0.0,
                                                 op0=ALU.mult, op1=ALU.add).then_inc(ev.sem, 1)
                            e.wait_ge(ev.sem, ev_ch)
                            e.tensor_copy(out=stbuf[:, c:c + 1], in_=endc[:, nsub:nsub + 1])
                            for sub in range(nsub):
                                ss = slice(sub * CH, (sub + 1) * CH)
                                i = e.scalar_tensor_tensor(out=rec[:, cc][:, ss], in0=t1[:, ss],
                                                           scalar=endc[:, sub:sub + 1],
                                                           in1=t2[:, ss], op0=ALU.mult, op1=ALU.add)
                            i.then_inc(ev.sem, 1)
                            e.wait_ge(ev.sem, ev_st)
                            e.tensor_copy(out=recb[:, cc], in_=rec[:, cc]).then_inc(ev.sem, 1)
                        sch.on("vector", _scan2)
                        ev_chunks.append(ev_done)
                    ev_o = []
                    for m in range(NFC):
                        pb = psm[m % 2]
                        if m >= 2:
                            sch.on("tensor", lambda e, t=ev_o[m - 2]: e.wait_ge(ev.sem, t))
                        else:
                            cse, ct = grp[-2 + m]
                            sch.on("tensor", lambda e, cse=cse, ct=ct: e.wait_ge(cse.sem, ct))
                        sch.on("tensor", lambda e, t=ev_chunks[-1]: e.wait_ge(ev.sem, t))
                        mmt = mm.bump()
                        def _mmo(e, m=m, pb=pb):
                            for kc in range(NFC):
                                i = e.matmul(out=pb[:], lhsT=wo[:, kc * D + m * 128: kc * D + (m + 1) * 128],
                                             rhs=recb[:, kc * TT:(kc + 1) * TT], start=(kc == 0), stop=(kc == NFC - 1))
                            i.then_inc(mm.sem, 1)
                        sch.on("tensor", _mmo)
                        if m == 0 and prev_store_done is not None:
                            sch.on("vector", lambda e, a=prev_store_done: e.wait_ge(env.sems["st0"].sem, a))
                        et = ev.bump()
                        def _evo(e, mmt=mmt, m=m, pb=pb, ln=ln):
                            e.wait_ge(mm.sem, mmt)
                            cc = slice(m * TT, (m + 1) * TT)
                            e.tensor_add(out=xaout[:, cc], in0=pb[:], in1=ln.xln[:, cc]).then_inc(ev.sem, 1)
                        sch.on("vector", _evo)
                        ev_o.append(et)
                    sch.on("gpsimd", lambda e, t=ev_o[-1]: e.wait_ge(ev.sem, t))
                    slot_dma(env, sch, "gpsimd", "st0",
                             xa_s[:, :, j * TT:(j + 1) * TT].rearrange("c p t -> p c t"),
                             xaout.rearrange("p (c t) -> p c t", c=NFC))
                    prev_store_done = env.sems["st0"].n
                    fence[0] = ev_o[-1]

        def stage_b(s):
            scope_fence()
            with ExitStack() as sb_:
                hn = env.sb(sb_, f"p5b_hn{s}", [128, NFC * T], BF16)
                with ExitStack() as sb1:
                    xabf = env.sb(sb1, f"p5b_xabf{s}", [128, NFC * TT], BF16)
                    sq2 = env.sb(sb1, f"p5b_sq{s}", [128, NFC * TT], BF16)
                    rows = env.sb(sb1, f"p5b_rows{s}", [1, 2 * TT], F32)
                    psA = env.ps(sb1, f"p5b_psA{s}", [1, TT])
                    psR = env.ps(sb1, f"p5b_psR{s}", [128, TT])
                    prev_hn = None
                    for j in range(NTILE):
                        gsl = "g0" if j % 2 == 0 else "g1"
                        if prev_hn is not None:
                            sch.on("gpsimd", lambda e, t=prev_hn: e.wait_ge(ev.sem, t))
                        lb = slot_dma(env, sch, "gpsimd", gsl, xabf.rearrange("p (c t) -> p c t", c=NFC),
                                      xa_s[:, :, j * TT:(j + 1) * TT].rearrange("c p t -> p c t"))
                        ac_sq = ac.bump()
                        def _sq(e, lb=lb, gsl=gsl):
                            e.wait_ge(env.sems[gsl].sem, lb)
                            e.activation(out=sq2[:], in_=xabf[:], func=AF.Square).then_inc(ac.sem, 1)
                        sch.on("scalar", _sq)
                        mm_s = mm.bump()
                        def _st(e, ac_sq=ac_sq):
                            e.wait_ge(ac.sem, ac_sq)
                            for kc in range(NFC):
                                i = e.matmul(out=psA[0:1, :], lhsT=ones_bf[:, 0:1], rhs=sq2[:, kc * TT:(kc + 1) * TT],
                                             start=(kc == 0), stop=(kc == NFC - 1))
                            i.then_inc(mm.sem, 1)
                        sch.on("tensor", _st)
                        m2r, rsr = rows[0:1, 0:TT], rows[0:1, TT:2 * TT]
                        ev_r = ev.bump()
                        def _rw(e, mm_s=mm_s):
                            e.wait_ge(mm.sem, mm_s)
                            e.tensor_scalar_mul(out=m2r, in0=psA[0:1, :], scalar1=1.0 / 1024.0).then_inc(ev.sem, 1)
                        sch.on("vector", _rw)
                        ac_rs = ac.bump(2)
                        def _sd(e, ev_r=ev_r):
                            e.wait_ge(ev.sem, ev_r)
                            e.activation(out=m2r, in_=m2r, func=AF.Ln, bias=env.fconst(1e-6)[0:1, :]).then_inc(ac.sem, 1)
                            e.activation(out=rsr, in_=m2r, func=AF.Exp, scale=-0.5).then_inc(ac.sem, 1)
                        sch.on("scalar", _sd)
                        mm_b = mm.bump()
                        def _bc(e, ac_rs=ac_rs):
                            e.wait_ge(ac.sem, ac_rs)
                            e.matmul(out=psR[:], lhsT=ones_row[0:1, :], rhs=rsr, start=True, stop=True).then_inc(mm.sem, 1)
                        sch.on("tensor", _bc)
                        et = ev.bump()
                        def _hn(e, mm_b=mm_b, j=j):
                            e.wait_ge(mm.sem, mm_b)
                            e.wait_ge(ldh.sem, par_ld)
                            last = None
                            for c in range(NFC):
                                dst = hn[:, c * T + j * TT: c * T + (j + 1) * TT]
                                e.tensor_mul(out=dst, in0=xabf[:, c * TT:(c + 1) * TT], in1=psR[:])
                                last = e.tensor_scalar_mul(out=dst, in0=dst, scalar1=ffnw[:, c:c + 1])
                            last.then_inc(ev.sem, 1)
                        sch.on("vector", _hn)
                        prev_hn = et
                gusb = env.sb(sb_, f"p5b_gu{s}", [128, 32 * TT], BF16)
                gp = env.sb(sb_, f"p5b_gp{s}", [128, TT], F32)
                upf = env.sb(sb_, f"p5b_up{s}", [128, TT], F32)
                wgu = [env.sb(sb_, f"p5b_wgu{i}_{s}", [128, NFC * 128], BF16) for i in range(2)]
                wup = [env.sb(sb_, f"p5b_wup{i}_{s}", [128, NFC * 128], BF16) for i in range(2)]
                wdn = [env.sb(sb_, f"p5b_wdn{i}_{s}", [128, D], BF16) for i in range(4)]
                xaf = env.sb(sb_, f"p5b_xaf{s}", [128, NFC * TT], F32)
                xff = env.sb(sb_, f"p5b_xf{s}", [128, NFC * TT], F32)
                xout = env.sb(sb_, f"p5b_xo{s}", [128, NFC * TT], F32)
                hn_done = prev_hn
                prev_evo = None
                for tt_ in range(NTILE):
                    with ExitStack() as sg:
                        psg = [env.ps(sg, f"p5b_psg{i}_{s}_{tt_}", [128, TT]) for i in range(2)]
                        grp = []
                        mm_gu_done = {}
                        GS4 = ("g0", "g1", "g2", "g3")
                        for cg in range(32):
                            if cg >= 2:
                                sch.on("gpsimd", lambda e, t=mm_gu_done[cg - 2]: e.wait_ge(mm.sem, t))
                            lg = slot_dma(env, sch, "gpsimd", GS4[(2 * cg) % 4],
                                          wgu[cg % 2].rearrange("p (kc d) -> p kc d", kc=NFC),
                                          env.dram["w5_gu_k"][cg].rearrange("(kc p) d -> p kc d", p=128))
                            lu = slot_dma(env, sch, "gpsimd", GS4[(2 * cg + 1) % 4],
                                          wup[cg % 2].rearrange("p (kc d) -> p kc d", kc=NFC),
                                          env.dram["w5_gu_k"][cg + 32].rearrange("(kc p) d -> p kc d", p=128))
                            for gu_i in range(2):
                                pb = psg[gu_i]
                                if len(grp) >= 2:
                                    cse, ct = grp[-2]
                                    sch.on("tensor", lambda e, cse=cse, ct=ct: e.wait_ge(cse.sem, ct))
                                elif prev_evo is not None:
                                    sch.on("tensor", lambda e, t=prev_evo: e.wait_ge(ev.sem, t))
                                mmt = mm.bump()
                                def _mmg(e, cg=cg, gu_i=gu_i, pb=pb, lg=lg, lu=lu, tt_=tt_, hn_done=hn_done):
                                    if cg == 0 and gu_i == 0:
                                        e.wait_ge(ev.sem, hn_done)
                                    e.wait_ge(env.sems[GS4[(2 * cg + (1 if gu_i else 0)) % 4]].sem, lu if gu_i else lg)
                                    wsb = (wgu if gu_i == 0 else wup)[cg % 2]
                                    for kc in range(NFC):
                                        i = e.matmul(out=pb[:], lhsT=wsb[:, kc * 128:(kc + 1) * 128],
                                                     rhs=hn[:, kc * T + tt_ * TT: kc * T + (tt_ + 1) * TT],
                                                     start=(kc == 0), stop=(kc == NFC - 1))
                                    i.then_inc(mm.sem, 1)
                                sch.on("tensor", _mmg)
                                if gu_i == 0:
                                    at = ac.bump()
                                    gp_free = grp[-1][1] if grp else None
                                    def _evg(e, mmt=mmt, pb=pb, gp_free=gp_free):
                                        e.wait_ge(mm.sem, mmt)
                                        if gp_free is not None:
                                            e.wait_ge(ev.sem, gp_free)
                                        e.activation(out=gp[:], in_=pb[:], func=AF.Silu).then_inc(ac.sem, 1)
                                    sch.on("scalar", _evg)
                                    grp.append((ac, at))
                                else:
                                    et = ev.bump()
                                    def _evu(e, mmt=mmt, pb=pb, cg=cg, ga=grp[-1][1]):
                                        e.wait_ge(mm.sem, mmt)
                                        e.tensor_copy(out=upf[:], in_=pb[:])
                                        e.wait_ge(ac.sem, ga)
                                        e.tensor_mul(out=gusb[:, cg * TT:(cg + 1) * TT], in0=gp[:],
                                                     in1=upf[:]).then_inc(ev.sem, 1)
                                    sch.on("vector", _evu)
                                    grp.append((ev, et))
                                if gu_i == 1:
                                    mm_gu_done[cg] = mm.n
                        gu_done = grp[-1][1]
                    with ExitStack() as sd:
                        psd = [env.ps(sd, f"p5b_psd{i}_{s}_{tt_}", [128, TT]) for i in range(8)]
                        sf = "s0" if tt_ % 2 == 0 else "s1"
                        if prev_evo is not None:
                            sch.on("sync", lambda e, t=prev_evo: e.wait_ge(ev.sem, t))
                        la = slot_dma(env, sch, "sync", sf, xaf.rearrange("p (c t) -> p c t", c=NFC),
                                      xa_s[:, :, tt_ * TT:(tt_ + 1) * TT].rearrange("c p t -> p c t"))
                        lx = slot_dma(env, sch, "sync", sf, xff.rearrange("p (c t) -> p c t", c=NFC),
                                      xsrc[s][:, :, tt_ * TT:(tt_ + 1) * TT].rearrange("c p t -> p c t"))
                        ev_o = []
                        mm_dn_done = {}
                        GS = ("g0", "g1", "g2", "g3")
                        for kc in range(32):
                            if kc >= 4:
                                sch.on("gpsimd", lambda e, t=mm_dn_done[kc - 4]: e.wait_ge(mm.sem, t))
                            ld_w = slot_dma(env, sch, "gpsimd", GS[kc % 4],
                                            wdn[kc % 4][:], env.dram["w5_dn_k"][kc])
                            mm.bump(8)
                            def _mmd(e, kc=kc, ld_w=ld_w, gu_done=gu_done, pe_guard=prev_evo):
                                if kc == 0:
                                    e.wait_ge(ev.sem, gu_done)
                                    if pe_guard is not None:
                                        e.wait_ge(ev.sem, pe_guard)
                                e.wait_ge(env.sems[GS[kc % 4]].sem, ld_w)
                                for m in range(8):
                                    e.matmul(out=psd[m][:], lhsT=wdn[kc % 4][:, m * 128:(m + 1) * 128],
                                             rhs=gusb[:, kc * TT:(kc + 1) * TT],
                                             start=(kc == 0), stop=(kc == 31)).then_inc(mm.sem, 1)
                            sch.on("tensor", _mmd)
                            mm_dn_done[kc] = mm.n
                        mm_fin = mm.n
                        for m in range(NFC):
                            et = ev.bump()
                            def _evd(e, mm_fin=mm_fin, m=m, la=la, lx=lx, sf=sf):
                                e.wait_ge(mm.sem, mm_fin)
                                if m == 0:
                                    e.wait_ge(env.sems[sf].sem, lx)
                                cc = slice(m * TT, (m + 1) * TT)
                                e.tensor_add(out=xout[:, cc], in0=psd[m][:], in1=xaf[:, cc])
                                e.tensor_add(out=xout[:, cc], in0=xout[:, cc], in1=xff[:, cc]).then_inc(ev.sem, 1)
                            sch.on("vector", _evd)
                            ev_o.append(et)
                        ssl = "st0" if tt_ % 2 == 0 else "st1"
                        sch.on("gpsimd", lambda e, t=ev_o[-1]: e.wait_ge(ev.sem, t))
                        dst = xdst[s][:, :, tt_ * TT:(tt_ + 1) * TT].rearrange("c p t -> p c t")
                        slot_dma(env, sch, "gpsimd", ssl, dst, xout.rearrange("p (c t) -> p c t", c=NFC))
                        prev_evo = ev_o[-1]
                fence[0] = prev_evo

        import os as _os
        if _os.environ.get("K_P5_DEBUG"):
            stage_a(0)
            o = nc.dram_tensor("dbg_xa", [NFC, 128, T], F32, kind="ExternalOutput").ap()
            stx = env.sems["st0"]
            base = stx.n
            def _d(e, base=base, t=fence[0]):
                e.wait_ge(ev.sem, t)
                e.wait_ge(stx.sem, base)
                e.dma_start(out=o, in_=xa_s).then_inc(stx.sem, 16)
                e.wait_ge(stx.sem, base + 16)
            sch.on("gpsimd", _d)
            stx.n += 16
            emit_barrier(env, sch)
            sch.flush(nc)
            return
        for s in range(BL):
            stage_a(s)
            stage_b(s)

        def _drain(e):
            e.wait_ge(env.sems["st0"].sem, env.sems["st0"].n)
            e.wait_ge(env.sems["st1"].sem, env.sems["st1"].n)
        sch.on("gpsimd", _drain)
        emit_barrier(env, sch)
        sch.flush(nc)


def _host_consts():
    c = {}
    c["identf"] = np.eye(128, dtype=np.float32)
    import ml_dtypes
    c["identb"] = np.eye(128, dtype=np.float32).astype(ml_dtypes.bfloat16)
    c["ones_col_bf"] = np.ones((128, 1), np.float32).astype(ml_dtypes.bfloat16)
    c["ones_row_f"] = np.ones((1, 128), np.float32)
    c["iota_col"] = np.arange(128, dtype=np.float32).reshape(128, 1)
    c["iota2d"] = (np.arange(128)[:, None] + 128 * np.arange(16)[None, :]).astype(np.float32)
    c["iota_row"] = np.arange(T, dtype=np.float32).reshape(1, T)
    s = np.arange(128)[:, None]
    t = np.arange(128)[None, :]
    c["mask01"] = (t >= s).astype(np.float32)
    dd, gg = np.divmod(np.arange(512), 16)
    c["ind_d"] = (np.arange(32)[:, None] == dd[None, :]).astype(np.float32)
    c["ind_g"] = (np.arange(16)[:, None] == gg[None, :]).astype(np.float32)
    R = np.zeros((128, 128), np.float32)
    for i in range(64):
        R[2 * i, 2 * i + 1] = -1.0
        R[2 * i + 1, 2 * i] = 1.0
    c["rt_bf"] = R.T.astype(ml_dtypes.bfloat16)
    return c


def _host_rotary():
    QK = 1024
    SCALE_BASE = 512.0
    dim_half = QK // 2
    base_scale = (np.arange(0, QK, 2, dtype=np.float32) + 0.4 * QK) / (1.4 * QK)
    pos = np.arange(T, dtype=np.float32)
    scale = base_scale[None, :] ** (pos[:, None] / SCALE_BASE)
    inv_freq = 1.0 / 10000.0 ** (np.arange(dim_half, dtype=np.float32) / dim_half)
    sinus = pos[:, None] * inv_freq[None, :]
    sin = np.sin(sinus).astype(np.float32)
    cos = np.cos(sinus).astype(np.float32)
    sc2 = np.repeat(scale, 2, -1)
    sin2 = np.repeat(sin, 2, -1)
    cos2 = np.repeat(cos, 2, -1)
    def per_head(tab):
        return np.ascontiguousarray(tab.T.reshape(8, 128, T))
    return {
        "rot_cq": per_head((cos2 * sc2).astype(np.float32)),
        "rot_sq": per_head((sin2 * sc2).astype(np.float32)),
        "rot_ck": per_head((cos2 / sc2).astype(np.float32)),
        "rot_sk": per_head((sin2 / sc2).astype(np.float32)),
    }


def host_prep(inputs):
    g = {}
    tr = lambda a: np.ascontiguousarray(np.asarray(a).T)
    rs = lambda a, *shp: np.ascontiguousarray(np.asarray(a, np.float32).reshape(*shp))
    pcol = lambda a: np.ascontiguousarray(np.asarray(a, np.float32).reshape(-1, 128).T)
    g.update(_host_consts())
    g.update(_host_rotary())
    g["w1_inT"] = tr(inputs["L0_attn_a_W_in"])
    g["w1_siT"] = tr(inputs["L0_attn_a_W_sel_in"])
    g["w1_soT"] = tr(inputs["L0_attn_a_W_sel_out"])
    g["w1_outT"] = tr(inputs["L0_attn_a_W_out"])
    g["p_dparam"] = pcol(inputs["L0_attn_a_d_param"])
    g["w2_rT"] = tr(inputs["L0_ffn_W_r"])
    g["w2_kT"] = tr(inputs["L0_ffn_W_k"])
    g["w2_vT"] = tr(inputs["L0_ffn_W_v"])
    g["p_mur"] = pcol(inputs["L0_ffn_mu_r"])
    g["p_muk"] = pcol(inputs["L0_ffn_mu_k"])
    g["w3_qkv"] = np.ascontiguousarray(np.asarray(inputs["L0_attn_b_W_qkv"]))
    g["w3_g"] = np.ascontiguousarray(np.asarray(inputs["L0_attn_b_W_g"]))
    g["w3_proj"] = np.ascontiguousarray(np.asarray(inputs["L0_attn_b_W_proj"]))
    g["p_decay"] = rs(inputs["L0_attn_b_decay"], 1, 8)
    g["w5_aT"] = tr(inputs["L1_attn_a_w_a"])
    g["w5_iT"] = tr(inputs["L1_attn_a_w_i"])
    g["w5_vT"] = tr(inputs["L1_attn_a_w_v"])
    g["w5_oT"] = tr(inputs["L1_attn_a_out_proj_w"])
    Wgu = tr(inputs["L1_attn_a_w_gate_up"])
    g["w5_gu_k"] = np.ascontiguousarray(Wgu.reshape(1024, 64, 128).transpose(1, 0, 2))
    g["w5_dn_k"] = np.ascontiguousarray(tr(inputs["L1_attn_a_w_down"]).reshape(32, 128, 1024))
    g["p_convw"] = np.ascontiguousarray(np.asarray(inputs["L1_attn_a_conv_w"], np.float32).reshape(NFC, 128, 4).transpose(1, 0, 2).reshape(128, NFC * 4))
    g["p_convb"] = pcol(inputs["L1_attn_a_conv_b"])
    g["p_prew"] = pcol(inputs["L1_attn_a_pre_norm_w"])
    g["p_ffnw"] = pcol(inputs["L1_attn_a_ffn_norm_w"])
    g["p_dbias"] = pcol(inputs["L1_attn_a_decay_bias"])
    for k in ("G1_R", "G2_R", "G1_I", "G2_I"):
        g["w6_" + k] = np.ascontiguousarray(np.asarray(inputs["L1_ffn_" + k], np.float32))
    g["p_b6"] = pcol(np.concatenate([np.asarray(inputs["L1_ffn_bias_real"]),
                                     np.asarray(inputs["L1_ffn_bias_imag"])]))
    g["w7_qkv"] = np.ascontiguousarray(np.asarray(inputs["L1_attn_b_W_qkv"]))
    g["w7_out"] = np.ascontiguousarray(np.asarray(inputs["L1_attn_b_W_out"]))
    g["p_b7qkv"] = pcol(inputs["L1_attn_b_b_qkv"])
    g["p_b7v_row"] = np.ascontiguousarray(np.asarray(inputs["L1_attn_b_b_qkv"], np.float32)[2048:].reshape(1, 1024))
    g["p_b7out"] = pcol(inputs["L1_attn_b_b_out"])
    x = np.asarray(inputs["L0_attn_a_x"], np.float32)
    xs = []
    for c in range(NCORES):
        sl = x[c * BL:(c + 1) * BL]
        fm = np.ascontiguousarray(sl.transpose(0, 2, 1).reshape(BL, NFC, 128, T))
        xs.append(fm)
    return g, xs


def declare_dram(nc, g, names, x_shape=None, want_y=True):
    import ml_dtypes
    dram = {}
    DT = {np.dtype(np.float32): F32, np.dtype(ml_dtypes.bfloat16): BF16}
    for n in names:
        a = g[n]
        dram[n] = nc.dram_tensor(n, list(a.shape), DT[a.dtype], kind="ExternalInput").ap()
    if x_shape is not None:
        dram["x"] = nc.dram_tensor("x", list(x_shape), F32, kind="ExternalInput").ap()
    if want_y:
        dram["y"] = nc.dram_tensor("y", [BL, NFC, 128, T], F32, kind="ExternalOutput").ap()
    return dram


CONST_NAMES = ["identf", "identb", "ones_col_bf", "ones_row_f", "iota_col", "iota_row",
               "mask01", "ind_d", "ind_g", "rt_bf"]


def load_consts(env, sch, names=("ones_col_bf", "ones_row_f")):
    nc = env.nc
    ld = env.sems["ldh"]
    shapes = {"identf": ([128, 128], F32), "identb": ([128, 128], BF16),
              "ones_col_bf": ([128, 1], BF16), "ones_row_f": ([1, 128], F32),
              "iota_col": ([128, 1], F32), "iota_row": ([1, T], F32),
              "iota2d": ([128, 16], F32),
              "mask01": ([128, 128], F32), "ind_d": ([32, 512], F32),
              "ind_g": ([16, 512], F32), "rt_bf": ([128, 128], BF16)}
    for n in names:
        shp, dt = shapes[n]
        tile = env.ctx.enter_context(nc.sbuf_tensor("c_" + n, shp, dt))
        env.consts[n] = tile[tuple(slice(0, s) for s in shp)]
        ld.bump(16)
        sch.on("sync", lambda e, d=env.consts[n], s=env.dram[n]: e.dma_start(out=d, in_=s).then_inc(ld.sem, 16))
    ev = env.sems["ev"]
    evt = ev.bump()
    def _pool(e):
        last = None
        for v, i in env.cvals.items():
            last = e.memset(env.cpool[:, i:i + 1], v)
        last.then_inc(ev.sem, 1)
    sch.on("vector", _pool)
    t = ld.n
    for e in ENGINES:
        sch.on(e, lambda eng, t=t, evt=evt: (eng.wait_ge(ld.sem, t), eng.wait_ge(ev.sem, evt)))



PHASE_WEIGHTS = {
    "p2": ["w2_rT", "w2_kT", "w2_vT", "p_mur", "p_muk"],
    "p6": ["w6_G1_R", "w6_G2_R", "w6_G1_I", "w6_G2_I", "p_b6"],
    "p1": ["w1_inT", "w1_siT", "w1_soT", "w1_outT", "p_dparam"],
    "p7": ["w7_qkv", "w7_out", "p_b7qkv", "p_b7out", "p_b7v_row"],
    "p3": ["w3_qkv", "w3_g", "w3_proj", "p_decay", "rot_cq", "rot_sq", "rot_ck", "rot_sk"],
    "p5": ["w5_aT", "w5_iT", "w5_vT", "w5_oT", "w5_gu_k", "w5_dn_k",
           "p_prew", "p_ffnw", "p_convw", "p_convb", "p_dbias"],
}
PHASE_CONSTS = {
    "p2": ["ones_col_bf", "ones_row_f"],
    "p6": ["ones_col_bf", "ones_row_f", "identf", "ind_d", "ind_g"],
    "p1": ["ones_col_bf", "ones_row_f"],
    "p7": ["ones_col_bf", "ones_row_f", "identb"],
    "p3": ["ones_col_bf", "ones_row_f", "iota_col", "iota_row", "iota2d", "mask01", "rt_bf"],
    "p5": ["ones_col_bf", "ones_row_f"],
}
PHASE_NEEDS_WBIG = {"p6", "p8"}
PHASE_EMIT = {}


def build_phase_program(phase):
    nc = bass.Bass("TRN2", target_bir_lowering=False, detect_race_conditions=False)
    ctx = ExitStack()
    env = Env(nc, ctx)
    with ctx:
        env.dram = declare_dram(nc, _HOST_G, PHASE_CONSTS[phase] + PHASE_WEIGHTS[phase],
                                x_shape=[BL, NFC, 128, T])
        sch = Sched()
        load_consts(env, sch, PHASE_CONSTS[phase])
        sch.flush(nc)
        if phase in PHASE_NEEDS_WBIG:
            emit_p0_wbig(env, sch)
        xsrc = [env.dram["x"][s] for s in range(BL)]
        xdst = [env.dram["y"][s] for s in range(BL)]
        PHASE_EMIT[phase](env, sch, xsrc, xdst)
    return nc


PHASE_EMIT["p2"] = emit_p_l0ffn
PHASE_EMIT["p6"] = emit_p_l1ffn
PHASE_EMIT["p1"] = emit_p_l0attna
PHASE_EMIT["p7"] = emit_p_l1attnb
PHASE_EMIT["p3"] = emit_p_l0attnb
PHASE_EMIT["p5"] = emit_p_l1attna

def build_fused_program():
    nc = bass.Bass("TRN2", target_bir_lowering=False, detect_race_conditions=False)
    ctx = ExitStack()
    env = Env(nc, ctx)
    seq = PHASE_SEQ
    cnames, wnames = [], []
    for ph in seq:
        for n in PHASE_CONSTS[ph] + (["identf", "ind_d", "ind_g"] if ph == "p6" else []):
            if n not in cnames:
                cnames.append(n)
        for n in PHASE_WEIGHTS[ph]:
            if n not in wnames:
                wnames.append(n)
    with ctx:
        env.dram = declare_dram(nc, _HOST_G, cnames + wnames, x_shape=[BL, NFC, 128, T])
        x_res = nc.dram_tensor("x_res", [BL, NFC, 128, T], F32, kind="Internal").ap()
        sch = Sched()
        load_consts(env, sch, cnames)
        sch.flush(nc)
        emit_p0_wbig(env, sch)
        for pi, ph in enumerate(seq):
            src_t = env.dram["x"] if pi == 0 else x_res
            dst_t = env.dram["y"] if pi == len(seq) - 1 else x_res
            xsrc = [src_t[s] for s in range(BL)]
            xdst = [dst_t[s] for s in range(BL)]
            PHASE_EMIT[ph](env, sch, xsrc, xdst)
    return nc, cnames + wnames


_HOST_G = None

PHASE_SEQ = ["p1", "p2", "p3", "p2", "p5", "p6", "p7", "p6"]


def kernel(**inputs):
    global _HOST_G
    g, xs = host_prep(inputs)
    _HOST_G = g
    import os
    ph_env = os.environ.get("K_PHASE")
    if ph_env or os.environ.get("K_UNFUSED"):
        seq = [ph_env] if ph_env else PHASE_SEQ
        progs = {}
        for ph in set(seq):
            progs[ph] = build_phase_program(ph)
        cur = xs
        for ph in seq:
            names = PHASE_CONSTS[ph] + PHASE_WEIGHTS[ph]
            in_maps = [{**{n: g[n] for n in names}, "x": cur[c]} for c in range(NCORES)]
            res = run_bass_kernel_spmd(progs[ph], in_maps, core_ids=list(range(NCORES)))
            cur = [np.ascontiguousarray(res.results[c]["y"]) for c in range(NCORES)]
        out = np.empty((B, T, D), np.float32)
        for c in range(NCORES):
            fm = cur[c].reshape(BL, NFC * 128, T)
            out[c * BL:(c + 1) * BL] = fm.transpose(0, 2, 1)
        return out
    nc, names = build_fused_program()
    in_maps = [{**{n: g[n] for n in names}, "x": xs[c]} for c in range(NCORES)]
    res = run_bass_kernel_spmd(nc, in_maps, core_ids=list(range(NCORES)))
    out = np.empty((B, T, D), np.float32)
    for c in range(NCORES):
        fm = res.results[c]["y"].reshape(BL, NFC * 128, T)
        out[c * BL:(c + 1) * BL] = fm.transpose(0, 2, 1)
    return out
